# version 03
# baseline (speedup 1.0000x reference)
import numpy as np
from contextlib import ExitStack
import concourse.bass as bass
import concourse.mybir as mybir
from concourse.bass_utils import run_bass_kernel_spmd

F32 = mybir.dt.float32
BF16 = mybir.dt.bfloat16
AF = mybir.ActivationFunctionType
ALU = mybir.AluOpType
AX = mybir.AxisListType

D = 1024
N = 2048
KT = D // 128
CH = 512
NCH = N // CH
ENGINES = ("tensor", "vector", "scalar", "gpsimd", "sync")


class Sem:
    __slots__ = ("h", "v")
    def __init__(self, h):
        self.h = h
        self.v = 0


class Ctx:
    def __init__(self, nc, stack):
        self.nc = nc
        self.stack = stack
        self.q = {e: [] for e in ENGINES}
        self._sems = {}
        self._blk = 0
        self.t = {}

    def sem(self, name) -> Sem:
        if name not in self._sems:
            self._sems[name] = Sem(self.stack.enter_context(self.nc.semaphore(name)))
        return self._sems[name]

    def on(self, eng, f):
        self.q[eng].append(f)

    def flush(self):
        qs = {e: self.q[e] for e in ENGINES if self.q[e]}
        self.q = {e: [] for e in ENGINES}
        if not qs:
            return
        self._blk += 1
        with self.nc.Block("b%d" % self._blk) as blk:
            for ename, fs in qs.items():
                getattr(blk, ename)(lambda engine, fs=fs: [f(engine) for f in fs])

    def dma_barrier(self, *sems):
        def f(g, sems=sems):
            for s in sems:
                g.wait_ge(s.h, s.v)
        self.on("gpsimd", f)
        self.flush()

    def sb(self, name, shape, dtype):
        t = self.stack.enter_context(self.nc.sbuf_tensor(name, list(shape), dtype))
        self.t[name] = t
        return t

    def ps(self, name, shape, dtype=F32):
        t = self.stack.enter_context(self.nc.psum_tensor(name, list(shape), dtype))
        self.t[name] = t
        return t


CONST_VALS = [1e-5, 1e-6, float(np.finfo(np.float32).eps), 0.0, 1.0, -1.0, 0.5,
              -30.0, 8.0, 0.125, 2048.0, -1e30]


def emit_setup(c: Ctx):
    ones_f = c.sb("ones_f", [128, 1], F32)
    onesr_f = c.sb("onesr_f", [1, 128], F32)
    onesr_b = c.sb("onesr_b", [1, 128], BF16)
    ones_b = c.sb("ones_b", [128, 1], BF16)
    idn = c.sb("idn", [128, 128], BF16)
    riot = c.sb("riot", [128, 128], F32)
    ciot = c.sb("ciot", [128, 1], F32)
    cpool = c.sb("cpool", [128, max(16, len(CONST_VALS))], F32)

    def g(g):
        g.memset(ones_f[:, :], 1.0)
        g.memset(onesr_f[:, :], 1.0)
        g.memset(onesr_b[:, :], 1.0)
        g.memset(ones_b[:, :], 1.0)
        for i, v in enumerate(CONST_VALS):
            g.memset(cpool[:, i:i + 1], v)
        g.iota(riot[:, :], [[1, 128]], channel_multiplier=0, allow_small_or_imprecise_dtypes=True)
        g.iota(ciot[:, :], [[0, 1]], channel_multiplier=1, allow_small_or_imprecise_dtypes=True)
    c.on("gpsimd", g)
    c.flush()
    for i, v in enumerate(CONST_VALS):
        c.nc.const_aps.register(F32, v, cpool[:, i:i + 1])

    def v(v):
        v.tensor_tensor(idn[:, :], riot[:, :], ciot[:, :].broadcast_to([128, 128]), ALU.is_equal)
    c.on("vector", v)
    c.flush()


def alloc_shared(c: Ctx, sq_cols=KT * CH):
    c.sb("xc", [128, KT * CH], F32)
    c.sb("sq", [128, sq_cols], F32)
    c.sb("strow", [1, 4 * CH], F32)
    c.sb("rstd_b", [128, CH], F32)
    c.sb("negm_b", [128, CH], F32)
    c.sb("tmp", [128, CH], F32)
    c.ps("ps_s", [1, CH])
    c.ps("ps_q", [1, CH])
    c.ps("ps_b1", [128, CH])
    c.ps("ps_b2", [128, CH])
    c.ps("mm_psA", [128, CH])
    c.ps("mm_psB", [128, CH])


def load_x_chunk(c: Ctx, x_dram, ci, n_tok=N, w=CH, dst=None, tag="lx"):
    sld = c.sem(tag)
    xc = c.t["xc"] if dst is None else dst
    off = ci * CH

    def f(sy, off=off, w=w):
        src = x_dram.rearrange("k p t -> p k t")[:, :, off:off + w]
        sy.dma_start(xc[:, :].rearrange("p (k t) -> p k t", k=KT)[:, :, :w], src).then_inc(sld.h, 16)
    sld.v += 16
    c.on("sync", f)
    return sld, sld.v


def emit_ln_chunk(c: Ctx, tag, ci, xn, n_tok=N, wait=None, xn_ktile_stride=None):
    T = c.t
    xc, sq, strow = T["xc"], T["sq"], T["strow"]
    ps_s, ps_q, ps_b1, ps_b2 = T["ps_s"], T["ps_q"], T["ps_b1"], T["ps_b2"]
    rstd_b, negm_b, tmp = T["rstd_b"], T["negm_b"], T["tmp"]
    ones_f, onesr_f = T["ones_f"], T["onesr_f"]
    ssq, smm, srow, ssct, srow2, sbc, sev = (c.sem(tag + s) for s in
                                             ("sq", "mm", "rw", "sc", "r2", "bc", "ev"))
    off = ci * CH
    w = CH
    stride = n_tok if xn_ktile_stride is None else xn_ktile_stride
    xoff = off if xn_ktile_stride is None else 0

    nslot = sq.shape[1] // CH
    if nslot >= KT:
        def fsq(s, wait=wait):
            if wait is not None:
                s.wait_ge(wait[0].h, wait[1])
            last = None
            for k in range(KT):
                last = s.activation(sq[:, k * CH:k * CH + w], xc[:, k * CH:k * CH + w], AF.Square)
            last.then_inc(ssq.h, 1)
        ssq.v += 1
        c.on("scalar", fsq)

        sqv, row_prev = ssq.v, srow.v

        def fmm(t, sqv=sqv, row_prev=row_prev, wait=wait):
            if wait is not None:
                t.wait_ge(wait[0].h, wait[1])
            t.wait_ge(ssq.h, sqv)
            if row_prev:
                t.wait_ge(srow.h, row_prev)
            for k in range(KT):
                t.matmul(ps_s[0:1, :w], ones_f[:, :], xc[:, k * CH:k * CH + w],
                         start=(k == 0), stop=(k == KT - 1))
            last = None
            for k in range(KT):
                last = t.matmul(ps_q[0:1, :w], ones_f[:, :], sq[:, k * CH:k * CH + w],
                                start=(k == 0), stop=(k == KT - 1))
            last.then_inc(smm.h, 1)
        smm.v += 1
        c.on("tensor", fmm)
    else:
        def fsq1(s, wait=wait, mmprev=smm.v):
            if wait is not None:
                s.wait_ge(wait[0].h, wait[1])
            if mmprev:
                s.wait_ge(smm.h, mmprev)
            last = None
            for k in range(4):
                last = s.activation(sq[:, k * CH:k * CH + w], xc[:, k * CH:k * CH + w], AF.Square)
            last.then_inc(ssq.h, 1)
        ssq.v += 1
        c.on("scalar", fsq1)
        sqv1, row_prev = ssq.v, srow.v

        def fmm1(t, sqv1=sqv1, row_prev=row_prev, wait=wait):
            if wait is not None:
                t.wait_ge(wait[0].h, wait[1])
            t.wait_ge(ssq.h, sqv1)
            if row_prev:
                t.wait_ge(srow.h, row_prev)
            for k in range(KT):
                t.matmul(ps_s[0:1, :w], ones_f[:, :], xc[:, k * CH:k * CH + w],
                         start=(k == 0), stop=(k == KT - 1))
            last = None
            for k in range(4):
                last = t.matmul(ps_q[0:1, :w], ones_f[:, :], sq[:, k * CH:k * CH + w],
                                start=(k == 0), stop=False)
            last.then_inc(smm.h, 1)
        smm.v += 1
        c.on("tensor", fmm1)
        mv1 = smm.v

        def fsq2(s, mv1=mv1):
            s.wait_ge(smm.h, mv1)
            last = None
            for k in range(4, KT):
                last = s.activation(sq[:, (k - 4) * CH:(k - 4) * CH + w],
                                    xc[:, k * CH:k * CH + w], AF.Square)
            last.then_inc(ssq.h, 1)
        ssq.v += 1
        c.on("scalar", fsq2)
        sqv2 = ssq.v

        def fmm2(t, sqv2=sqv2):
            t.wait_ge(ssq.h, sqv2)
            last = None
            for k in range(4, KT):
                last = t.matmul(ps_q[0:1, :w], ones_f[:, :], sq[:, (k - 4) * CH:(k - 4) * CH + w],
                                start=False, stop=(k == KT - 1))
            last.then_inc(smm.h, 1)
        smm.v += 1
        c.on("tensor", fmm2)

    mmv = smm.v

    def frow(v, mmv=mmv):
        v.wait_ge(smm.h, mmv)
        v.tensor_scalar_mul(strow[0:1, 0 * CH:0 * CH + w], ps_s[0:1, :w], 1.0 / D)
        v.tensor_tensor(strow[0:1, 1 * CH:1 * CH + w],
                        strow[0:1, 0 * CH:0 * CH + w],
                        strow[0:1, 0 * CH:0 * CH + w], ALU.mult)
        v.scalar_tensor_tensor(strow[0:1, 1 * CH:1 * CH + w],
                               ps_q[0:1, :w], 1.0 / D,
                               strow[0:1, 1 * CH:1 * CH + w],
                               ALU.mult, ALU.subtract).then_inc(srow.h, 1)
    srow.v += 1
    c.on("vector", frow)

    rowv = srow.v

    def fsqr(s, rowv=rowv):
        s.wait_ge(srow.h, rowv)
        s.activation(strow[0:1, 1 * CH:1 * CH + w],
                     strow[0:1, 1 * CH:1 * CH + w], AF.Sqrt,
                     bias=1e-5, scale=1.0).then_inc(ssct.h, 1)
    ssct.v += 1
    c.on("scalar", fsqr)

    sctv = ssct.v

    def frow2(v, sctv=sctv, r2a=srow2.v + 1):
        v.wait_ge(ssct.h, sctv)
        v.reciprocal(strow[0:1, 2 * CH:2 * CH + w],
                     strow[0:1, 1 * CH:1 * CH + w]).then_inc(srow2.h, 1)
        v.wait_ge(srow2.h, r2a)
        v.scalar_tensor_tensor(strow[0:1, 3 * CH:3 * CH + w],
                               strow[0:1, 0 * CH:0 * CH + w], -1.0,
                               strow[0:1, 2 * CH:2 * CH + w],
                               ALU.mult, ALU.mult).then_inc(srow2.h, 1)
    srow2.v += 2
    c.on("vector", frow2)

    r2v, ev_prev = srow2.v, sev.v

    def fbc(t, r2v=r2v, ev_prev=ev_prev):
        t.wait_ge(srow2.h, r2v)
        if ev_prev:
            t.wait_ge(sev.h, ev_prev)
        t.matmul(ps_b1[:, :w], onesr_f[0:1, :], strow[0:1, 2 * CH:2 * CH + w],
                 start=True, stop=True)
        t.matmul(ps_b2[:, :w], onesr_f[0:1, :], strow[0:1, 3 * CH:3 * CH + w],
                 start=True, stop=True).then_inc(sbc.h, 1)
    sbc.v += 1
    c.on("tensor", fbc)

    bcv = sbc.v

    def fev(v, bcv=bcv):
        v.wait_ge(sbc.h, bcv)
        v.tensor_copy(rstd_b[:, :w], ps_b1[:, :w])
        last = v.tensor_copy(negm_b[:, :w], ps_b2[:, :w])
        if xn is not None:
            for k in range(KT):
                v.tensor_tensor(tmp[:, :w], xc[:, k * CH:k * CH + w], rstd_b[:, :w], ALU.mult)
                last = v.tensor_tensor(xn[:, k * stride + xoff:k * stride + xoff + w],
                                       tmp[:, :w], negm_b[:, :w], ALU.add)
        last.then_inc(sev.h, 1)
    sev.v += 1
    c.on("vector", fev)
    return sev, sev.v


def emit_ln_all(c: Ctx, tag, x_dram, xn):
    for ci in range(NCH):
        sld, v = load_x_chunk(c, x_dram, ci, tag=tag + "lx%d" % ci)
        emit_ln_chunk(c, tag, ci, xn, wait=(sld, v))
        c.flush()


def emit_proj(c: Ctx, tag, lhsT, rhs, n_k, n_of, chunks, epil, psA=None, psB=None):
    smm = c.sem(tag + "mm")
    sev = c.sem(tag + "ev")
    psA = psA or c.t["mm_psA"]
    psB = psB or c.t["mm_psB"]
    i = 0
    for of in range(n_of):
        for ci in range(len(chunks)):
            off, w = chunks[ci]
            ps = (psA, psB)[i % 2]
            ev_need = sev.v - 1

            def fmm(t, of=of, ci=ci, ps=ps, w=w, ev_need=ev_need):
                if ev_need > 0:
                    t.wait_ge(sev.h, ev_need)
                last = None
                for k in range(n_k):
                    last = t.matmul(ps[:, :w], lhsT(k, of), rhs(k, ci),
                                    start=(k == 0), stop=(k == n_k - 1))
                last.then_inc(smm.h, 1)
            smm.v += 1
            c.on("tensor", fmm)
            epil(of, ci, ps, smm.v, smm, sev)
            sev.v += 1
            i += 1
    assert sev.v == smm.v


def dma_weight(c: Ctx, tag, dram_ap, sbuf_t, rows, cols):
    s = c.sem(tag)

    def f(g):
        src = dram_ap.rearrange("(k p) o -> p k o", p=128)
        dst = sbuf_t[:, 0:(rows // 128) * cols].rearrange("p (k o) -> p k o", o=cols)
        g.dma_start(dst, src).then_inc(s.h, 16)
    s.v += 16
    c.on("gpsimd", f)
    return s, s.v


def store_x_chunk(c: Ctx, x_out, ci, wait_sem, wait_v, w=CH):
    sout = c.sem("sout")
    xc = c.t["xc"]
    off = ci * CH

    def f(sy, off=off, w=w, wait_v=wait_v):
        sy.wait_ge(wait_sem.h, wait_v)
        dst = x_out.rearrange("k p t -> p k t")[:, :, off:off + w]
        sy.dma_start(dst, xc[:, :].rearrange("p (k t) -> p k t", k=KT)[:, :, :w]).then_inc(sout.h, 16)
    sout.v += 16
    c.on("sync", f)


def emit_l0f(c: Ctx, x_in, x_out, wd):
    w_s = c.sb("l0f_w", [128, KT * D], BF16)
    a1_s = c.sb("l0f_a1", [128, KT * D], BF16)
    a2_s = c.sb("l0f_a2", [128, KT * D], BF16)
    b_s = c.sb("l0f_b", [128, 8], F32)
    xn = c.sb("xn", [128, KT * N], BF16)
    h1 = c.sb("h1", [128, KT * N], BF16)
    h2 = c.sb("h2", [128, KT * N], BF16)
    tmp2 = c.sb("tmp2", [128, 2 * CH], F32)

    dma_weight(c, "wl1", wd["l0f_w_d"], w_s, D, D)
    dma_weight(c, "wl2", wd["l0f_a1_d"], a1_s, D, D)
    dma_weight(c, "wl3", wd["l0f_a2_d"], a2_s, D, D)
    sb_ = c.sem("wl4")

    def fb(sy):
        sy.dma_start(b_s[:, :], wd["l0f_b_d"]).then_inc(sb_.h, 16)
    sb_.v += 16
    c.on("sync", fb)

    emit_ln_all(c, "ln", x_in, xn)
    c.flush()
    c.dma_barrier(c.sem("wl1"), c.sem("wl2"), c.sem("wl3"), c.sem("wl4"))

    def wsl(t):
        return lambda k, of: t[:, k * D + of * 128: k * D + (of + 1) * 128]

    def xsl(t):
        return lambda k, ci: t[:, k * N + ci * CH: (k * N + ci * CH) + CH]

    chunks = [(i * CH, CH) for i in range(NCH)]

    def cp_epil(dst):
        def epil(of, ci, ps, mmv, smm, sev):
            def f(s, of=of, ci=ci, ps=ps, mmv=mmv):
                s.wait_ge(smm.h, mmv)
                s.copy(dst[:, of * N + ci * CH: of * N + ci * CH + CH],
                       ps[:, :CH]).then_inc(sev.h, 1)
            c.on("scalar", f)
        return epil

    emit_proj(c, "f1", wsl(w_s), xsl(xn), KT, KT, chunks, cp_epil(h1))
    c.flush()
    emit_proj(c, "f2", wsl(a1_s), xsl(h1), KT, KT, chunks, cp_epil(h2))
    c.flush()

    smm = c.sem("f3mm")
    sth = c.sem("f3th")
    sv = c.sem("f3v")
    psA, psB = c.t["mm_psA"], c.t["mm_psB"]
    for ci in range(NCH):
        sld, ldv = load_x_chunk(c, x_in, ci, tag="f3lx%d" % ci)
        for of in range(KT):
            i = ci * KT + of
            ps = (psA, psB)[i % 2]
            ev_need = sth.v - 1

            def fmm(t, of=of, ci=ci, ps=ps, ev_need=ev_need):
                if ev_need > 0:
                    t.wait_ge(sth.h, ev_need)
                last = None
                for k in range(KT):
                    last = t.matmul(ps[:, :CH], a2_s[:, k * D + of * 128:k * D + (of + 1) * 128],
                                    h2[:, k * N + ci * CH:k * N + ci * CH + CH],
                                    start=(k == 0), stop=(k == KT - 1))
                last.then_inc(smm.h, 1)
            smm.v += 1
            c.on("tensor", fmm)
            mmv = smm.v
            vprev = sv.v
            slot = tmp2[:, (i % 2) * CH:(i % 2) * CH + CH]

            def fs(s, of=of, ps=ps, mmv=mmv, vprev=vprev, slot=slot):
                s.wait_ge(smm.h, mmv)
                if vprev > 1:
                    s.wait_ge(sv.h, vprev - 1)
                s.activation(slot, ps[:, :CH], AF.Tanh,
                             bias=b_s[:, of:of + 1], scale=1.0).then_inc(sth.h, 1)
            sth.v += 1
            c.on("scalar", fs)
            thv = sth.v

            def fv(v, of=of, ci=ci, thv=thv, ldv=ldv, slot=slot):
                v.wait_ge(sth.h, thv)
                v.wait_ge(c.sem("f3lx%d" % ci).h, ldv)
                sl = c.t["xc"][:, of * CH:(of + 1) * CH]
                v.tensor_tensor(sl, sl, slot, ALU.add).then_inc(sv.h, 1)
            sv.v += 1
            c.on("vector", fv)
        store_x_chunk(c, x_out, ci, sv, sv.v)
        c.flush()


def emit_l3f(c: Ctx, x_in, x_out, wd):
    FF = 4096
    FT = FF // 128
    w1b = c.sb("w1b", [128, KT * FF], BF16)
    w2b = c.sb("w2b", [128, FT * D], BF16)
    wstage = c.t["xc"]
    ternA = c.t["sq"]
    xnc = c.sb("xnc", [128, KT * CH], BF16)
    h1c = c.sb("h1c", [128, 8 * CH], BF16)
    outacc = c.t["sq"]
    b1_s = c.sb("b1_s", [128, FT], F32)
    b2_s = c.sb("b2_s", [128, KT], F32)
    acc1 = c.sb("acc1", [128, KT + FT + 2], F32)
    srow1 = c.sb("trow", [1, 16], F32)
    thr = c.sb("thr", [128, 4], F32)
    ps2A = c.ps("ps2A", [128, CH])
    ps2B = c.ps("ps2B", [128, CH])

    sq = c.t["sq"]
    sdw = c.sem("l3dw")
    sbias = c.sem("l3bias")

    def fbias(sy):
        sy.dma_start(b1_s[:, :], wd["l3f_b1_d"]).then_inc(sbias.h, 16)
        sy.dma_start(b2_s[:, :], wd["l3f_b2_d"]).then_inc(sbias.h, 16)
    sbias.v += 32
    c.on("sync", fbias)

    sred = c.sem("l3red")
    npieces = KT + FT
    for p in range(npieces):
        if p < KT:
            src = wd["l3f_w1t_d"][p * 128:(p + 1) * 128, :]
            wcols = FF
        else:
            k2 = p - KT
            src = wd["l3f_w2t_d"][k2 * 128:(k2 + 1) * 128, :]
            wcols = D
        redprev = sred.v

        def fdma(g, p=p, src=src, wcols=wcols, redprev=redprev):
            if redprev:
                g.wait_ge(sred.h, redprev)
            g.dma_start(wstage[:, :wcols], src).then_inc(sdw.h, 16)
        sdw.v += 16
        c.on("gpsimd", fdma)
        dv = sdw.v

        def fred(v, p=p, dv=dv, wcols=wcols):
            v.wait_ge(sdw.h, dv)
            v.reduce_sum(acc1[:, p:p + 1], wstage[:, :wcols], axis=AX.X,
                         apply_absolute_value=True).then_inc(sred.h, 1)
        sred.v += 1
        c.on("vector", fred)
    c.flush()

    ps_s = c.t["ps_s"]
    stsc = c.sem("l3sc")

    def fsc_t(t):
        t.matmul(ps_s[0:1, 0:KT], c.t["ones_f"][:, :], acc1[:, 0:KT], start=True, stop=True)
        t.matmul(ps_s[0:1, KT:KT + FT], c.t["ones_f"][:, :], acc1[:, KT:KT + FT],
                 start=True, stop=True).then_inc(stsc.h, 1)
    stsc.v += 1
    c.on("tensor", fsc_t)
    scv = stsc.v

    def fsc_v(v, scv=scv):
        v.wait_ge(stsc.h, scv)
        v.reduce_sum(srow1[0:1, 0:1], ps_s[0:1, 0:KT], axis=AX.X)
        v.reduce_sum(srow1[0:1, 1:2], ps_s[0:1, KT:KT + FT], axis=AX.X)
        v.tensor_scalar_mul(srow1[0:1, 2:3], srow1[0:1, 0:1], 0.5 / (D * FF))
        v.tensor_scalar_mul(srow1[0:1, 3:4], srow1[0:1, 1:2], 0.5 / (D * FF))
        v.tensor_scalar_max(srow1[0:1, 2:3], srow1[0:1, 2:3], 0.5e-8)
        v.tensor_scalar_max(srow1[0:1, 3:4], srow1[0:1, 3:4], 0.5e-8)
        v.tensor_scalar_mul(srow1[0:1, 4:5], srow1[0:1, 2:3], -1.0)
        v.tensor_scalar_mul(srow1[0:1, 5:6], srow1[0:1, 3:4], -1.0).then_inc(stsc.h, 1)
    stsc.v += 1
    c.on("vector", fsc_v)
    scv2 = stsc.v

    def fsc_t2(t, scv2=scv2):
        t.wait_ge(stsc.h, scv2)
        t.matmul(c.t["ps_b1"][:, 0:4], c.t["onesr_f"][0:1, :], srow1[0:1, 2:6],
                 start=True, stop=True).then_inc(stsc.h, 1)
    stsc.v += 1
    c.on("tensor", fsc_t2)
    scv3 = stsc.v

    def fsc_v2(v, scv3=scv3):
        v.wait_ge(stsc.h, scv3)
        v.tensor_copy(thr[:, :], c.t["ps_b1"][:, 0:4]).then_inc(stsc.h, 1)
    stsc.v += 1
    c.on("vector", fsc_v2)
    c.flush()

    stn = c.sem("l3tn")
    for p in range(npieces):
        if p < KT:
            src = wd["l3f_w1t_d"][p * 128:(p + 1) * 128, :]
            wcols, tc, ntc = FF, 0, 2
            dst = w1b[:, p * FF:(p + 1) * FF]
        else:
            k2 = p - KT
            src = wd["l3f_w2t_d"][k2 * 128:(k2 + 1) * 128, :]
            wcols, tc, ntc = D, 1, 3
            dst = w2b[:, k2 * D:(k2 + 1) * D]
        tprev = stn.v

        def fdma(g, src=src, wcols=wcols, tprev=tprev):
            if tprev:
                g.wait_ge(stn.h, tprev)
            g.dma_start(wstage[:, :wcols], src).then_inc(sdw.h, 16)
        sdw.v += 16
        c.on("gpsimd", fdma)
        dv = sdw.v

        def ftn(v, dv=dv, wcols=wcols, tc=tc, ntc=ntc, dst=dst):
            v.wait_ge(sdw.h, dv)
            v.tensor_scalar(ternA[:, :wcols], wstage[:, :wcols], thr[:, ntc:ntc + 1], None, ALU.is_le)
            v.scalar_tensor_tensor(dst, wstage[:, :wcols], thr[:, tc:tc + 1],
                                   ternA[:, :wcols], ALU.is_ge, ALU.subtract).then_inc(stn.h, 1)
        stn.v += 1
        c.on("vector", ftn)
    c.flush()

    c.dma_barrier(sbias)
    psA, psB = c.t["mm_psA"], c.t["mm_psB"]
    for ci in range(NCH):
        sld, ldv = load_x_chunk(c, x_in, ci, tag="l3lx%d" % ci)
        lnsem, lnv = emit_ln_chunk(c, "l3ln", ci, xnc, wait=(sld, ldv), xn_ktile_stride=CH)
        c.flush()
        for slab in range(4):
            smm = c.sem("l3m%d_%d" % (ci, slab))
            sg = c.sem("l3g%d_%d" % (ci, slab))
            for of1 in range(8):
                fft = slab * 8 + of1
                ps = (psA, psB)[of1 % 2]
                gprev = sg.v - 1

                def fmm(t, fft=fft, ps=ps, gprev=gprev):
                    if gprev > 0:
                        t.wait_ge(sg.h, gprev)
                    last = None
                    for k in range(KT):
                        last = t.matmul(ps[:, :CH],
                                        w1b[:, k * FF + fft * 128:k * FF + (fft + 1) * 128],
                                        xnc[:, k * CH:(k + 1) * CH],
                                        start=(k == 0), stop=(k == KT - 1))
                    last.then_inc(smm.h, 1)
                smm.v += 1
                c.on("tensor", fmm)
                mmv = smm.v

                def fgelu(s, fft=fft, of1=of1, ps=ps, mmv=mmv):
                    s.wait_ge(smm.h, mmv)
                    s.activation(h1c[:, of1 * CH:(of1 + 1) * CH], ps[:, :CH], AF.Gelu,
                                 bias=b1_s[:, fft:fft + 1], scale=1.0).then_inc(sg.h, 1)
                sg.v += 1
                c.on("scalar", fgelu)
            sv2 = c.sem("l3v%d_%d" % (ci, slab))
            gfin = sg.v
            for of2 in range(KT):
                ps = (ps2A, ps2B)[of2 % 2]
                vprev = sv2.v - 1

                def fmm2(t, of2=of2, slab=slab, ps=ps, vprev=vprev, gfin=gfin):
                    t.wait_ge(sg.h, gfin)
                    if vprev > 0:
                        t.wait_ge(sv2.h, vprev)
                    last = None
                    for k2 in range(8):
                        gk = slab * 8 + k2
                        last = t.matmul(ps[:, :CH],
                                        w2b[:, gk * D + of2 * 128:gk * D + (of2 + 1) * 128],
                                        h1c[:, k2 * CH:(k2 + 1) * CH],
                                        start=(k2 == 0), stop=(k2 == 7))
                    last.then_inc(smm.h, 1)
                smm.v += 1
                c.on("tensor", fmm2)
                mmv = smm.v

                def facc(v, of2=of2, slab=slab, ps=ps, mmv=mmv):
                    v.wait_ge(smm.h, mmv)
                    dst = outacc[:, of2 * CH:(of2 + 1) * CH]
                    if slab == 0:
                        v.tensor_copy(dst, ps[:, :CH]).then_inc(sv2.h, 1)
                    else:
                        v.tensor_tensor(dst, dst, ps[:, :CH], ALU.add).then_inc(sv2.h, 1)
                sv2.v += 1
                c.on("vector", facc)
            c.flush()
        sfin = c.sem("l3fin%d" % ci)
        for of in range(KT):
            def ffin(v, of=of, ci=ci):
                sl = c.t["xc"][:, of * CH:(of + 1) * CH]
                v.tensor_tensor(sl, sl, outacc[:, of * CH:(of + 1) * CH], ALU.add)
                v.tensor_scalar(sl, sl, b2_s[:, of:of + 1], None, ALU.add).then_inc(sfin.h, 1)
            sfin.v += 1
            c.on("vector", ffin)
        store_x_chunk(c, x_out, ci, sfin, sfin.v)
        c.flush()


STAGE_DRAM = {
    "l0f": [("l0f_w_d", [D, D]), ("l0f_a1_d", [D, D]), ("l0f_a2_d", [D, D]),
            ("l0f_b_d", [128, 8])],
    "l3f": [("l3f_w1t_d", [D, 4096]), ("l3f_w2t_d", [4096, D]),
            ("l3f_b1_d", [128, 32]), ("l3f_b2_d", [128, 8])],
}

EMITTERS = {}


def _reg():
    EMITTERS["l0f"] = emit_l0f
    EMITTERS["l3f"] = emit_l3f


def col128(v):
    n = v.shape[0]
    return np.ascontiguousarray(v.reshape(n // 128, 128).T)


def stage_weights(stage, inp):
    if stage == "l0f":
        return {"l0f_w_d": inp["L0_ffn_weight"],
                "l0f_a1_d": inp["L0_ffn_aux_weight1"],
                "l0f_a2_d": inp["L0_ffn_aux_weight2"],
                "l0f_b_d": col128(inp["L0_ffn_bias"])}
    if stage == "l3f":
        return {"l3f_w1t_d": np.ascontiguousarray(inp["L3_ffn_W1"].T),
                "l3f_w2t_d": np.ascontiguousarray(inp["L3_ffn_W2"].T),
                "l3f_b1_d": col128(inp["L3_ffn_b1"]),
                "l3f_b2_d": col128(inp["L3_ffn_b2"])}
    raise KeyError(stage)


_BUILD_CACHE = {}


DEBUG_DUMPS = []


def build_stage(stage):
    if stage in _BUILD_CACHE:
        return _BUILD_CACHE[stage]
    if "l0f" not in EMITTERS:
        _reg()
    stack = ExitStack()
    nc = bass.Bass("TRN2", target_bir_lowering=False)
    x_in = nc.dram_tensor("x_in", [KT, 128, N], F32, kind="ExternalInput").ap()
    x_out = nc.dram_tensor("x_out", [KT, 128, N], F32, kind="ExternalOutput").ap()
    wd = {nm: nc.dram_tensor(nm, list(sh), F32, kind="ExternalInput").ap()
          for nm, sh in STAGE_DRAM[stage]}
    c = Ctx(nc, stack)
    emit_setup(c)
    alloc_shared(c, sq_cols=(4 * CH if stage == "l2f" else KT * CH))
    EMITTERS[stage](c, x_in, x_out, wd)
    import os as _os2
    for nm in _os2.environ.get("DBG_DUMP", "").split(","):
        if nm and nm in c.t:
            tt = c.t[nm]
            sh = [tt.shape[0], tt.shape[1]]
            dmp = nc.dram_tensor("dbg_" + nm, sh, F32, kind="ExternalOutput").ap()
            sd = c.sem("sout")

            def fd(g, tt=tt, dmp=dmp, sd=sd):
                g.dma_start(dmp, tt[:, :]).then_inc(sd.h, 16)
            sd.v += 16
            c.on("gpsimd", fd)
            c.flush()
    sout = c.sem("sout")

    def fw(sy):
        sy.wait_ge(sout.h, sout.v)
    c.on("sync", fw)
    c.flush()
    stack.close()
    _BUILD_CACHE[stage] = nc
    return nc


def x_to_dev(xb):
    return np.ascontiguousarray(xb.T.reshape(KT, 128, N))


def x_from_dev(xd):
    return np.ascontiguousarray(xd.reshape(D, N).T)


def run_stage(stage, xs, inp):
    nc = build_stage(stage)
    w = stage_weights(stage, inp)
    maps = [dict(x_in=xs[ci], **w) for ci in range(8)]
    res = run_bass_kernel_spmd(nc, maps, core_ids=list(range(8)))
    return [np.asarray(r["x_out"]) for r in res.results]


STAGE_ORDER = ["l0a", "l0f", "l1a", "l1f", "l2a", "l2f", "l3a", "l3f"]


def kernel(**inputs):
    xs = [x_to_dev(inputs["L0_attn_x"][b]) for b in range(8)]
    for stage in STAGE_ORDER:
        xs = run_stage(stage, xs, inputs)
    return np.stack([x_from_dev(x) for x in xs])


def emit_mha(c: Ctx, kind, x_in, x_out, wd):
    nc = c.nc
    is_l1 = kind == "l1a"
    xn = c.sb("xn", [128, KT * N], BF16)
    wqk = c.sb("wqk", [128, KT * 2048], BF16)
    wv = c.sb("wv", [128, KT * D], BF16)
    wo = c.sb("wo", [128, KT * D], BF16)
    q_sb = c.sb("q_sb", [128, N], BF16)
    k_sb = c.sb("k_sb", [128, N], BF16)
    qkf = c.t["xc"][:, 0:N]
    rot_sb = c.t["sq"][:, N:2 * N]
    v_tm = c.sb("v_tm", [128, 16 * 128], BF16)
    s_sb = c.t["sq"][:, 0:N]
    p_sb = c.sb("p_sb", [128, N], BF16)
    v_sb = p_sb
    pT_sb = c.sb("pT_sb", [128, 16 * 128], BF16)
    z_sb = c.sb("z_sb", [64, N], BF16) if is_l1 else None
    zbuf = c.sb("zbuf", [128, KT * N], BF16)
    cmask = c.sb("cmask", [128, 4 * CH], BF16)
    mrow = c.sb("mrow", [128, 4], F32)
    ps_tr = c.ps("ps_tr", [128, 512], BF16)
    ps_z = c.ps("ps_z", [64, 128])
    if is_l1:
        qb = c.sb("qb", [128, 8], F32)
        kb = c.sb("kb", [128, 8], F32)
        vb = c.sb("vb", [128, 8], F32)
    else:
        cos2 = c.sb("cos2", [128, N], BF16)
        sin2 = c.sb("sin2", [128, N], BF16)

    import os as _os
    if _os.environ.get("MHA_NOW1", "0") != "1":
        dma_weight(c, kind + "w1", wd[kind + "_wqk_d"], wqk, D, 2048)
    dma_weight(c, kind + "w2", wd[kind + "_wv_d"], wv, D, D)
    dma_weight(c, kind + "w3", wd[kind + "_wo_d"], wo, D, D)
    sx = c.sem(kind + "aux")

    if _os.environ.get("MHA_NOAUX", "0") == "1":
        def faux(g):
            g.wait_ge(sx.h, 0)
        c.on("gpsimd", faux)
        sx.v += 0
        emit_ln_all(c, kind + "ln", x_in, xn)
        c.flush()
        c.dma_barrier(c.sem(kind + "w2"), c.sem(kind + "w3"))
        _skip_aux = True
    else:
        _skip_aux = False

    def faux(g):
        g.dma_start(cmask[:, :], wd["cmask_d"]).then_inc(sx.h, 16)
        if is_l1:
            g.dma_start(qb[:, :], wd["l1a_qb_d"]).then_inc(sx.h, 16)
            g.dma_start(kb[:, :], wd["l1a_kb_d"]).then_inc(sx.h, 16)
            g.dma_start(vb[:, :], wd["l1a_vb_d"]).then_inc(sx.h, 16)
        else:
            g.dma_start(cos2[:, :], wd["l3a_cos_d"]).then_inc(sx.h, 16)
            g.dma_start(sin2[:, :], wd["l3a_sin_d"]).then_inc(sx.h, 16)
    if not _skip_aux:
        sx.v += 64 if is_l1 else 48
        c.on("gpsimd", faux)
        emit_ln_all(c, kind + "ln", x_in, xn)
        c.flush()
        if _os.environ.get("MHA_NOW1", "0") != "1":
            c.dma_barrier(c.sem(kind + "w1"), c.sem(kind + "w2"), c.sem(kind + "w3"), sx)
        else:
            c.dma_barrier(c.sem(kind + "w2"), c.sem(kind + "w3"), sx)

    psA, psB = c.t["mm_psA"], c.t["mm_psB"]

    def proj_pair(tagp, colbase, dst_sb, bias_col, j):
        sp = c.sem(kind + "pp")
        se = c.sem(kind + "pe")
        se_start = se.v
        wsb = wv if colbase is None else wqk
        for ci in range(NCH):
            ps = (psA, psB)[ci % 2]
            eneed = se.v - 1

            def fmm(t, ci=ci, ps=ps, eneed=eneed, j=j, colbase=colbase, wsb=wsb):
                if eneed > 0:
                    t.wait_ge(se.h, eneed)
                cb = j * 128 if colbase is None else colbase
                stride = D if colbase is None else 2048
                last = None
                for k in range(KT):
                    last = t.matmul(ps[:, :CH], wsb[:, k * stride + cb:k * stride + cb + 128],
                                    xn[:, k * N + ci * CH:k * N + ci * CH + CH],
                                    start=(k == 0), stop=(k == KT - 1))
                last.then_inc(sp.h, 1)
            sp.v += 1
            c.on("tensor", fmm)
            mmv = sp.v

            if is_l1:
                def fev(s, ci=ci, ps=ps, mmv=mmv, se_start=se_start):
                    s.wait_ge(sp.h, mmv)
                    if se_start:
                        s.wait_ge(se.h, se_start)
                    s.activation(dst_sb[:, ci * CH:(ci + 1) * CH], ps[:, :CH], AF.Identity,
                                 bias=bias_col, scale=1.0).then_inc(se.h, 1)
                se.v += 1
                c.on("scalar", fev)
            else:
                def fev(s, ci=ci, ps=ps, mmv=mmv, se_start=se_start):
                    s.wait_ge(sp.h, mmv)
                    if se_start:
                        s.wait_ge(se.h, se_start)
                    s.copy(qkf[:, ci * CH:(ci + 1) * CH], ps[:, :CH]).then_inc(se.h, 1)
                se.v += 1
                c.on("scalar", fev)
        if not is_l1 and dst_sb is not v_sb:
            qev = se.v

            def frope(v, qev=qev, dst_sb=dst_sb):
                v.wait_ge(se.h, qev)
                v.tensor_copy(rot_sb[0:32, :], qkf[32:64, :])
                v.tensor_copy(rot_sb[32:64, :], qkf[0:32, :])
                v.tensor_copy(rot_sb[64:96, :], qkf[96:128, :])
                v.tensor_copy(rot_sb[96:128, :], qkf[64:96, :])
                v.tensor_tensor(rot_sb[:, :], rot_sb[:, :], sin2[:, :], ALU.mult)
                v.tensor_tensor(qkf[:, :], qkf[:, :], cos2[:, :], ALU.mult)
                v.tensor_tensor(dst_sb[:, :], qkf[:, :], rot_sb[:, :], ALU.add).then_inc(se.h, 1)
            se.v += 1
            c.on("vector", frope)
        elif not is_l1:
            qev = se.v

            def fvcp(v, qev=qev):
                v.wait_ge(se.h, qev)
                v.tensor_copy(v_sb[:, :], qkf[:, :]).then_inc(se.h, 1)
            se.v += 1
            c.on("vector", fvcp)
        return se

    import os
    npairs = int(os.environ.get("MHA_PAIRS", "8"))
    nqt = int(os.environ.get("MHA_QT", "16"))
    skip_sm = os.environ.get("MHA_SKIP_SM", "0") == "1"
    for j in range(npairs):
        if is_l1:
            sev_ = proj_pair(kind + "v%d" % j, None, v_sb, vb[:, j:j + 1], j)
        else:
            sev_ = proj_pair(kind + "v%d" % j, None, v_sb, None, j)
        str_ = c.sem(kind + "tr")
        ste = c.sem(kind + "te")
        evfin = sev_.v
        for g in range(4):
            teneed = ste.v

            def ftr(t, g=g, evfin=evfin, teneed=teneed, sev_=sev_):
                t.wait_ge(sev_.h, evfin)
                if teneed > 0:
                    t.wait_ge(ste.h, teneed)
                last = None
                for ii in range(4):
                    i = g * 4 + ii
                    last = t.transpose(ps_tr[:, ii * 128:(ii + 1) * 128],
                                       v_sb[:, i * 128:(i + 1) * 128], c.t["idn"][:, :])
                last.then_inc(str_.h, 1)
            str_.v += 1
            c.on("tensor", ftr)
            trv = str_.v

            def fte(v, g=g, trv=trv):
                v.wait_ge(str_.h, trv)
                v.tensor_copy(v_tm[:, g * 512:(g + 1) * 512], ps_tr[:, :]).then_inc(ste.h, 1)
            ste.v += 1
            c.on("vector", fte)
        c.flush()
        if is_l1:
            proj_pair(kind + "q%d" % j, j * 256, q_sb, qb[:, j:j + 1], j)
            proj_pair(kind + "k%d" % j, j * 256 + 128, k_sb, kb[:, j:j + 1], j)
        else:
            proj_pair(kind + "q%d" % j, j * 256, q_sb, None, j)
            proj_pair(kind + "k%d" % j, j * 256 + 128, k_sb, None, j)
        c.flush()

        for hh in range(2):
            h = 2 * j + hh
            base = hh * 64
            for qt in range(nqt):
                nkc = qt // 4 + 1
                nkt = qt + 1
                nk = nkt * 128
                ssc = c.sem(kind + "sc")
                sse = c.sem(kind + "se")
                for kc in range(nkc):
                    ps = (psA, psB)[kc % 2]
                    eneed = sse.v - 1

                    def fsc(t, qt=qt, kc=kc, ps=ps, eneed=eneed, base=base):
                        if eneed > 0:
                            t.wait_ge(sse.h, eneed)
                        t.matmul(ps[:, :CH], q_sb[base:base + 64, qt * 128:(qt + 1) * 128],
                                 k_sb[base:base + 64, kc * CH:(kc + 1) * CH],
                                 start=True, stop=True).then_inc(ssc.h, 1)
                    ssc.v += 1
                    c.on("tensor", fsc)
                    scv = ssc.v
                    diag = (kc == nkc - 1)

                    def fse(v, qt=qt, kc=kc, ps=ps, scv=scv, diag=diag):
                        v.wait_ge(ssc.h, scv)
                        dst = s_sb[:, kc * CH:(kc + 1) * CH]
                        if diag:
                            dl = (qt * 128 - kc * CH) // 128
                            v.tensor_tensor(dst, ps[:, :CH],
                                            cmask[:, dl * CH:(dl + 1) * CH], ALU.add).then_inc(sse.h, 1)
                        else:
                            v.tensor_copy(dst, ps[:, :CH]).then_inc(sse.h, 1)
                    sse.v += 1
                    c.on("vector", fse)
                sev = sse.v

                def fsm1(v, qt=qt, nk=nk, sev=sev, e1=sse.v + 1):
                    v.wait_ge(sse.h, sev)
                    v.memset(mrow[:, 2:3], 0.0)
                    v.reduce_max(mrow[:, 0:1], s_sb[:, :nk], axis=AX.X).then_inc(sse.h, 1)
                    v.wait_ge(sse.h, e1)
                    v.tensor_scalar_mul(mrow[:, 1:2], mrow[:, 0:1], -0.125).then_inc(sse.h, 1)
                sse.v += 2
                c.on("vector", fsm1)
                sev2 = sse.v

                def fsm2(s, qt=qt, nk=nk, sev2=sev2):
                    s.wait_ge(sse.h, sev2)
                    s.activation(p_sb[:, :nk], s_sb[:, :nk], AF.Exp,
                                 bias=mrow[:, 1:2], scale=0.125,
                                 accum_out=mrow[:, 2:3]).then_inc(sse.h, 1)
                sse.v += 1
                c.on("scalar", fsm2)
                sev3 = sse.v

                def fsm3(v, sev3=sev3, nk=nk, ra=sse.v + 1):
                    v.wait_ge(sse.h, sev3)
                    v.reciprocal(mrow[:, 3:4], mrow[:, 2:3]).then_inc(sse.h, 1)
                    v.wait_ge(sse.h, ra)
                    if is_l1:
                        v.tensor_scalar(p_sb[:, :nk], p_sb[:, :nk], mrow[:, 3:4], 0.125,
                                        ALU.mult, ALU.mult).then_inc(sse.h, 1)
                    else:
                        v.tensor_scalar_mul(p_sb[:, :nk], p_sb[:, :nk], mrow[:, 3:4]).then_inc(sse.h, 1)
                sse.v += 2
                c.on("vector", fsm3)
                stp = c.sem(kind + "tp")
                stq = c.sem(kind + "tq")
                pscale = sse.v
                ngrp = (nkt + 3) // 4
                for g in range(ngrp):
                    gw = min(4, nkt - g * 4)
                    tqneed = stq.v

                    def ftr(t, g=g, gw=gw, pscale=pscale, tqneed=tqneed, stp=stp, stq=stq):
                        t.wait_ge(sse.h, pscale)
                        if tqneed > 0:
                            t.wait_ge(stq.h, tqneed)
                        last = None
                        for ii in range(gw):
                            i = g * 4 + ii
                            last = t.transpose(ps_tr[:, ii * 128:(ii + 1) * 128],
                                               p_sb[:, i * 128:(i + 1) * 128], c.t["idn"][:, :])
                        last.then_inc(stp.h, 1)
                    stp.v += 1
                    c.on("tensor", ftr)
                    tpv = stp.v

                    def fte(v, g=g, gw=gw, tpv=tpv, stp=stp, stq=stq):
                        v.wait_ge(stp.h, tpv)
                        v.tensor_copy(pT_sb[:, g * 512:g * 512 + gw * 128],
                                      ps_tr[:, :gw * 128]).then_inc(stq.h, 1)
                    stq.v += 1
                    c.on("vector", fte)
                saz = c.sem(kind + "az")
                tqfin = stq.v
                azprev = saz.v

                def fav(t, qt=qt, nkt=nkt, hh=hh, tqfin=tqfin, azprev=azprev, stq=stq, saz=saz):
                    t.wait_ge(stq.h, tqfin)
                    if azprev:
                        t.wait_ge(saz.h, azprev)
                    last = None
                    for i in range(nkt):
                        last = t.matmul(ps_z[:, :], v_tm[:, i * 128 + hh * 64:i * 128 + hh * 64 + 64],
                                        pT_sb[:, i * 128:(i + 1) * 128],
                                        start=(i == 0), stop=(i == nkt - 1))
                    last.then_inc(saz.h, 1)
                saz.v += 1
                c.on("tensor", fav)
                azv = saz.v

                def fze(v, qt=qt, h=h, azv=azv, saz=saz):
                    v.wait_ge(saz.h, azv)
                    if is_l1:
                        v.tensor_copy(z_sb[:, qt * 128:(qt + 1) * 128], ps_z[:, :]).then_inc(saz.h, 1)
                    else:
                        v.tensor_copy(zbuf[(h % 2) * 64:(h % 2) * 64 + 64,
                                           (h // 2) * N + qt * 128:(h // 2) * N + (qt + 1) * 128],
                                      ps_z[:, :]).then_inc(saz.h, 1)
                saz.v += 1
                c.on("vector", fze)
            if is_l1:
                def fscat(v, h=h):
                    for g in range(16):
                        v.tensor_copy(zbuf[(g % 2) * 64:(g % 2) * 64 + 64,
                                           (g // 2) * N + h * 128:(g // 2) * N + (h + 1) * 128],
                                      z_sb[:, g::16])
                c.on("vector", fscat)
            c.flush()

    smm = c.sem(kind + "omm")
    sv = c.sem(kind + "ov")
    _noproj = _os.environ.get("MHA_NOPROJ", "0") == "1"
    if _os.environ.get("MHA_MEMSET_Z", "0") == "1":
        def fmz(g):
            g.memset(zbuf[:, :], 0.0)
        c.on("gpsimd", fmz)
        c.flush()
    for ci in range(NCH):
        sld, ldv = load_x_chunk(c, x_in, ci, tag=kind + "olx%d" % ci)
        if _noproj:
            store_x_chunk(c, x_out, ci, sld, ldv)
            c.flush()
            continue
        for of in range(KT):
            i = ci * KT + of
            ps = (psA, psB)[i % 2]
            vneed = sv.v - 1

            def fmm(t, of=of, ci=ci, ps=ps, vneed=vneed):
                if vneed > 0:
                    t.wait_ge(sv.h, vneed)
                last = None
                for k in range(KT):
                    last = t.matmul(ps[:, :CH], wo[:, k * D + of * 128:k * D + (of + 1) * 128],
                                    zbuf[:, k * N + ci * CH:k * N + ci * CH + CH],
                                    start=(k == 0), stop=(k == KT - 1))
                last.then_inc(smm.h, 1)
            smm.v += 1
            c.on("tensor", fmm)
            mmv = smm.v
            slot = (c.t["tmp"][:, :CH], c.t["sq"][:, :CH])[i % 2]

            def fs(s, ps=ps, mmv=mmv, vprev=sv.v, slot=slot):
                s.wait_ge(smm.h, mmv)
                if vprev > 1:
                    s.wait_ge(sv.h, vprev - 1)
                s.copy(slot, ps[:, :CH]).then_inc(smm.h, 1)
            smm.v += 1
            c.on("scalar", fs)
            scv = smm.v

            def fv(v, of=of, ci=ci, scv=scv, ldv=ldv, slot=slot):
                v.wait_ge(smm.h, scv)
                v.wait_ge(c.sem(kind + "olx%d" % ci).h, ldv)
                sl = c.t["xc"][:, of * CH:(of + 1) * CH]
                v.tensor_tensor(sl, sl, slot, ALU.add).then_inc(sv.h, 1)
            sv.v += 1
            c.on("vector", fv)
        store_x_chunk(c, x_out, ci, sv, sv.v)
        c.flush()


def emit_l1a(c, x_in, x_out, wd):
    emit_mha(c, "l1a", x_in, x_out, wd)


def emit_l3a(c, x_in, x_out, wd):
    emit_mha(c, "l3a", x_in, x_out, wd)


STAGE_DRAM["l1a"] = [("l1a_wqk_d", [D, 2048]), ("l1a_wv_d", [D, D]), ("l1a_wo_d", [D, D]),
                     ("l1a_qb_d", [128, 8]), ("l1a_kb_d", [128, 8]), ("l1a_vb_d", [128, 8]),
                     ("cmask_d", [128, 4 * CH])]
STAGE_DRAM["l3a"] = [("l3a_wqk_d", [D, 2048]), ("l3a_wv_d", [D, D]), ("l3a_wo_d", [D, D]),
                     ("l3a_cos_d", [128, N]), ("l3a_sin_d", [128, N]), ("cmask_d", [128, 4 * CH])]


def _causal_mask():
    m = np.zeros((128, 4 * CH), np.float32)
    p = np.arange(128)[:, None]
    f = np.arange(CH)[None, :]
    for dl in range(4):
        m[:, dl * CH:(dl + 1) * CH] = np.where(f <= dl * 128 + p, 0.0, -1e30)
    return m


def _rope_tables():
    DH = 64
    inv_freq = 1.0 / 10000.0 ** (np.arange(0, DH, 2, dtype=np.float32) / DH)
    t = np.arange(N, dtype=np.float32)
    freqs = np.outer(t, inv_freq)
    emb = np.concatenate([freqs, freqs], -1)
    cos = np.cos(emb).T.astype(np.float32)
    sin = np.sin(emb).T.astype(np.float32)
    cos2 = np.concatenate([cos, cos], 0)
    sins = np.concatenate([-sin[0:32], sin[32:64]], 0)
    sin2 = np.concatenate([sins, sins], 0)
    return np.ascontiguousarray(cos2), np.ascontiguousarray(sin2)


def _stage_l1a(inp):
    Wq, Wk, Wv = inp["L1_attn_Wq"], inp["L1_attn_Wk"], inp["L1_attn_Wv"]
    bq, bk, bv = inp["L1_attn_bq"], inp["L1_attn_bk"], inp["L1_attn_bv"]
    wqk = np.zeros((D, 2048), np.float32)
    qb = np.zeros((128, 8), np.float32)
    kb = np.zeros((128, 8), np.float32)
    for jj in range(8):
        wqk[:, jj * 256 + 0:jj * 256 + 64] = Wq[2 * jj].T
        wqk[:, jj * 256 + 64:jj * 256 + 128] = Wq[2 * jj + 1].T
        wqk[:, jj * 256 + 128:jj * 256 + 192] = Wk[2 * jj].T
        wqk[:, jj * 256 + 192:jj * 256 + 256] = Wk[2 * jj + 1].T
        qb[0:64, jj] = bq[2 * jj]
        qb[64:128, jj] = bq[2 * jj + 1]
        kb[0:64, jj] = bk[2 * jj]
        kb[64:128, jj] = bk[2 * jj + 1]
    wv = np.ascontiguousarray(Wv.reshape(1024, 1024).T)
    vb = np.ascontiguousarray(bv.reshape(1024).reshape(8, 128).T)
    return {"l1a_wqk_d": np.ascontiguousarray(wqk), "l1a_wv_d": wv,
            "l1a_wo_d": np.ascontiguousarray(inp["L1_attn_proj"]),
            "l1a_qb_d": qb, "l1a_kb_d": kb, "l1a_vb_d": vb, "cmask_d": _causal_mask()}


def _stage_l3a(inp):
    Wq, Wk = inp["L3_attn_Wq"], inp["L3_attn_Wk"]
    wqk = np.zeros((D, 2048), np.float32)
    for jj in range(8):
        wqk[:, jj * 256 + 0:jj * 256 + 128] = Wq.T[:, jj * 128:(jj + 1) * 128]
        wqk[:, jj * 256 + 128:jj * 256 + 256] = Wk.T[:, jj * 128:(jj + 1) * 128]
    cosqk, sinqk = _rope_tables()
    return {"l3a_wqk_d": np.ascontiguousarray(wqk),
            "l3a_wv_d": np.ascontiguousarray(inp["L3_attn_Wv"].T),
            "l3a_wo_d": np.ascontiguousarray(inp["L3_attn_Wo"].T),
            "l3a_cos_d": cosqk, "l3a_sin_d": sinqk, "cmask_d": _causal_mask()}


_OLD_STAGE = stage_weights


def stage_weights(stage, inp):
    if stage == "l1a":
        return _stage_l1a(inp)
    if stage == "l3a":
        return _stage_l3a(inp)
    return _OLD_STAGE(stage, inp)


def _reg2():
    _reg()
    EMITTERS["l1a"] = emit_l1a
    EMITTERS["l3a"] = emit_l3a


_OLD_BUILD = build_stage


def build_stage(stage):
    if not EMITTERS:
        _reg2()
    EMITTERS.setdefault("l1a", emit_l1a)
    EMITTERS.setdefault("l3a", emit_l3a)
    return _OLD_BUILD(stage)


def emit_l2f(c: Ctx, x_in, x_out, wd):
    nc = c.nc
    s = c.sb("s_strm", [128, KT * N], F32)
    NP1 = N + 1
    sbf = c.sb("sbf", [128, KT * NP1], BF16)
    ubuf = c.sb("ubuf", [128, 16384], BF16)
    q_sb = ubuf[:, 0:2048]
    k_sb = ubuf[:, 2048:4096]
    p_sb = ubuf[:, 4096:5120]
    pT_sb = ubuf[:, 5120:6144]
    v_tm = ubuf[:, 6144:8192]
    v_sb = ubuf[:, 4096:6144]
    zl = ubuf[:, 8192:16384]
    wpair = c.sb("wpair", [128, KT * 384], BF16)
    wscr = c.sb("wscr", [128, 4864], BF16)
    wo2 = wscr[:, 0:4096]
    w1s = wscr[:, 0:2048]
    w2s = wscr[:, 2048:4864]
    cosk = c.sb("cosk", [128, N], BF16)
    sink = c.sb("sink", [128, N], BF16)
    etab = c.sb("etab", [128, N], BF16)
    cmask = c.sb("cmask2", [128, 4 * CH], BF16)
    pmask = c.sb("pmask", [128, 4 * CH], BF16)
    mrow = c.sb("mrow2", [128, 4], F32)
    smallp = c.sb("smallp", [128, 2 * 44 + 2 * 8 + 2 * 16 + 2 + 2], F32)
    b1p = smallp[:, 0:88]
    b2p = smallp[:, 88:104]
    lgp = smallp[:, 104:120]
    lbp = smallp[:, 120:136]
    qsp = smallp[:, 136:138]
    ksp = smallp[:, 138:140]
    ps_tr = c.ps("ps_tr2", [128, 512], BF16)
    ps_z = c.ps("ps_z2", [64, 128])
    psA, psB = c.t["mm_psA"], c.t["mm_psB"]
    ps_s, ps_q, ps_b1, ps_b2 = c.t["ps_s"], c.t["ps_q"], c.t["ps_b1"], c.t["ps_b2"]
    rb_sb, qn32, tmp = c.t["rstd_b"], c.t["negm_b"], c.t["tmp"]
    sqb = c.t["sq"]
    strow = c.t["strow"]

    sx = c.sem("l2faux")
    import os as _os7
    _noaux2 = _os7.environ.get("L2F_NOAUX2", "0") == "1"

    def faux(g):
        g.dma_start(cosk[:, :], wd["l2f_cosk_d"]).then_inc(sx.h, 16)
        g.dma_start(sink[:, :], wd["l2f_sink_d"]).then_inc(sx.h, 16)
        g.dma_start(etab[:, :], wd["l2f_etab_d"]).then_inc(sx.h, 16)
        g.dma_start(cmask[:, :], wd["cmask_d"]).then_inc(sx.h, 16)
        g.dma_start(pmask[:, :], wd["l2f_pmask_d"]).then_inc(sx.h, 16)
    if not _noaux2:
        sx.v += 80
        c.on("gpsimd", faux)
    sy_ = c.sem("l2fsp")

    def fsp(sy):
        sy.dma_start(b1p[:, :], wd["l2f_b1p_d"]).then_inc(sy_.h, 16)
        sy.dma_start(b2p[:, :], wd["l2f_b2_d"]).then_inc(sy_.h, 16)
        sy.dma_start(lgp[:, :], wd["l2f_lg_d"]).then_inc(sy_.h, 16)
        sy.dma_start(lbp[:, :], wd["l2f_lb_d"]).then_inc(sy_.h, 16)
        sy.dma_start(qsp[:, :], wd["l2f_qs_d"]).then_inc(sy_.h, 16)
        sy.dma_start(ksp[:, :], wd["l2f_ks_d"]).then_inc(sy_.h, 16)
    if not _noaux2:
        sy_.v += 96
        c.on("sync", fsp)

    import os as _os9
    if _os9.environ.get("DBG_MEMSET", "0") == "1":
        def fmz(g):
            g.memset(ubuf[:, :], 0.0)
            g.memset(wpair[:, :], 0.0)
            g.memset(wscr[:, :], 0.0)
            g.memset(sbf[:, :], 0.0)
            g.memset(s[:, :], 0.0)
            g.memset(c.t["sq"][:, :], 0.0)
            g.memset(tmp[:, :], 0.0)
            g.memset(mrow[:, :], 0.0)
            g.memset(strow[0:1, :], 0.0)
        c.on("gpsimd", fmz)
        c.flush()
    _noln = _os9.environ.get("L2F_NOLN", "0") == "1"
    _nox = _os9.environ.get("L2F_NOX", "0") == "1"
    for ci in range([] if _nox else range(NCH)) if False else (range(0) if _nox else range(NCH)):
        sld, v_ = load_x_chunk(c, x_in, ci, tag="l2flx%d" % ci)
        if _noln:
            def fcp(vv, ci=ci, sld=sld, v_=v_):
                vv.wait_ge(sld.h, v_)
                for k in range(KT):
                    vv.tensor_copy(s[:, k * N + ci * CH:k * N + ci * CH + CH],
                                   c.t["xc"][:, k * CH:(k + 1) * CH])
            c.on("vector", fcp)
        else:
            emit_ln_chunk(c, "l2fln", ci, s, wait=(sld, v_))
        c.flush()
    c.dma_barrier(sx, sy_)

    def shslice(k, off, w):
        base = k * NP1 + (1 if k < 4 else 0) + off
        return sbf[:, base:base + w]

    def shift_emit(tag):
        first = not hasattr(c, "_l2f_sbf_init")
        c._l2f_sbf_init = True

        def f(vv, first=first):
            for k in range(KT):
                if first:
                    vv.memset(sbf[:, k * NP1:k * NP1 + 1], 0.0)
                vv.tensor_copy(sbf[:, k * NP1 + 1:(k + 1) * NP1], s[:, k * N:(k + 1) * N])
        c.on("vector", f)
        c.flush()

    import os as _os4
    _rlevel = int(_os4.environ.get("L2F_ROPE_LEVEL", "5"))

    def lnorm_rope(pj, isq, lyr, sp, emit_mm):
        dst = q_sb if isq else k_sb
        cosT, sinT = cosk, sink
        scol = qsp[:, lyr:lyr + 1] if isq else ksp[:, lyr:lyr + 1]
        se = c.sem("l2fqe")
        rot2 = sqb[:, 0:CH]
        prev_nmv = [0]
        prev_bcrd = [0]
        psfree = {}
        for ci in range(NCH):
            ps = (psA, psB)[ci % 2]
            mmv = emit_mm(ci, ps, psfree.get(ci - 2, 0))
            if _rlevel <= 1:
                def fev0(sc, ps=ps, mmv=mmv, ci=ci):
                    sc.wait_ge(sp.h, mmv)
                    sc.copy(dst[:, ci * CH:(ci + 1) * CH], ps[:, :CH]).then_inc(se.h, 1)
                se.v += 1
                psfree[ci] = se.v
                c.on("scalar", fev0)
                continue

            def fsq(sc, ps=ps, mmv=mmv, ci=ci, pn=prev_nmv[0]):
                sc.wait_ge(sp.h, mmv)
                if pn:
                    sc.wait_ge(se.h, pn)
                sc.activation(tmp[:, :CH], ps[:, :CH], AF.Square).then_inc(se.h, 1)
            se.v += 1
            c.on("scalar", fsq)
            sqv = se.v

            if _rlevel <= 2:
                def fev0(sc, ps=ps, ci=ci, sqv=sqv):
                    sc.wait_ge(se.h, sqv)
                    sc.copy(dst[:, ci * CH:(ci + 1) * CH], ps[:, :CH]).then_inc(se.h, 7)
                se.v += 7
                psfree[ci] = se.v
                c.on("scalar", fev0)
                continue

            def fnm(t, sqv=sqv, ci=ci):
                t.wait_ge(se.h, sqv)
                t.matmul(ps_s[0:1, :CH], c.t["ones_f"][0:64, :], tmp[0:64, :CH],
                         start=True, stop=True)
                t.matmul(ps_q[0:1, :CH], c.t["ones_f"][64:128, :], tmp[64:128, :CH],
                         start=True, stop=True).then_inc(se.h, 1)
            se.v += 1
            c.on("tensor", fnm)
            nmv = se.v

            def fsc2(sc, nmv=nmv):
                sc.wait_ge(se.h, nmv)
                sc.activation(strow[0:1, 0:CH], ps_s[0:1, :CH], AF.Sqrt)
                sc.activation(strow[0:1, CH:2 * CH], ps_q[0:1, :CH], AF.Sqrt).then_inc(se.h, 1)
            se.v += 1
            c.on("scalar", fsc2)
            scv = se.v
            prev_nmv[0] = scv

            def frw(vv, scv=scv, ra=se.v + 1):
                vv.wait_ge(se.h, scv)
                vv.tensor_scalar_max(strow[0:1, 0:CH], strow[0:1, 0:CH], 1e-12)
                vv.tensor_scalar_max(strow[0:1, CH:2 * CH], strow[0:1, CH:2 * CH], 1e-12)
                vv.reciprocal(strow[0:1, 2 * CH:3 * CH], strow[0:1, 0:CH]).then_inc(se.h, 1)
                vv.wait_ge(se.h, ra)
                vv.reciprocal(strow[0:1, 3 * CH:4 * CH], strow[0:1, CH:2 * CH]).then_inc(se.h, 1)
            se.v += 2
            c.on("vector", frw)
            rwv = se.v

            if _rlevel <= 3:
                def fev0(sc, ps=ps, ci=ci, rwv=rwv):
                    sc.wait_ge(se.h, rwv)
                    sc.copy(dst[:, ci * CH:(ci + 1) * CH], ps[:, :CH]).then_inc(se.h, 3)
                se.v += 3
                psfree[ci] = se.v
                c.on("scalar", fev0)
                continue

            def fbc(t, rwv=rwv, pb=prev_bcrd[0]):
                t.wait_ge(se.h, rwv)
                if pb:
                    t.wait_ge(se.h, pb)
                t.matmul(ps_b1[0:64, :CH], c.t["onesr_f"][0:1, 0:64],
                         strow[0:1, 2 * CH:3 * CH], start=True, stop=True)
                t.matmul(ps_b1[64:128, :CH], c.t["onesr_f"][0:1, 0:64],
                         strow[0:1, 3 * CH:4 * CH], start=True, stop=True,
                         tile_position=(0, 64)).then_inc(se.h, 1)
            se.v += 1
            c.on("tensor", fbc)
            bcv = se.v

            if _rlevel <= 4:
                def fev0(sc, ps=ps, ci=ci, bcv=bcv):
                    sc.wait_ge(se.h, bcv)
                    sc.copy(dst[:, ci * CH:(ci + 1) * CH], ps[:, :CH]).then_inc(se.h, 2)
                se.v += 2
                psfree[ci] = se.v
                c.on("scalar", fev0)
                continue

            prev_bcrd[0] = se.v + 2
            psfree[ci] = se.v + 2
            def fvx(vv, ps=ps, ci=ci, bcv=bcv, ra=se.v + 1, e2=se.v + 2, e3=se.v + 3, e4=se.v + 4):
                vv.wait_ge(se.h, bcv)
                vv.tensor_copy(rb_sb[:, :CH], ps_b1[:, :CH]).then_inc(se.h, 1)
                vv.wait_ge(se.h, ra)
                vv.tensor_tensor(qn32[:, :CH], ps[:, :CH], rb_sb[:, :CH], ALU.mult).then_inc(se.h, 1)
                vv.wait_ge(se.h, e2)
                vv.tensor_scalar_mul(qn32[:, :CH], qn32[:, :CH], scol)
                vv.tensor_copy(rot2[0:32, :], qn32[32:64, :CH])
                vv.tensor_copy(rot2[32:64, :], qn32[0:32, :CH])
                vv.tensor_copy(rot2[64:96, :], qn32[96:128, :CH])
                vv.tensor_copy(rot2[96:128, :], qn32[64:96, :CH]).then_inc(se.h, 1)
                vv.wait_ge(se.h, e3)
                vv.tensor_tensor(rot2[:, :], rot2[:, :], sinT[:, ci * CH:(ci + 1) * CH], ALU.mult)
                vv.tensor_tensor(qn32[:, :CH], qn32[:, :CH], cosT[:, ci * CH:(ci + 1) * CH], ALU.mult).then_inc(se.h, 1)
                vv.wait_ge(se.h, e4)
                if isq:
                    vv.tensor_tensor(qn32[:, :CH], qn32[:, :CH], rot2[:, :], ALU.add).then_inc(se.h, 1)
                    vv.wait_ge(se.h, e4 + 1)
                    vv.tensor_tensor(dst[:, ci * CH:(ci + 1) * CH], qn32[:, :CH],
                                     etab[:, ci * CH:(ci + 1) * CH], ALU.mult).then_inc(se.h, 1)
                else:
                    vv.tensor_tensor(dst[:, ci * CH:(ci + 1) * CH], qn32[:, :CH],
                                     rot2[:, :], ALU.add).then_inc(se.h, 2)
            se.v += 6
            c.on("vector", fvx)
        return se

    import os as _os3
    _nlyr = int(_os3.environ.get("L2F_LAYERS", "2"))
    _skip_att = _os3.environ.get("L2F_SKIP_ATT", "0") == "1"
    _skip_ffn = _os3.environ.get("L2F_SKIP_FFN", "0") == "1"
    for lyr in range(_nlyr):
        swq = c.sem("l2fwq")
        if _os3.environ.get("L2F_NOSHIFT", "0") == "1":
            pass
        else:
            shift_emit("sh%d" % lyr)

        sp = c.sem("l2fpp")
        spe = c.sem("l2fpe")
        _npj = 0 if _skip_att else int(_os3.environ.get("L2F_PAIRS", "4"))
        _skip_qk = _os3.environ.get("L2F_SKIP_QK", "0") == "1"
        _skip_loc = _os3.environ.get("L2F_SKIP_LOC", "0") == "1"
        _nofwp = _os3.environ.get("L2F_NOFWP", "0") == "1"
        for pj in range(_npj):
            wp_gate = sp.v

            def fwp(g, lyr=lyr, pj=pj, wp_gate=wp_gate):
                if wp_gate:
                    g.wait_ge(sp.h, wp_gate)
                srcw = wd["l2f_wp_d"][lyr, pj].rearrange("(k p) o -> p k o", p=128)
                g.dma_start(wpair[:, :].rearrange("p (k o) -> p k o", o=384),
                            srcw).then_inc(swq.h, 16)
            swq.v += 16
            c.on("gpsimd", fwp)
            if _nofwp:
                c.flush()
            wpv = swq.v
            for ci in range(NCH):
                ps = (psA, psB)[ci % 2]
                eneed = spe.v - 1

                def fmm(t, ci=ci, ps=ps, pj=pj, eneed=eneed, wpv=wpv):
                    t.wait_ge(swq.h, wpv)
                    if eneed > 0:
                        t.wait_ge(spe.h, eneed)
                    last = None
                    for k in range(KT):
                        last = t.matmul(ps[:, :CH],
                                        wpair[:, k * 384 + 256:k * 384 + 384],
                                        shslice(k, ci * CH, CH),
                                        start=(k == 0), stop=(k == KT - 1))
                    last.then_inc(sp.h, 1)
                sp.v += 1
                c.on("tensor", fmm)
                mmv = sp.v

                def fev(sc, ci=ci, ps=ps, mmv=mmv):
                    sc.wait_ge(sp.h, mmv)
                    sc.copy(v_sb[:, ci * CH:(ci + 1) * CH], ps[:, :CH]).then_inc(spe.h, 1)
                spe.v += 1
                c.on("scalar", fev)
            str_ = c.sem("l2ftr")
            ste = c.sem("l2fte")
            evfin = spe.v
            for g_ in range(4):
                teneed = ste.v

                def ftr(t, g_=g_, evfin=evfin, teneed=teneed):
                    t.wait_ge(spe.h, evfin)
                    if teneed > 0:
                        t.wait_ge(ste.h, teneed)
                    last = None
                    for ii in range(4):
                        i = g_ * 4 + ii
                        last = t.transpose(ps_tr[:, ii * 128:(ii + 1) * 128],
                                           v_sb[:, i * 128:(i + 1) * 128], c.t["idn"][:, :])
                    last.then_inc(str_.h, 1)
                str_.v += 1
                c.on("tensor", ftr)
                trv = str_.v

                def fte(vv, g_=g_, trv=trv):
                    vv.wait_ge(str_.h, trv)
                    vv.tensor_copy(v_tm[:, g_ * 512:(g_ + 1) * 512], ps_tr[:, :]).then_inc(ste.h, 1)
                ste.v += 1
                c.on("vector", fte)
            c.flush()
            _qkl = {"0": [], "q": [True], "k": [False], "qk": [True, False]}[_os3.environ.get("L2F_QKMODE", "qk") if not _skip_qk else "0"]
            for isq in _qkl:
                boff2 = 0 if isq else 128

                def emit_mm(ci, ps, psfree_ev, boff2=boff2, pj=pj, wpv=swq.v):
                    def fmm(t, ci=ci, ps=ps, psfree_ev=psfree_ev, boff2=boff2, pj=pj, wpv=wpv):
                        t.wait_ge(swq.h, wpv)
                        if psfree_ev:
                            t.wait_ge(c.sem("l2fqe").h, psfree_ev)
                        last = None
                        for k in range(KT):
                            last = t.matmul(ps[:, :CH],
                                            wpair[:, k * 384 + boff2:k * 384 + boff2 + 128],
                                            shslice(k, ci * CH, CH),
                                            start=(k == 0), stop=(k == KT - 1))
                        last.then_inc(sp.h, 1)
                    sp.v += 1
                    c.on("tensor", fmm)
                    return sp.v
                lnorm_rope(pj, isq, lyr, sp, emit_mm)
                c.flush()
            if _os3.environ.get("L2F_HALT", "") == "afterqk":
                break
            for hh in range(0 if (_skip_loc or _skip_qk) else 2):
                h = 2 * pj + hh
                base = hh * 64
                ssc = c.sem("l2fsc")
                sse = c.sem("l2fse")
                for wi in range(4):
                    for qt in range(4):
                        kcs = ([wi - 1, wi] if wi > 0 else [wi])
                        nkt0 = (wi - 1 if wi > 0 else wi) * 4
                        nloc = len(kcs) * 4
                        for li, kc in enumerate(kcs):
                            ps = (psA, psB)[li % 2]
                            eneed = sse.v - 1

                            def fsc(t, wi=wi, qt=qt, kc=kc, ps=ps, eneed=eneed, base=base):
                                if eneed > 0:
                                    t.wait_ge(sse.h, eneed)
                                t.matmul(ps[:, :CH],
                                         q_sb[base:base + 64, wi * CH + qt * 128:wi * CH + (qt + 1) * 128],
                                         k_sb[base:base + 64, kc * CH:(kc + 1) * CH],
                                         start=True, stop=True).then_inc(ssc.h, 1)
                            ssc.v += 1
                            c.on("tensor", fsc)
                            scv = ssc.v
                            cur = (kc == wi)

                            def fse(vv, li=li, qt=qt, ps=ps, scv=scv, cur=cur):
                                vv.wait_ge(ssc.h, scv)
                                dst2 = sqb[:, li * CH:(li + 1) * CH]
                                msk = cmask if cur else pmask
                                vv.tensor_tensor(dst2, ps[:, :CH],
                                                 msk[:, qt * CH:(qt + 1) * CH], ALU.add).then_inc(sse.h, 1)
                            sse.v += 1
                            c.on("vector", fse)
                        nk = nloc * 128
                        sev = sse.v

                        def fsm1(vv, nk=nk, sev=sev, e1=sse.v + 1):
                            vv.wait_ge(sse.h, sev)
                            vv.memset(mrow[:, 2:3], 0.0)
                            vv.reduce_max(mrow[:, 0:1], sqb[:, :nk], axis=AX.X).then_inc(sse.h, 1)
                            vv.wait_ge(sse.h, e1)
                            vv.tensor_scalar_mul(mrow[:, 1:2], mrow[:, 0:1], -1.0).then_inc(sse.h, 1)
                        sse.v += 2
                        c.on("vector", fsm1)
                        sev2 = sse.v

                        def fsm2(sc, nk=nk, sev2=sev2):
                            sc.wait_ge(sse.h, sev2)
                            sc.activation(p_sb[:, :nk], sqb[:, :nk], AF.Exp,
                                          bias=mrow[:, 1:2], scale=1.0,
                                          accum_out=mrow[:, 2:3]).then_inc(sse.h, 1)
                        sse.v += 1
                        c.on("scalar", fsm2)
                        sev3 = sse.v

                        def fsm3(vv, sev3=sev3, nk=nk, ra=sse.v + 1):
                            vv.wait_ge(sse.h, sev3)
                            vv.reciprocal(mrow[:, 3:4], mrow[:, 2:3]).then_inc(sse.h, 1)
                            vv.wait_ge(sse.h, ra)
                            vv.tensor_scalar_mul(p_sb[:, :nk], p_sb[:, :nk],
                                                 mrow[:, 3:4]).then_inc(sse.h, 1)
                        sse.v += 2
                        c.on("vector", fsm3)
                        stp = c.sem("l2ftp")
                        stq = c.sem("l2ftq")
                        pscale = sse.v
                        ngrp = nloc // 4
                        for g_ in range(ngrp):
                            tqneed = stq.v

                            def ftr(t, g_=g_, pscale=pscale, tqneed=tqneed):
                                t.wait_ge(sse.h, pscale)
                                if tqneed > 0:
                                    t.wait_ge(stq.h, tqneed)
                                last = None
                                for ii in range(4):
                                    i = g_ * 4 + ii
                                    last = t.transpose(ps_tr[:, ii * 128:(ii + 1) * 128],
                                                       p_sb[:, i * 128:(i + 1) * 128], c.t["idn"][:, :])
                                last.then_inc(stp.h, 1)
                            stp.v += 1
                            c.on("tensor", ftr)
                            tpv = stp.v

                            def fte(vv, g_=g_, tpv=tpv):
                                vv.wait_ge(stp.h, tpv)
                                vv.tensor_copy(pT_sb[:, g_ * 512:(g_ + 1) * 512],
                                               ps_tr[:, :]).then_inc(stq.h, 1)
                            stq.v += 1
                            c.on("vector", fte)
                        saz = c.sem("l2faz")
                        tqfin = stq.v
                        azprev = saz.v

                        def fav(t, wi=wi, nloc=nloc, nkt0=nkt0, hh=hh, tqfin=tqfin, azprev=azprev):
                            t.wait_ge(stq.h, tqfin)
                            if azprev:
                                t.wait_ge(saz.h, azprev)
                            last = None
                            for i in range(nloc):
                                akt = nkt0 + i
                                last = t.matmul(ps_z[:, :],
                                                v_tm[:, akt * 128 + hh * 64:akt * 128 + hh * 64 + 64],
                                                pT_sb[:, i * 128:(i + 1) * 128],
                                                start=(i == 0), stop=(i == nloc - 1))
                            last.then_inc(saz.h, 1)
                        saz.v += 1
                        c.on("tensor", fav)
                        azv = saz.v

                        def fze(vv, wi=wi, qt=qt, h=h, azv=azv):
                            vv.wait_ge(saz.h, azv)
                            vv.tensor_copy(zl[(h % 2) * 64:(h % 2) * 64 + 64,
                                              (h // 2) * N + wi * CH + qt * 128:(h // 2) * N + wi * CH + (qt + 1) * 128],
                                           ps_z[:, :]).then_inc(saz.h, 1)
                        saz.v += 1
                        c.on("vector", fze)
                c.flush()
        if _skip_att:
            pass
        if _os3.environ.get("L2F_HALT", "") == "afterqk":
            break
        swo = c.sem("l2fwo")
        wo_gate = sp.v

        def fwo(g, lyr=lyr, wo_gate=wo_gate):
            g.wait_ge(sp.h, wo_gate)
            src2 = wd["l2f_wo_d"][lyr].rearrange("(k p) o -> p k o", p=128)
            g.dma_start(wo2[:, :].rearrange("p (k o) -> p k o", o=D), src2).then_inc(swo.h, 16)
        swo.v += 16
        c.on("gpsimd", fwo)
        c.dma_barrier(swo)
        smm = c.sem("l2fom")
        svv = c.sem("l2fov")
        for ci in range(0 if _skip_att else NCH):
            for of in range(KT):
                i = ci * KT + of
                ps = (psA, psB)[i % 2]
                vneed = svv.v - 1

                def fmm(t, of=of, ci=ci, ps=ps, vneed=vneed):
                    if vneed > 0:
                        t.wait_ge(svv.h, vneed)
                    last = None
                    for k in range(4):
                        last = t.matmul(ps[:, :CH], wo2[:, k * D + of * 128:k * D + (of + 1) * 128],
                                        zl[:, k * N + ci * CH:k * N + ci * CH + CH],
                                        start=(k == 0), stop=(k == 3))
                    last.then_inc(smm.h, 1)
                smm.v += 1
                c.on("tensor", fmm)
                mmv = smm.v
                slot = (tmp[:, :CH], sqb[:, :CH])[i % 2]

                def fs(sc, ps=ps, mmv=mmv, vprev=svv.v, slot=slot):
                    sc.wait_ge(smm.h, mmv)
                    if vprev > 1:
                        sc.wait_ge(svv.h, vprev - 1)
                    sc.copy(slot, ps[:, :CH]).then_inc(smm.h, 1)
                smm.v += 1
                c.on("scalar", fs)
                scv = smm.v

                def fv(vv, of=of, ci=ci, scv=scv, slot=slot):
                    vv.wait_ge(smm.h, scv)
                    sl = s[:, of * N + ci * CH:of * N + ci * CH + CH]
                    vv.tensor_tensor(sl, sl, slot, ALU.add).then_inc(svv.h, 1)
                svv.v += 1
                c.on("vector", fv)
        c.flush()

        if _skip_ffn:
            continue
        shift_emit("sh2_%d" % lyr)
        sff = c.sem("l2fff")
        sfe = c.sem("l2ffe")
        sga = c.sem("l2fga")
        sw1 = c.sem("l2fw1")
        sw2 = c.sem("l2fw2")
        for ci in range(NCH):
            lnh = ubuf[:, 0:4096]

            def fsq(sc, ci=ci, mmprev=sff.v):
                if mmprev:
                    sc.wait_ge(sff.h, mmprev)
                last = None
                for k in range(4):
                    last = sc.activation(sqb[:, k * CH:(k + 1) * CH],
                                         shslice(k, ci * CH, CH), AF.Square)
                last.then_inc(sff.h, 1)
            sff.v += 1
            c.on("scalar", fsq)
            sqv = sff.v

            def fmm(t, ci=ci, sqv=sqv):
                t.wait_ge(sff.h, sqv)
                for k in range(KT):
                    t.matmul(ps_s[0:1, :CH], c.t["ones_b"][:, :], shslice(k, ci * CH, CH),
                             start=(k == 0), stop=(k == KT - 1))
                last = None
                for k in range(4):
                    last = t.matmul(ps_q[0:1, :CH], c.t["ones_f"][:, :], sqb[:, k * CH:(k + 1) * CH],
                                    start=(k == 0), stop=False)
                last.then_inc(sff.h, 1)
            sff.v += 1
            c.on("tensor", fmm)
            mv1 = sff.v

            def fsq2(sc, ci=ci, mv1=mv1):
                sc.wait_ge(sff.h, mv1)
                last = None
                for k in range(4, KT):
                    last = sc.activation(sqb[:, (k - 4) * CH:(k - 3) * CH],
                                         shslice(k, ci * CH, CH), AF.Square)
                last.then_inc(sff.h, 1)
            sff.v += 1
            c.on("scalar", fsq2)
            sqv2 = sff.v

            def fmm2b(t, sqv2=sqv2):
                t.wait_ge(sff.h, sqv2)
                last = None
                for k in range(4, KT):
                    last = t.matmul(ps_q[0:1, :CH], c.t["ones_f"][:, :],
                                    sqb[:, (k - 4) * CH:(k - 3) * CH],
                                    start=False, stop=(k == KT - 1))
                last.then_inc(sff.h, 1)
            sff.v += 1
            c.on("tensor", fmm2b)
            mmv = sff.v

            def frow(vv, mmv=mmv):
                vv.wait_ge(sff.h, mmv)
                vv.tensor_scalar_mul(strow[0:1, 0:CH], ps_s[0:1, :CH], 1.0 / D)
                vv.tensor_tensor(strow[0:1, CH:2 * CH], strow[0:1, 0:CH], strow[0:1, 0:CH], ALU.mult)
                vv.scalar_tensor_tensor(strow[0:1, CH:2 * CH], ps_q[0:1, :CH], 1.0 / D,
                                        strow[0:1, CH:2 * CH], ALU.mult, ALU.subtract).then_inc(sff.h, 1)
            sff.v += 1
            c.on("vector", frow)
            rwv = sff.v

            def fsqr(sc, rwv=rwv):
                sc.wait_ge(sff.h, rwv)
                sc.activation(strow[0:1, CH:2 * CH], strow[0:1, CH:2 * CH], AF.Sqrt,
                              bias=1e-5, scale=1.0).then_inc(sff.h, 1)
            sff.v += 1
            c.on("scalar", fsqr)
            sqv2 = sff.v

            def frow2(vv, sqv2=sqv2, ra=sff.v + 1):
                vv.wait_ge(sff.h, sqv2)
                vv.reciprocal(strow[0:1, 2 * CH:3 * CH], strow[0:1, CH:2 * CH]).then_inc(sff.h, 1)
                vv.wait_ge(sff.h, ra)
                vv.scalar_tensor_tensor(strow[0:1, 3 * CH:4 * CH], strow[0:1, 0:CH], -1.0,
                                        strow[0:1, 2 * CH:3 * CH], ALU.mult, ALU.mult).then_inc(sff.h, 1)
            sff.v += 2
            c.on("vector", frow2)
            rw2 = sff.v

            def fbc(t, rw2=rw2):
                t.wait_ge(sff.h, rw2)
                t.matmul(ps_b1[:, :CH], c.t["onesr_f"][0:1, :], strow[0:1, 2 * CH:3 * CH],
                         start=True, stop=True)
                t.matmul(ps_b2[:, :CH], c.t["onesr_f"][0:1, :], strow[0:1, 3 * CH:4 * CH],
                         start=True, stop=True).then_inc(sff.h, 1)
            sff.v += 1
            c.on("tensor", fbc)
            bcv = sff.v

            def fnorm(vv, ci=ci, bcv=bcv, lyr=lyr, base=sff.v):
                vv.wait_ge(sff.h, bcv)
                vv.tensor_copy(rb_sb[:, :CH], ps_b1[:, :CH])
                vv.tensor_copy(qn32[:, :CH], ps_b2[:, :CH]).then_inc(sff.h, 1)
                vv.wait_ge(sff.h, base + 1)
                ev = base + 1
                for k in range(KT):
                    sl_ = (tmp[:, :CH], rot2 := sqb[:, 0:CH])[k % 2] if False else tmp[:, :CH]
                    vv.tensor_tensor(sl_, shslice(k, ci * CH, CH), rb_sb[:, :CH], ALU.mult).then_inc(sff.h, 1)
                    ev += 1
                    vv.wait_ge(sff.h, ev)
                    vv.tensor_tensor(sl_, sl_, qn32[:, :CH], ALU.add).then_inc(sff.h, 1)
                    ev += 1
                    vv.wait_ge(sff.h, ev)
                    vv.tensor_scalar(lnh[:, k * CH:(k + 1) * CH], sl_,
                                     lgp[:, lyr * 8 + k:lyr * 8 + k + 1],
                                     lbp[:, lyr * 8 + k:lyr * 8 + k + 1],
                                     ALU.mult, ALU.add).then_inc(sff.h, 1)
                    ev += 1
                    if k < KT - 1:
                        vv.wait_ge(sff.h, ev)
            sff.v += 1 + 3 * KT
            c.on("vector", fnorm)
            lnv = sff.v

            act = ubuf[:, 4096:4096 + 22 * CH]
            if ci == 0 and lyr == 0:
                c._l2f_m1v = {}
                c._l2f_acv = {}
                c._l2f_s2v = {}
            for of1 in range(22):
                slab = 0
                w1gate = c._l2f_m1v.get(("m1", lyr, ci, of1 - 1)) or c._l2f_m1v.get(("m1p",))

                def fw1(g, of1=of1, slab=slab, lyr=lyr, w1gate=w1gate):
                    if w1gate:
                        g.wait_ge(sff.h, w1gate)
                    src = wd["l2f_w1p_d"][lyr].rearrange("(k p) o -> p k o", p=128)
                    g.dma_start(w1s[:, slab:slab + 1024].rearrange("p (k o) -> p k o", o=128),
                                src[:, :, of1 * 128:(of1 + 1) * 128]).then_inc(sw1.h, 16)
                    g.dma_start(w1s[:, slab + 1024:slab + 2048].rearrange("p (k o) -> p k o", o=128),
                                src[:, :, 2816 + of1 * 128:2816 + (of1 + 1) * 128]).then_inc(sw1.h, 16)
                sw1.v += 32
                c.on("gpsimd", fw1)
                w1v = sw1.v
                psgate = c._l2f_acv.get(("ac", lyr, ci, of1 - 1)) or c._l2f_acv.get(("acp",))

                def fmm1(t, of1=of1, slab=slab, w1v=w1v, lnv=lnv, psgate=psgate):
                    t.wait_ge(sw1.h, w1v)
                    t.wait_ge(sff.h, lnv)
                    if psgate:
                        t.wait_ge(sff.h, psgate)
                    for k in range(KT):
                        t.matmul(psA[:, :CH], w1s[:, slab + k * 128:slab + (k + 1) * 128],
                                 lnh[:, k * CH:(k + 1) * CH], start=(k == 0), stop=(k == KT - 1))
                    last = None
                    for k in range(KT):
                        last = t.matmul(psB[:, :CH], w1s[:, slab + 1024 + k * 128:slab + 1024 + (k + 1) * 128],
                                        lnh[:, k * CH:(k + 1) * CH], start=(k == 0), stop=(k == KT - 1))
                    last.then_inc(sff.h, 1)
                sff.v += 1
                c.on("tensor", fmm1)
                m1v = sff.v
                c._l2f_m1v[("m1", lyr, ci, of1)] = m1v
                scr_gate = sga.v

                def fact(sc, of1=of1, lyr=lyr, m1v=m1v, scr_gate=scr_gate):
                    sc.wait_ge(sff.h, m1v)
                    if scr_gate:
                        sc.wait_ge(sga.h, scr_gate)
                    sc.activation(tmp[:, :CH], psB[:, :CH], AF.Gelu,
                                  bias=b1p[:, lyr * 44 + 22 + of1:lyr * 44 + 22 + of1 + 1],
                                  scale=1.0)
                    sc.activation(qn32[:, :CH], psA[:, :CH], AF.Identity,
                                  bias=b1p[:, lyr * 44 + of1:lyr * 44 + of1 + 1],
                                  scale=1.0).then_inc(sff.h, 1)
                sff.v += 1
                c.on("scalar", fact)
                acv = sff.v
                c._l2f_acv[("ac", lyr, ci, of1)] = acv

                def fmul(vv, of1=of1, acv=acv):
                    vv.wait_ge(sff.h, acv)
                    vv.tensor_tensor(act[:, of1 * CH:(of1 + 1) * CH], qn32[:, :CH],
                                     tmp[:, :CH], ALU.mult).then_inc(sga.h, 1)
                sga.v += 1
                c.on("vector", fmul)
            c._l2f_m1v[("m1p",)] = c._l2f_m1v[("m1", lyr, ci, 21)]
            c._l2f_acv[("acp",)] = c._l2f_acv[("ac", lyr, ci, 21)]
            gfin = sga.v
            for of2 in range(KT):
                slab2 = 0
                w2gate = c._l2f_s2v.get(("s2", lyr, ci, of2 - 1)) or c._l2f_s2v.get(("s2p",))

                def fw2(g, of2=of2, slab2=slab2, lyr=lyr, w2gate=w2gate):
                    if w2gate:
                        g.wait_ge(sff.h, w2gate)
                    src = wd["l2f_w2p_d"][lyr].rearrange("(k p) o -> p k o", p=128)
                    g.dma_start(w2s[:, slab2:slab2 + 2816].rearrange("p (k o) -> p k o", o=128),
                                src[:, :, of2 * 128:(of2 + 1) * 128]).then_inc(sw2.h, 16)
                sw2.v += 16
                c.on("gpsimd", fw2)
                w2v = sw2.v
                psgate2 = c._l2f_s2v.get(("s2", lyr, ci, of2 - 1)) or c._l2f_s2v.get(("s2p",))

                def fmm2(t, of2=of2, slab2=slab2, w2v=w2v, gfin=gfin, psgate2=psgate2):
                    t.wait_ge(sw2.h, w2v)
                    t.wait_ge(sga.h, gfin)
                    if psgate2:
                        t.wait_ge(sff.h, psgate2)
                    last = None
                    for k2 in range(22):
                        last = t.matmul(psA[:, :CH] if of2 % 2 == 0 else psB[:, :CH],
                                        w2s[:, slab2 + k2 * 128:slab2 + (k2 + 1) * 128],
                                        act[:, k2 * CH:(k2 + 1) * CH],
                                        start=(k2 == 0), stop=(k2 == 21))
                    last.then_inc(sff.h, 1)
                sff.v += 1
                c.on("tensor", fmm2)
                m2v = sff.v
                slot = (tmp[:, :CH], qn32[:, :CH])[of2 % 2]
                scr2 = sga.v - 1

                def fs2(sc, of2=of2, m2v=m2v, lyr=lyr, slot=slot, scr2=scr2):
                    sc.wait_ge(sff.h, m2v)
                    if scr2 > gfin:
                        sc.wait_ge(sga.h, scr2)
                    sc.activation(slot, (psA if of2 % 2 == 0 else psB)[:, :CH], AF.Identity,
                                  bias=b2p[:, lyr * 8 + of2:lyr * 8 + of2 + 1],
                                  scale=1.0).then_inc(sff.h, 1)
                sff.v += 1
                c.on("scalar", fs2)
                s2v = sff.v
                c._l2f_s2v[("s2", lyr, ci, of2)] = s2v

                def fv2(vv, of2=of2, ci=ci, s2v=s2v, slot=slot):
                    vv.wait_ge(sff.h, s2v)
                    sl = s[:, of2 * N + ci * CH:of2 * N + ci * CH + CH]
                    vv.tensor_tensor(sl, sl, slot, ALU.add).then_inc(sga.h, 1)
                sga.v += 1
                c.on("vector", fv2)
            c._l2f_s2v[("s2p",)] = c._l2f_s2v[("s2", lyr, ci, 7)]
            c.flush()

    sfin = c.sem("l2ffin")
    if _os9.environ.get("L2F_NOFINAL", "0") == "1":
        sout_ = c.sem("sout")

        def fst(sy):
            sy.dma_start(x_out.rearrange("k p t -> p k t"),
                         s[:, :].rearrange("p (k t) -> p k t", t=N)).then_inc(sout_.h, 16)
        sout_.v += 16
        c.on("sync", fst)
        c.flush()
        return
    for ci in range(NCH):
        sld, ldv = load_x_chunk(c, x_in, ci, tag="l2fflx%d" % ci)

        def fadd(vv, ci=ci, ldv=ldv):
            vv.wait_ge(c.sem("l2fflx%d" % ci).h, ldv)
            last = None
            for k in range(KT):
                sl = c.t["xc"][:, k * CH:(k + 1) * CH]
                last = vv.tensor_tensor(sl, sl, s[:, k * N + ci * CH:k * N + ci * CH + CH], ALU.add)
            last.then_inc(sfin.h, 1)
        sfin.v += 1
        c.on("vector", fadd)
        store_x_chunk(c, x_out, ci, sfin, sfin.v)
        c.flush()


STAGE_DRAM["l2f"] = [("l2f_wp_d", [2, 4, D, 384]), ("l2f_wo_d", [2, 512, D]),
                     ("l2f_w1p_d", [2, D, 5632]), ("l2f_w2p_d", [2, 2816, D]),
                     ("l2f_b1p_d", [128, 88]), ("l2f_b2_d", [128, 16]),
                     ("l2f_lg_d", [128, 16]), ("l2f_lb_d", [128, 16]),
                     ("l2f_qs_d", [128, 2]), ("l2f_ks_d", [128, 2]),
                     ("l2f_cosk_d", [128, N]), ("l2f_sink_d", [128, N]),
                     ("l2f_etab_d", [128, N]),
                     ("l2f_pmask_d", [128, 4 * CH]), ("cmask_d", [128, 4 * CH])]


def _stage_l2f(inp):
    DHD, WIN = 64, 512
    half_idx = np.arange(0, DHD, 2, dtype=np.float32)
    inv_freq = 1.0 / 10000.0 ** (half_idx / DHD)
    tt = np.arange(N, dtype=np.float32)
    freqs = tt[:, None] * inv_freq[None, :]
    freqs = np.concatenate([freqs, freqs], -1)
    base = (half_idx + 0.4 * DHD) / (1.4 * DHD)
    power = (tt - N // 2) / float(WIN // 2)
    scale = base[None, :] ** power[:, None]
    scale = np.concatenate([scale, scale], -1)
    cos = np.cos(freqs); sin = np.sin(freqs)
    ck = (cos / scale).T.astype(np.float32)
    sk = (sin / scale).T.astype(np.float32)
    et = (8.0 * scale * scale).T.astype(np.float32)
    def fold(s_):
        return np.concatenate([-s_[0:32], s_[32:64]], 0)
    cosk = np.ascontiguousarray(np.concatenate([ck, ck], 0))
    sink = np.ascontiguousarray(np.concatenate([fold(sk), fold(sk)], 0))
    etab = np.ascontiguousarray(np.concatenate([et, et], 0))
    pm = np.zeros((128, 4 * CH), np.float32)
    p = np.arange(128)[:, None]
    f = np.arange(CH)[None, :]
    for qt in range(4):
        pm[:, qt * CH:(qt + 1) * CH] = np.where(f >= qt * 128 + p, 0.0, -1e30)
    w1 = inp["L2_ffn_w1"]; w2 = inp["L2_ffn_w2"]; b1 = inp["L2_ffn_b1"]
    HID = 2730
    w1p = np.zeros((2, D, 5632), np.float32)
    w1p[:, :, :HID] = w1[:, :, :HID]
    w1p[:, :, 2816:2816 + HID] = w1[:, :, HID:]
    w2p = np.zeros((2, 2816, D), np.float32)
    w2p[:, :HID, :] = w2
    b1p = np.zeros((128, 88), np.float32)
    for l in range(2):
        a = np.zeros(2816, np.float32); a[:HID] = b1[l, :HID]
        g = np.zeros(2816, np.float32); g[:HID] = b1[l, HID:]
        b1p[:, l * 44:l * 44 + 22] = a.reshape(22, 128).T
        b1p[:, l * 44 + 22:l * 44 + 44] = g.reshape(22, 128).T
    def two(vname):
        v = inp[vname]
        return np.ascontiguousarray(np.stack([v[0].reshape(8, 128).T, v[1].reshape(8, 128).T], 1).reshape(128, 16))
    b2p = np.zeros((128, 16), np.float32)
    lgp = np.zeros((128, 16), np.float32)
    lbp = np.zeros((128, 16), np.float32)
    for l in range(2):
        b2p[:, l * 8:(l + 1) * 8] = inp["L2_ffn_b2"][l].reshape(8, 128).T
        lgp[:, l * 8:(l + 1) * 8] = inp["L2_ffn_ln_g"][l].reshape(8, 128).T
        lbp[:, l * 8:(l + 1) * 8] = inp["L2_ffn_ln_b"][l].reshape(8, 128).T
    qsp = np.zeros((128, 2), np.float32)
    ksp = np.zeros((128, 2), np.float32)
    for l in range(2):
        qsp[:, l] = np.tile(inp["L2_ffn_q_scale"][l], 2)
        ksp[:, l] = np.tile(inp["L2_ffn_k_scale"][l], 2)
    wqkv_in = inp["L2_ffn_wqkv"]
    wp = np.zeros((2, 4, D, 384), np.float32)
    for l in range(2):
        for pj in range(4):
            wp[l, pj, :, 0:128] = wqkv_in[l][:, pj * 128:(pj + 1) * 128]
            wp[l, pj, :, 128:256] = wqkv_in[l][:, 512 + pj * 128:512 + (pj + 1) * 128]
            wp[l, pj, :, 256:384] = wqkv_in[l][:, 1024 + pj * 128:1024 + (pj + 1) * 128]
    return {"l2f_wp_d": wp,
            "l2f_wo_d": np.ascontiguousarray(inp["L2_ffn_wo"]),
            "l2f_w1p_d": w1p, "l2f_w2p_d": w2p,
            "l2f_b1p_d": b1p, "l2f_b2_d": b2p, "l2f_lg_d": lgp, "l2f_lb_d": lbp,
            "l2f_qs_d": qsp, "l2f_ks_d": ksp,
            "l2f_cosk_d": cosk, "l2f_sink_d": sink, "l2f_etab_d": etab,
            "l2f_pmask_d": pm, "cmask_d": _causal_mask()}


_OLD_STAGE2 = stage_weights


def stage_weights(stage, inp):
    if stage == "l2f":
        return _stage_l2f(inp)
    return _OLD_STAGE2(stage, inp)


_OLD_BUILD2 = build_stage


def build_stage(stage):
    EMITTERS.setdefault("l2f", emit_l2f)
    return _OLD_BUILD2(stage)


def emit_l2a(c: Ctx, x_in, x_out, wd):
    DI, DS = 2048, 64
    DIT = DI // 128
    xr = c.sb("xr", [128, KT * CH], BF16)
    winslab = c.sb("winslab", [128, KT * 128], BF16)
    xm_sb = c.sb("xm_sb", [128, DIT * (CH + 3)], BF16)
    z_sb2 = c.sb("z_sb2", [128, DIT * CH], BF16)
    u_bf = c.sb("u_bf", [128, DIT * CH], BF16)
    dl32 = c.sb("dl32", [128, DIT * CH], BF16)
    w32 = c.sb("w32", [128, DIT * CH], BF16)
    y32 = c.sb("y32", [128, DIT * CH], F32)
    wxT = c.sb("wxT", [128, DIT * 192], BF16)
    wdtT = c.sb("wdtT", [64, 2048], BF16)
    wouts = c.sb("wouts", [128, DIT * 128], BF16)
    a32 = c.sb("a32", [128, DIT * 64], F32)
    hcar = c.sb("hcar", [128, DIT * 64], F32)
    convw = c.sb("convw", [128, DIT * 4], F32)
    convb = c.sb("convb", [128, DIT], F32)
    bdt = c.sb("bdt", [128, DIT], F32)
    dpcol = c.sb("dpcol", [128, DIT], F32)
    nwcol = c.sb("nwcol", [128, KT], F32)
    dtr_b = c.sb("dtr_b", [64, CH], BF16)
    brow_s = c.sb("brow_s", [1, CH], BF16)
    crow_s = c.sb("crow_s", [1, CH], BF16)
    bb_n = c.sb("bb_n", [128, CH], BF16)
    cc_n = c.sb("cc_n", [128, CH], BF16)
    scr = c.sb("scr", [128, CH], F32)
    ps_w = c.ps("ps_w", [128, CH])
    psA, psB = c.t["mm_psA"], c.t["mm_psB"]
    ps_s, ps_b1 = c.t["ps_s"], c.t["ps_b1"]
    tmp = c.t["tmp"]
    sqb = c.t["sq"]
    xc = c.t["xc"]
    xn32 = w32

    sld = c.sem("l2aw")

    def fw(g):
        g.dma_start(wxT[:, :].rearrange("p (k o) -> p k o", o=192),
                    wd["l2a_wxT_d"].rearrange("(k p) o -> p k o", p=128)).then_inc(sld.h, 16)
        g.dma_start(wdtT[:, :], wd["l2a_wdtT_d"]).then_inc(sld.h, 16)
        g.dma_start(a32[:, :], wd["l2a_alog_d"]).then_inc(sld.h, 16)
        g.dma_start(convw[:, :], wd["l2a_convw_d"]).then_inc(sld.h, 16)
        g.dma_start(convb[:, :], wd["l2a_convb_d"]).then_inc(sld.h, 16)
        g.dma_start(bdt[:, :], wd["l2a_bdt_d"]).then_inc(sld.h, 16)
        g.dma_start(dpcol[:, :], wd["l2a_dp_d"]).then_inc(sld.h, 16)
        g.dma_start(nwcol[:, :], wd["l2a_nw_d"]).then_inc(sld.h, 16)
    sld.v += 128
    c.on("gpsimd", fw)
    c.dma_barrier(sld)

    def fsetup(v):
        v.memset(hcar[:, :], 0.0)
        v.memset(xm_sb[:, :], 0.0)
        v.memset(y32[:, :], 0.0)
    c.on("vector", fsetup)

    def fa(s):
        s.activation(a32[:, :], a32[:, :], AF.Exp)
    c.on("scalar", fa)
    c.flush()

    def fa2(v):
        v.tensor_scalar_mul(a32[:, :], a32[:, :], -1.0)
    c.on("vector", fa2)
    c.flush()

    se = c.sem("l2ase")
    sw = c.sem("l2asw")
    for ci in range(NCH):
        sldx, ldv = load_x_chunk(c, x_in, ci, tag="l2alx%d" % ci)
        emit_ln_chunk(c, "l2aln", ci, xn32, wait=(sldx, ldv), xn_ktile_stride=CH)
        c.flush()

        def fr1(s, base=se.v):
            last = None
            for k in range(KT):
                last = s.activation(sqb[:, k * CH:(k + 1) * CH],
                                    xn32[:, k * CH:(k + 1) * CH], AF.Square)
            last.then_inc(se.h, 1)
        se.v += 1
        c.on("scalar", fr1)
        e1 = se.v

        def fr2(t, e1=e1):
            t.wait_ge(se.h, e1)
            last = None
            for k in range(KT):
                last = t.matmul(ps_s[0:1, :CH], c.t["ones_f"][:, :], sqb[:, k * CH:(k + 1) * CH],
                                start=(k == 0), stop=(k == KT - 1))
            last.then_inc(se.h, 1)
        se.v += 1
        c.on("tensor", fr2)
        e2 = se.v

        def fr3(s, e2=e2):
            s.wait_ge(se.h, e2)
            s.activation(c.t["strow"][0:1, 0:CH], ps_s[0:1, :CH], AF.Sqrt,
                         bias=float(np.finfo(np.float32).eps), scale=1.0 / D).then_inc(se.h, 1)
        se.v += 1
        c.on("scalar", fr3)
        e3 = se.v

        def fr4(v, e3=e3, r1=se.v + 1):
            v.wait_ge(se.h, e3)
            v.reciprocal(c.t["strow"][0:1, CH:2 * CH], c.t["strow"][0:1, 0:CH]).then_inc(se.h, 1)
        se.v += 1
        c.on("vector", fr4)
        e4 = se.v

        def fr5(t, e4=e4):
            t.wait_ge(se.h, e4)
            t.matmul(ps_b1[:, :CH], c.t["onesr_f"][0:1, :], c.t["strow"][0:1, CH:2 * CH],
                     start=True, stop=True).then_inc(se.h, 1)
        se.v += 1
        c.on("tensor", fr5)
        e5 = se.v

        def fr6(v, ci=ci, e5=e5, r1=se.v + 1):
            v.wait_ge(se.h, e5)
            v.tensor_copy(c.t["rstd_b"][:, :CH], ps_b1[:, :CH]).then_inc(se.h, 1)
            v.wait_ge(se.h, r1)
            for k in range(KT):
                v.tensor_tensor(tmp[:, :CH], xn32[:, k * CH:(k + 1) * CH],
                                c.t["rstd_b"][:, :CH], ALU.mult).then_inc(se.h, 1)
                v.wait_ge(se.h, r1 + 1 + 2 * k)
                v.tensor_scalar_mul(xr[:, k * CH:(k + 1) * CH], tmp[:, :CH],
                                    nwcol[:, k:k + 1]).then_inc(se.h, 1)
                if k < KT - 1:
                    v.wait_ge(se.h, r1 + 2 + 2 * k)
        se.v += 1 + 2 * KT
        c.on("vector", fr6)
        xr_done = se.v

        for of in range(32):
            prevmm = se.v

            def fwl2(g, of=of, prevmm=prevmm):
                if prevmm:
                    g.wait_ge(se.h, prevmm)
                src2 = wd["l2a_winT_d"].rearrange("(k p) o -> p k o", p=128)
                g.dma_start(winslab[:, :].rearrange("p (k o) -> p k o", o=128),
                            src2[:, :, of * 128:(of + 1) * 128]).then_inc(sw.h, 16)
            sw.v += 16
            c.on("gpsimd", fwl2)
            wv = sw.v
            evgate = se.v - 1

            def fmm(t, of=of, wv=wv, xr_done=xr_done, evgate=evgate):
                t.wait_ge(sw.h, wv)
                t.wait_ge(se.h, xr_done)
                if evgate > xr_done:
                    t.wait_ge(se.h, evgate)
                ps = (psA, psB)[of % 2]
                last = None
                for k in range(KT):
                    last = t.matmul(ps[:, :CH], winslab[:, k * 128:(k + 1) * 128],
                                    xr[:, k * CH:(k + 1) * CH],
                                    start=(k == 0), stop=(k == KT - 1))
                last.then_inc(se.h, 1)
            se.v += 1
            c.on("tensor", fmm)
            mmv = se.v

            def fev(s, of=of, ci=ci, mmv=mmv):
                s.wait_ge(se.h, mmv)
                ps = (psA, psB)[of % 2]
                if of < DIT:
                    s.copy(xm_sb[:, of * (CH + 3) + 3:(of + 1) * (CH + 3)], ps[:, :CH]).then_inc(se.h, 1)
                else:
                    s.copy(z_sb2[:, (of - DIT) * CH:(of - DIT + 1) * CH], ps[:, :CH]).then_inc(se.h, 1)
            se.v += 1
            c.on("scalar", fev)
        xz_done = se.v
        c.flush()

        for k in range(DIT):
            base = k * (CH + 3)

            def fcv(v, k=k, base=base, e0=se.v):
                v.tensor_scalar_mul(scr[:, :CH], xm_sb[:, base:base + CH],
                                    convw[:, k * 4:k * 4 + 1]).then_inc(se.h, 1)
                for j in range(1, 4):
                    v.wait_ge(se.h, e0 + j)
                    v.scalar_tensor_tensor(scr[:, :CH], xm_sb[:, base + j:base + j + CH],
                                           convw[:, k * 4 + j:k * 4 + j + 1], scr[:, :CH],
                                           ALU.mult, ALU.add).then_inc(se.h, 1)
            se.v += 4
            c.on("vector", fcv)
            cvv = se.v

            def fsl(s, k=k, cvv=cvv):
                s.wait_ge(se.h, cvv)
                s.activation(scr[:, :CH], scr[:, :CH], AF.Silu,
                             bias=convb[:, k:k + 1], scale=1.0).then_inc(se.h, 1)
            se.v += 1
            c.on("scalar", fsl)
            slv = se.v

            def fub(v, k=k, slv=slv):
                v.wait_ge(se.h, slv)
                v.tensor_copy(u_bf[:, k * CH:(k + 1) * CH], scr[:, :CH]).then_inc(se.h, 1)
            se.v += 1
            c.on("vector", fub)
            ubv = se.v

            def fhl(v, k=k, base=base, ubv=ubv):
                v.wait_ge(se.h, ubv)
                v.tensor_copy(xm_sb[:, base:base + 3],
                              xm_sb[:, base + CH:base + CH + 3]).then_inc(se.h, 1)
            se.v += 1
            c.on("vector", fhl)
        u_done = se.v
        c.flush()

        def fxd(t, u_done=u_done):
            last = None
            for k in range(DIT):
                last = t.matmul(psA[:, :CH], wxT[:, k * 192:k * 192 + 128],
                                u_bf[:, k * CH:(k + 1) * CH], start=(k == 0), stop=(k == DIT - 1))
            for k in range(DIT):
                last = t.matmul(ps_w[0:64, :CH], wxT[:, k * 192 + 128:k * 192 + 192],
                                u_bf[:, k * CH:(k + 1) * CH], start=(k == 0), stop=(k == DIT - 1))
            last.then_inc(se.h, 1)
        se.v += 1
        c.on("tensor", fxd)
        xdv = se.v

        def fxe(s, xdv=xdv):
            s.wait_ge(se.h, xdv)
            s.copy(dtr_b[0:64, :CH], psA[0:64, :CH])
            s.copy(bb_n[64:128, :CH], psA[64:128, :CH])
            s.copy(bb_n[0:64, :CH], ps_w[0:64, :CH]).then_inc(se.h, 1)
        se.v += 1
        c.on("scalar", fxe)
        c.flush()

        spv_hist = {}
        for of in range(DIT):
            def fdm(t, of=of, gate=spv_hist.get(of - 2, 0)):
                if gate:
                    t.wait_ge(se.h, gate)
                t.matmul(psA[:, :CH] if of % 2 == 0 else psB[:, :CH],
                         wdtT[:, of * 128:(of + 1) * 128], dtr_b[:, :CH],
                         start=True, stop=True).then_inc(se.h, 1)
            se.v += 1
            c.on("tensor", fdm)
            dmv = se.v

            def fsp_(s, of=of, dmv=dmv, dlgate=spv_hist.get(("dl", of - 1), 0), eA=se.v + 1):
                s.wait_ge(se.h, dmv)
                if dlgate:
                    s.wait_ge(se.h, dlgate)
                ps = (psA, psB)[of % 2]
                s.activation(scr[:, :CH], ps[:, :CH], AF.Abs, bias=bdt[:, of:of + 1], scale=1.0)
                s.activation(tmp[:, :CH], ps[:, :CH], AF.Relu, bias=bdt[:, of:of + 1], scale=1.0)
                s.activation(scr[:, :CH], scr[:, :CH], AF.Exp, bias=0.0,
                             scale=-1.0).then_inc(se.h, 1)
                s.wait_ge(se.h, eA)
                s.activation(scr[:, :CH], scr[:, :CH], AF.Ln, bias=1.0,
                             scale=1.0).then_inc(se.h, 1)
            se.v += 2
            c.on("scalar", fsp_)
            spv = se.v
            spv_hist[of] = spv

            def fdl(v, of=of, spv=spv, e0=se.v + 1):
                v.wait_ge(se.h, spv)
                v.tensor_tensor(dl32[:, of * CH:(of + 1) * CH], scr[:, :CH], tmp[:, :CH],
                                ALU.add).then_inc(se.h, 1)
                v.wait_ge(se.h, e0)
                v.tensor_tensor(w32[:, of * CH:(of + 1) * CH], dl32[:, of * CH:(of + 1) * CH],
                                u_bf[:, of * CH:(of + 1) * CH], ALU.mult).then_inc(se.h, 1)
            se.v += 2
            spv_hist[("dl", of)] = se.v
            c.on("vector", fdl)
        delta_done = se.v
        c.flush()

        sy2 = c.sem("l2asy")
        for n in range(DS):
            def fsel(t, n=n, prev=sy2.v):
                if prev:
                    t.wait_ge(sy2.h, prev)
                t.matmul(ps_w[0:1, :CH], c.t["idn"][64:128, 64 + n:64 + n + 1],
                         bb_n[64:128, :CH], start=True, stop=True)
                t.matmul(ps_b1[0:1, :CH], c.t["idn"][0:64, n:n + 1],
                         bb_n[0:64, :CH], start=True, stop=True).then_inc(sy2.h, 1)
            sy2.v += 1
            c.on("tensor", fsel)
            selv = sy2.v

            def fsee(s, n=n, selv=selv):
                s.wait_ge(sy2.h, selv)
                s.copy(brow_s[0:1, :CH], ps_w[0:1, :CH])
                s.copy(crow_s[0:1, :CH], ps_b1[0:1, :CH]).then_inc(sy2.h, 1)
            sy2.v += 1
            c.on("scalar", fsee)
            seev = sy2.v

            def fbcm(t, n=n, seev=seev):
                t.wait_ge(sy2.h, seev)
                t.matmul(ps_w[:, :CH], c.t["onesr_b"][0:1, :], brow_s[0:1, :CH],
                         start=True, stop=True)
                t.matmul(ps_b1[:, :CH], c.t["onesr_b"][0:1, :], crow_s[0:1, :CH],
                         start=True, stop=True).then_inc(sy2.h, 1)
            sy2.v += 1
            c.on("tensor", fbcm)
            bcmv = sy2.v

            def fbce(s, n=n, bcmv=bcmv):
                s.wait_ge(sy2.h, bcmv)
                s.copy(cc_n[:, :CH], ps_b1[:, :CH])
                s.copy(scr[:, :CH], ps_w[:, :CH]).then_inc(sy2.h, 1)
            sy2.v += 1
            c.on("scalar", fbce)
            bcev = sy2.v
            for k in range(DIT):
                def fda(s, n=n, k=k, bcev=bcev, prevv=sy2.v):
                    s.wait_ge(sy2.h, bcev)
                    if prevv > bcev:
                        s.wait_ge(sy2.h, prevv)
                    s.activation(tmp[:, :CH], dl32[:, k * CH:(k + 1) * CH], AF.Exp,
                                 scale=a32[:, k * 64 + n:k * 64 + n + 1]).then_inc(sy2.h, 1)
                sy2.v += 1
                c.on("scalar", fda)
                dav = sy2.v

                def fscan(v, n=n, k=k, dav=dav, e0=sy2.v + 1, e1=sy2.v + 2, e2=sy2.v + 3):
                    v.wait_ge(sy2.h, dav)
                    v.tensor_tensor(sqb[:, :CH], w32[:, k * CH:(k + 1) * CH],
                                    scr[:, :CH], ALU.mult).then_inc(sy2.h, 1)
                    v.wait_ge(sy2.h, e0)
                    v.tensor_tensor_scan(sqb[:, CH:2 * CH], tmp[:, :CH], sqb[:, 0:CH],
                                         hcar[:, k * 64 + n:k * 64 + n + 1],
                                         ALU.mult, ALU.add).then_inc(sy2.h, 1)
                    v.wait_ge(sy2.h, e1)
                    v.tensor_copy(hcar[:, k * 64 + n:k * 64 + n + 1], sqb[:, 2 * CH - 1:2 * CH])
                    v.tensor_tensor(sqb[:, 2 * CH:3 * CH], sqb[:, CH:2 * CH], cc_n[:, :CH],
                                    ALU.mult).then_inc(sy2.h, 1)
                    v.wait_ge(sy2.h, e2)
                    v.tensor_tensor(y32[:, k * CH:(k + 1) * CH], y32[:, k * CH:(k + 1) * CH],
                                    sqb[:, 2 * CH:3 * CH], ALU.add).then_inc(sy2.h, 1)
                sy2.v += 4
                c.on("vector", fscan)
        scan_done = sy2.v
        c.flush()

        for k in range(DIT):
            def fyd(v, k=k, e0=se.v + 1):
                v.scalar_tensor_tensor(y32[:, k * CH:(k + 1) * CH], u_bf[:, k * CH:(k + 1) * CH],
                                       dpcol[:, k:k + 1], y32[:, k * CH:(k + 1) * CH],
                                       ALU.mult, ALU.add).then_inc(se.h, 1)
            se.v += 1
            c.on("vector", fyd)
            ydv = se.v

            def fsz(s, k=k, ydv=ydv):
                s.wait_ge(se.h, ydv)
                s.activation(scr[:, :CH], z_sb2[:, k * CH:(k + 1) * CH], AF.Silu).then_inc(se.h, 1)
            se.v += 1
            c.on("scalar", fsz)
            szv = se.v

            def fyg(v, k=k, szv=szv):
                v.wait_ge(se.h, szv)
                v.tensor_tensor(u_bf[:, k * CH:(k + 1) * CH], y32[:, k * CH:(k + 1) * CH],
                                scr[:, :CH], ALU.mult).then_inc(se.h, 1)
            se.v += 1
            c.on("vector", fyg)
        ygate_done = se.v
        c.flush()
        def fyz(v):
            v.memset(y32[:, :], 0.0)
        c.on("vector", fyz)

        sldx2, ldv2 = load_x_chunk(c, x_in, ci, tag="l2alx2_%d" % ci)
        emit_ln_chunk(c, "l2aln2", ci, xn32, wait=(sldx2, ldv2), xn_ktile_stride=CH)
        c.flush()

        sfin = c.sem("l2afin")
        for of in range(KT):
            prevmm = se.v

            def fwo(g, of=of, prevmm=prevmm):
                if prevmm:
                    g.wait_ge(se.h, prevmm)
                src2 = wd["l2a_woutT_d"].rearrange("(k p) o -> p k o", p=128)
                g.dma_start(wouts[:, :].rearrange("p (k o) -> p k o", o=128),
                            src2[:, :, of * 128:(of + 1) * 128]).then_inc(sw.h, 16)
            sw.v += 16
            c.on("gpsimd", fwo)
            wv = sw.v
            vgate = sfin.v - 1

            def fmm(t, of=of, wv=wv, vgate=vgate):
                t.wait_ge(sw.h, wv)
                if vgate > 0:
                    t.wait_ge(sfin.h, vgate)
                ps = (psA, psB)[of % 2]
                last = None
                for k in range(DIT):
                    last = t.matmul(ps[:, :CH], wouts[:, k * 128:(k + 1) * 128],
                                    u_bf[:, k * CH:(k + 1) * CH],
                                    start=(k == 0), stop=(k == DIT - 1))
                last.then_inc(se.h, 1)
            se.v += 1
            c.on("tensor", fmm)
            mmv = se.v
            slot = (tmp[:, :CH], scr[:, :CH])[of % 2]

            def fse(s, of=of, mmv=mmv, slot=slot, vprev=sfin.v):
                s.wait_ge(se.h, mmv)
                if vprev > 1:
                    s.wait_ge(sfin.h, vprev - 1)
                ps = (psA, psB)[of % 2]
                s.copy(slot, ps[:, :CH]).then_inc(se.h, 1)
            se.v += 1
            c.on("scalar", fse)
            sev_ = se.v

            def fvv(v, of=of, ci=ci, sev_=sev_, slot=slot, e0=sfin.v + 1):
                v.wait_ge(se.h, sev_)
                sl = xc[:, of * CH:(of + 1) * CH]
                v.tensor_tensor(sl, sl, slot, ALU.add).then_inc(sfin.h, 1)
                v.wait_ge(sfin.h, e0)
                v.tensor_tensor(sl, sl, xn32[:, of * CH:(of + 1) * CH], ALU.add).then_inc(sfin.h, 1)
            sfin.v += 2
            c.on("vector", fvv)
        store_x_chunk(c, x_out, ci, sfin, sfin.v)
        c.flush()


STAGE_DRAM["l2a"] = [("l2a_winT_d", [D, 4096]), ("l2a_wxT_d", [DIT_ := 2048, 192]),
                     ("l2a_wdtT_d", [64, 2048]), ("l2a_woutT_d", [2048, D]),
                     ("l2a_alog_d", [128, 16 * 64]), ("l2a_convw_d", [128, 16 * 4]),
                     ("l2a_convb_d", [128, 16]), ("l2a_bdt_d", [128, 16]),
                     ("l2a_dp_d", [128, 16]), ("l2a_nw_d", [128, 8])]


def _stage_l2a(inp):
    def col128s(v):
        n = v.shape[0]
        return np.ascontiguousarray(v.reshape(n // 128, 128).T)
    alog = inp["L2_attn_A_log"]
    a = np.zeros((128, 16 * 64), np.float32)
    for k in range(16):
        a[:, k * 64:(k + 1) * 64] = alog[k * 128:(k + 1) * 128, :]
    cw = inp["L2_attn_conv_w"][:, 0, :]
    cwl = np.zeros((128, 64), np.float32)
    for k in range(16):
        cwl[:, k * 4:(k + 1) * 4] = cw[k * 128:(k + 1) * 128, :]
    return {"l2a_winT_d": np.ascontiguousarray(inp["L2_attn_W_in"].T),
            "l2a_wxT_d": np.ascontiguousarray(inp["L2_attn_W_x"].T),
            "l2a_wdtT_d": np.ascontiguousarray(inp["L2_attn_W_dt"].T),
            "l2a_woutT_d": np.ascontiguousarray(inp["L2_attn_W_out"].T),
            "l2a_alog_d": a, "l2a_convw_d": cwl,
            "l2a_convb_d": col128s(inp["L2_attn_conv_b"]),
            "l2a_bdt_d": col128s(inp["L2_attn_b_dt"]),
            "l2a_dp_d": col128s(inp["L2_attn_Dp"]),
            "l2a_nw_d": col128s(inp["L2_attn_norm_w"]),
            }


_OLD_STAGE3 = stage_weights


def stage_weights(stage, inp):
    if stage == "l2a":
        return _stage_l2a(inp)
    return _OLD_STAGE3(stage, inp)


_OLD_BUILD3 = build_stage


def build_stage(stage):
    EMITTERS.setdefault("l2a", emit_l2a)
    return _OLD_BUILD3(stage)


def emit_l1fa(c: Ctx, x_in, wd, fl_out, cnt_out):
    xn = c.sb("xnA", [128, KT * N], BF16)
    rwT = c.sb("rwT", [128, KT * 8], BF16)
    rb = c.sb("rb", [128, 1], F32)
    lg_sb = c.sb("lg_sb", [8, N], F32)
    idnf = c.sb("idnf", [128, 128], F32)
    ltm = c.sb("ltm", [128, 8], F32)
    fcols = c.sb("fcols", [128, 32], F32)
    mrow3 = c.sb("mrow3", [128, 6], F32)
    fl_sb = c.sb("fl_sb", [8, 4 * N], F32)
    cnt_sb = c.sb("cnt_sb", [16, 1], F32)
    ps_t2 = c.ps("ps_t2", [128, 32])
    ps_t = c.t["ps_b1"]
    ps_c = c.t["ps_b2"]
    psA, psB = c.t["mm_psA"], c.t["mm_psB"]

    sld = c.sem("l1faw")

    def fw(g):
        g.dma_start(rwT[:, :].rearrange("p (k o) -> p k o", o=8),
                    wd["l1f_rwT_d"].rearrange("(k p) o -> p k o", p=128)).then_inc(sld.h, 16)
        g.dma_start(rb[0:8, :], wd["l1f_rb_d"]).then_inc(sld.h, 16)
    sld.v += 32
    c.on("gpsimd", fw)

    def fidn(v):
        v.tensor_tensor(idnf[:, :], c.t["riot"][:, :],
                        c.t["ciot"][:, :].broadcast_to([128, 128]), ALU.is_equal)
    c.on("vector", fidn)
    emit_ln_all(c, "l1faln", x_in, xn)
    c.flush()
    c.dma_barrier(sld)

    se = c.sem("l1fase")
    for ci in range(NCH):
        def fmm(t, ci=ci, gate=se.v - 1):
            if gate > 0:
                t.wait_ge(se.h, gate)
            ps = (psA, psB)[ci % 2]
            last = None
            for k in range(KT):
                last = t.matmul(ps[0:8, :CH], rwT[:, k * 8:(k + 1) * 8],
                                xn[:, k * N + ci * CH:k * N + ci * CH + CH],
                                start=(k == 0), stop=(k == KT - 1))
            last.then_inc(se.h, 1)
        se.v += 1
        c.on("tensor", fmm)
        mmv = se.v

        def fev(s, ci=ci, mmv=mmv):
            s.wait_ge(se.h, mmv)
            ps = (psA, psB)[ci % 2]
            s.activation(lg_sb[0:8, ci * CH:(ci + 1) * CH], ps[0:8, :CH], AF.Identity,
                         bias=rb[0:8, 0:1], scale=1.0).then_inc(se.h, 1)
        se.v += 1
        c.on("scalar", fev)
    c.flush()

    s2 = c.sem("l1fas2")
    for ti in range(16):
        def ftr(t, ti=ti, gate=s2.v):
            if gate:
                t.wait_ge(s2.h, gate)
            t.transpose(ps_t2[0:128, 0:8], lg_sb[0:8, ti * 128:(ti + 1) * 128],
                        idnf[0:8, 0:8]).then_inc(s2.h, 1)
        s2.v += 1
        c.on("tensor", ftr)
        trv = s2.v

        def fl1(v, ti=ti, trv=trv, e1=s2.v + 1, e2_=s2.v + 2, e3=s2.v + 3, e4=s2.v + 4,
                e5=s2.v + 5, e6=s2.v + 6):
            v.wait_ge(s2.h, trv)
            v.tensor_copy(ltm[:, 0:8], ps_t2[:, 0:8]).then_inc(s2.h, 1)
            v.wait_ge(s2.h, e1)
            v.reduce_max(mrow3[:, 0:1], ltm[:, 0:8], axis=AX.X).then_inc(s2.h, 1)
            v.wait_ge(s2.h, e2_)
            v.tensor_scalar_mul(mrow3[:, 1:2], mrow3[:, 0:1], -1.0)
            v.tensor_scalar(fcols[:, 16:24], ltm[:, 0:8], mrow3[:, 0:1], None,
                            ALU.is_equal).then_inc(s2.h, 1)
            v.wait_ge(s2.h, e3)
            v.tensor_scalar(fcols[:, 0:8], fcols[:, 16:24], -1e30, None, ALU.mult).then_inc(s2.h, 1)
            v.wait_ge(s2.h, e4)
            v.tensor_tensor(fcols[:, 8:16], ltm[:, 0:8], fcols[:, 0:8], ALU.add).then_inc(s2.h, 1)
            v.wait_ge(s2.h, e5)
            v.reduce_max(mrow3[:, 2:3], fcols[:, 8:16], axis=AX.X).then_inc(s2.h, 1)
            v.wait_ge(s2.h, e6)
            v.tensor_scalar(fcols[:, 24:32], fcols[:, 8:16], mrow3[:, 2:3], None,
                            ALU.is_equal).then_inc(s2.h, 1)
        s2.v += 7
        c.on("vector", fl1)
        l1v = s2.v

        def fl2(s, ti=ti, l1v=l1v):
            s.wait_ge(s2.h, l1v)
            s.activation(mrow3[:, 3:4], mrow3[:, 2:3], AF.Exp,
                         bias=mrow3[:, 1:2], scale=1.0).then_inc(s2.h, 1)
        s2.v += 1
        c.on("scalar", fl2)
        l2v = s2.v

        def fl3(v, ti=ti, l2v=l2v, e1=s2.v + 1, e2_=s2.v + 2, e3=s2.v + 3):
            v.wait_ge(s2.h, l2v)
            v.tensor_scalar_add(mrow3[:, 4:5], mrow3[:, 3:4], 1.0).then_inc(s2.h, 1)
            v.wait_ge(s2.h, e1)
            v.reciprocal(mrow3[:, 4:5], mrow3[:, 4:5]).then_inc(s2.h, 1)
            v.wait_ge(s2.h, e2_)
            v.tensor_tensor(mrow3[:, 5:6], mrow3[:, 3:4], mrow3[:, 4:5], ALU.mult).then_inc(s2.h, 1)
            v.wait_ge(s2.h, e3)
            v.tensor_scalar_mul(fcols[:, 0:8], fcols[:, 16:24], mrow3[:, 4:5])
            v.tensor_scalar_mul(fcols[:, 8:16], fcols[:, 24:32], mrow3[:, 5:6]).then_inc(s2.h, 1)
        s2.v += 4
        c.on("vector", fl3)
        l3v = s2.v

        for j in range(4):
            def ftrj(t, ti=ti, j=j, l3v=l3v, gate=s2.v):
                t.wait_ge(s2.h, l3v)
                if gate > l3v:
                    t.wait_ge(s2.h, gate)
                t.transpose(ps_t[0:8, 0:128], fcols[:, j * 8:(j + 1) * 8],
                            idnf[:, :]).then_inc(s2.h, 1)
            s2.v += 1
            c.on("tensor", ftrj)
            tv = s2.v

            def fevj(s, ti=ti, j=j, tv=tv):
                s.wait_ge(s2.h, tv)
                s.copy(fl_sb[0:8, j * N + ti * 128:j * N + (ti + 1) * 128],
                       ps_t[0:8, 0:128]).then_inc(s2.h, 1)
            s2.v += 1
            c.on("scalar", fevj)
        def fcnt(t, ti=ti, gate=s2.v):
            t.wait_ge(s2.h, gate)
            t.matmul(ps_c[0:16, 0:1], fcols[:, 16:32], c.t["ones_f"][:, :],
                     start=(ti == 0), stop=(ti == 15)).then_inc(s2.h, 1)
        s2.v += 1
        c.on("tensor", fcnt)
    cntv = s2.v

    def fce(v, cntv=cntv):
        v.wait_ge(s2.h, cntv)
        v.tensor_copy(cnt_sb[0:16, :], ps_c[0:16, 0:1]).then_inc(s2.h, 1)
    s2.v += 1
    c.on("vector", fce)
    c.flush()

    sout = c.sem("sout")

    def fst(sy):
        sy.dma_start(fl_out, fl_sb[0:8, :]).then_inc(sout.h, 16)
        sy.dma_start(cnt_out, cnt_sb[0:16, :]).then_inc(sout.h, 16)
    sout.v += 32
    c.on("sync", fst)
    c.flush()


def emit_l1fb(c: Ctx, x_in, x_out, wd):
    CAP = 2048.0
    xn = c.sb("xnB", [128, KT * CH], BF16)
    fl = c.sb("fl", [8, 4 * N], BF16)
    cntp = c.sb("cntp", [8, 16], F32)
    offr = c.sb("offr", [1, 16], F32)
    grow = c.sb("grow", [8, 2 * N], F32)
    gtot = c.sb("gtot", [8, N], F32)
    wslab = c.sb("wslab", [128, KT * 2048], BF16)
    vslab = c.sb("vslab", [128, KT * 2048], BF16)
    oslab = c.sb("oslab", [128, 16 * 128], BF16)
    outacc = c.sb("outacc2", [128, KT * CH], F32)
    a_sc = c.t["tmp"]
    b_sc = c.t["rstd_b"]
    g_bc = c.t["negm_b"]
    act2 = c.sb("act2", [128, 16 * CH], BF16)
    idnf2 = c.sb("idnf2t", [128, 128], F32)
    c.t["idnf2"] = idnf2
    ps_o = c.ps("ps_o", [1, CH])
    psA, psB = c.t["mm_psA"], c.t["mm_psB"]

    sld = c.sem("l1fbw")

    def fw(g):
        g.dma_start(fl[0:8, :], wd["l1f_fl_d"]).then_inc(sld.h, 16)
        g.dma_start(cntp[0:8, :], wd["l1f_cntp_d"]).then_inc(sld.h, 16)
    sld.v += 32
    c.on("gpsimd", fw)

    def fidn2(v):
        v.tensor_tensor(idnf2[:, :], c.t["riot"][:, :],
                        c.t["ciot"][:, :].broadcast_to([128, 128]), ALU.is_equal)
    c.on("vector", fidn2)
    c.flush()
    c.dma_barrier(sld)

    se = c.sem("l1fbse")

    def foff(t):
        t.matmul(c.t["ps_s"][0:1, 0:16], c.t["ones_f"][0:8, :], cntp[0:8, :],
                 start=True, stop=True).then_inc(se.h, 1)
    se.v += 1
    c.on("tensor", foff)
    ofv = se.v

    def foe(s, ofv=ofv):
        s.wait_ge(se.h, ofv)
        s.copy(offr[0:1, :], c.t["ps_s"][0:1, 0:16]).then_inc(se.h, 1)
    se.v += 1
    c.on("scalar", foe)
    fv_ = se.v

    def fot(v, fv_=fv_):
        v.wait_ge(se.h, fv_)
        v.tensor_tensor(offr[0:1, 0:8], offr[0:1, 0:8], offr[0:1, 8:16], ALU.add).then_inc(se.h, 1)
    se.v += 1
    c.on("vector", fot)
    c.flush()

    def fgate(v, e0=se.v):
        f0 = fl[0:8, 2 * N:3 * N]
        f1 = fl[0:8, 3 * N:4 * N]
        g0 = fl[0:8, 0:N]
        g1 = fl[0:8, N:2 * N]
        cA = grow[0:8, 0:N]
        cB = grow[0:8, N:2 * N]
        ob = c.t["ones_b"][0:8, 0:1].broadcast_to([8, N])
        v.tensor_tensor_scan(cA, ob, f0, 0.0, ALU.mult, ALU.add).then_inc(se.h, 1)
        v.wait_ge(se.h, e0 + 1)
        v.tensor_tensor_scan(cB, ob, f1, 0.0, ALU.mult, ALU.add).then_inc(se.h, 1)
        v.wait_ge(se.h, e0 + 2)
        v.tensor_tensor(cA, cA, cB, ALU.add).then_inc(se.h, 1)
        v.wait_ge(se.h, e0 + 3)
        v.tensor_tensor(cB, cA, f1, ALU.subtract).then_inc(se.h, 1)
        v.wait_ge(se.h, e0 + 4)
        v.engine_nop() if False else None
    se.v += 4
    c.on("vector", fgate)
    gv = se.v

    def ftro(t, gv=gv):
        t.wait_ge(se.h, gv)
        t.transpose(c.t["ps_b1"][0:16, 0:1] if False else c.t["ps_b1"][0:128, 0:1],
                    offr[0:1, 0:16], c.t["idn"][:, :]) if False else None
        t.transpose(c.t["ps_b1"][0:16, 0:128], offr[0:1, 0:128][0:1, 0:16], c.t["idn"][:, :]) if False else None
        t.matmul(c.t["ps_b1"][0:8, 0:1], offr[0:1, 0:8], c.t["ones_f"][0:1, 0:1],
                 start=True, stop=True).then_inc(se.h, 1)
    se.v += 1
    c.on("tensor", ftro)
    trv = se.v

    def fthr(v, trv=trv):
        v.wait_ge(se.h, trv)
        v.tensor_scalar(b_sc[0:8, 0:1], c.t["ps_b1"][0:8, 0:1], -1.0, CAP + 1.0,
                        ALU.mult, ALU.add).then_inc(se.h, 1)
    se.v += 1
    c.on("vector", fthr)
    c.flush()

    def fgate2(v, e0=se.v):
        cA = grow[0:8, 0:N]
        cB = grow[0:8, N:2 * N]
        g0 = fl[0:8, 0:N]
        g1 = fl[0:8, N:2 * N]
        v.tensor_scalar(gtot[0:8, :], cB, b_sc[0:8, 0:1], None, ALU.is_lt).then_inc(se.h, 1)
        v.wait_ge(se.h, e0 + 1)
        v.tensor_tensor(gtot[0:8, :], gtot[0:8, :], g0, ALU.mult).then_inc(se.h, 1)
        v.wait_ge(se.h, e0 + 2)
        v.tensor_scalar(cB, cA, b_sc[0:8, 0:1], None, ALU.is_lt).then_inc(se.h, 1)
        v.wait_ge(se.h, e0 + 3)
        v.tensor_tensor(cB, cB, g1, ALU.mult).then_inc(se.h, 1)
        v.wait_ge(se.h, e0 + 4)
        v.tensor_tensor(gtot[0:8, :], gtot[0:8, :], cB, ALU.add).then_inc(se.h, 1)
    se.v += 5
    c.on("vector", fgate2)
    c.flush()

    sw = c.sem("l1fbsw")
    sv2 = c.sem("l1fbv")
    prev_gbe = [0]
    for tc in range(NCH):
        sldx0, ldv0 = load_x_chunk(c, x_in, tc, tag="l1fblnx%d" % tc)
        emit_ln_chunk(c, "l1fbln", tc, xn, wait=(sldx0, ldv0), xn_ktile_stride=CH)
        c.flush()
        for e in range(8):
            gatev = se.v

            def fwl(g, e=e, tc=tc, gate=se.v):
                g.wait_ge(se.h, gate)
                g.dma_start(wslab[:, :].rearrange("p (k o) -> p k o", o=2048),
                            wd["l1f_W_d"][e].rearrange("(k p) o -> p k o", p=128)).then_inc(sw.h, 16)
                g.dma_start(vslab[:, :].rearrange("p (k o) -> p k o", o=2048),
                            wd["l1f_V_d"][e].rearrange("(k p) o -> p k o", p=128)).then_inc(sw.h, 16)
            sw.v += 32
            c.on("gpsimd", fwl)
            wv = sw.v
            def fgs(t, e=e, tc=tc, gate=se.v):
                if gate:
                    t.wait_ge(se.h, gate)
                t.matmul(ps_o[0:1, :CH], c.t["idnf2"][0:8, e:e + 1],
                         gtot[0:8, tc * CH:(tc + 1) * CH], start=True, stop=True).then_inc(se.h, 1)
            se.v += 1
            c.on("tensor", fgs)
            gsv = se.v

            def fge(s, gsv=gsv):
                s.wait_ge(se.h, gsv)
                s.copy(c.t["strow"][0:1, 0:CH], ps_o[0:1, :CH]).then_inc(se.h, 1)
            se.v += 1
            c.on("scalar", fge)
            gev = se.v

            def fgb(t, gev=gev, pg=prev_gbe[0]):
                t.wait_ge(se.h, gev)
                if pg:
                    t.wait_ge(se.h, pg)
                t.matmul(c.t["ps_b2"][:, :CH], c.t["onesr_f"][0:1, :], c.t["strow"][0:1, 0:CH],
                         start=True, stop=True).then_inc(se.h, 1)
            se.v += 1
            c.on("tensor", fgb)
            gbv = se.v

            def fgbe(v, gbv=gbv):
                v.wait_ge(se.h, gbv)
                v.tensor_copy(g_bc[:, :CH], c.t["ps_b2"][:, :CH]).then_inc(se.h, 1)
            se.v += 1
            c.on("vector", fgbe)
            gbev = se.v
            prev_gbe[0] = gbev
            for of in range(16):
                def fmm(t, e=e, tc=tc, of=of, wv=wv, gbev=gbev, gate=se.v - 1):
                    t.wait_ge(sw.h, wv)
                    t.wait_ge(se.h, gbev)
                    if gate > gbev:
                        t.wait_ge(se.h, gate)
                    last = None
                    for k in range(KT):
                        last = t.matmul(psA[:, :CH], wslab[:, k * 2048 + of * 128:k * 2048 + (of + 1) * 128],
                                        xn[:, k * CH:(k + 1) * CH],
                                        start=(k == 0), stop=(k == KT - 1))
                    for k in range(KT):
                        last = t.matmul(psB[:, :CH], vslab[:, k * 2048 + of * 128:k * 2048 + (of + 1) * 128],
                                        xn[:, k * CH:(k + 1) * CH],
                                        start=(k == 0), stop=(k == KT - 1))
                    last.then_inc(se.h, 1)
                se.v += 1
                c.on("tensor", fmm)
                mmv = se.v

                def fab(s, of=of, mmv=mmv, gate=se.v + 1):
                    s.wait_ge(se.h, mmv)
                    s.copy(a_sc[:, :CH], psA[:, :CH])
                    s.activation(b_sc[:, :CH], psB[:, :CH], AF.Silu).then_inc(se.h, 1)
                se.v += 1
                c.on("scalar", fab)
                abv = se.v

                def fact(v, of=of, abv=abv, e0=se.v + 1):
                    v.wait_ge(se.h, abv)
                    v.tensor_tensor(a_sc[:, :CH], a_sc[:, :CH], b_sc[:, :CH], ALU.mult).then_inc(se.h, 1)
                    v.wait_ge(se.h, e0)
                    v.tensor_tensor(act2[:, of * CH:(of + 1) * CH], a_sc[:, :CH],
                                    g_bc[:, :CH], ALU.mult).then_inc(se.h, 1)
                se.v += 2
                c.on("vector", fact)
            actv = se.v
            for of2 in range(KT):
                def fwo2(g, e=e, of2=of2, gate=se.v):
                    if gate:
                        g.wait_ge(se.h, gate)
                    src3 = wd["l1f_Wout_d"][e].rearrange("(k p) o -> p k o", p=128)
                    g.dma_start(oslab[:, :].rearrange("p (k o) -> p k o", o=128),
                                src3[:, :, of2 * 128:(of2 + 1) * 128]).then_inc(sw.h, 16)
                sw.v += 16
                c.on("gpsimd", fwo2)
                wv2 = sw.v

                def fmm2(t, e=e, of2=of2, actv=actv, gate=se.v - 1, wv2=wv2):
                    t.wait_ge(sw.h, wv2)
                    t.wait_ge(se.h, actv)
                    if gate > actv:
                        t.wait_ge(se.h, gate)
                    ps = (psA, psB)[of2 % 2]
                    last = None
                    for k in range(16):
                        last = t.matmul(ps[:, :CH], oslab[:, k * 128:(k + 1) * 128],
                                        act2[:, k * CH:(k + 1) * CH],
                                        start=(k == 0), stop=(k == 15))
                    last.then_inc(se.h, 1)
                se.v += 1
                c.on("tensor", fmm2)
                m2v = se.v

                def facc(v, e=e, of2=of2, m2v=m2v):
                    v.wait_ge(se.h, m2v)
                    ps = (psA, psB)[of2 % 2]
                    dst = outacc[:, of2 * CH:(of2 + 1) * CH]
                    if e == 0:
                        v.tensor_copy(dst, ps[:, :CH]).then_inc(se.h, 1)
                    else:
                        v.tensor_tensor(dst, dst, ps[:, :CH], ALU.add).then_inc(se.h, 1)
                se.v += 1
                c.on("vector", facc)
            c.flush()
        sfin = c.sem("l1fbfin")
        sldx, ldv = load_x_chunk(c, x_in, tc, tag="l1fblx%d" % tc)

        def fadd(v, tc=tc, ldv=ldv):
            v.wait_ge(c.sem("l1fblx%d" % tc).h, ldv)
            last = None
            for k in range(KT):
                sl = c.t["xc"][:, k * CH:(k + 1) * CH]
                last = v.tensor_tensor(sl, sl, outacc[:, k * CH:(k + 1) * CH], ALU.add)
            last.then_inc(sfin.h, 1)
        sfin.v += 1
        c.on("vector", fadd)
        store_x_chunk(c, x_out, tc, sfin, sfin.v)
        c.flush()


STAGE_DRAM["l1fa"] = [("l1f_rwT_d", [D, 8]), ("l1f_rb_d", [8, 1])]
STAGE_DRAM["l1fb"] = [("l1f_fl_d", [8, 4 * N]), ("l1f_cntp_d", [8, 16]),
                      ("l1f_W_d", [8, D, 2048]), ("l1f_V_d", [8, D, 2048]),
                      ("l1f_Wout_d", [8, 2048, D])]

_BUILD_CACHE_MOE = {}


def build_l1fa():
    if "a" in _BUILD_CACHE_MOE:
        return _BUILD_CACHE_MOE["a"]
    stack = ExitStack()
    nc = bass.Bass("TRN2", target_bir_lowering=False)
    x_in = nc.dram_tensor("x_in", [KT, 128, N], F32, kind="ExternalInput").ap()
    fl_out = nc.dram_tensor("fl_out", [8, 4 * N], F32, kind="ExternalOutput").ap()
    cnt_out = nc.dram_tensor("cnt_out", [16, 1], F32, kind="ExternalOutput").ap()
    wd = {nm: nc.dram_tensor(nm, list(sh), F32, kind="ExternalInput").ap()
          for nm, sh in STAGE_DRAM["l1fa"]}
    c = Ctx(nc, stack)
    emit_setup(c)
    alloc_shared(c)
    emit_l1fa(c, x_in, wd, fl_out, cnt_out)
    sout = c.sem("sout")

    def fwait(sy):
        sy.wait_ge(sout.h, sout.v)
    c.on("sync", fwait)
    c.flush()
    stack.close()
    _BUILD_CACHE_MOE["a"] = nc
    return nc


def build_l1fb():
    if "b" in _BUILD_CACHE_MOE:
        return _BUILD_CACHE_MOE["b"]
    stack = ExitStack()
    nc = bass.Bass("TRN2", target_bir_lowering=False)
    x_in = nc.dram_tensor("x_in", [KT, 128, N], F32, kind="ExternalInput").ap()
    x_out = nc.dram_tensor("x_out", [KT, 128, N], F32, kind="ExternalOutput").ap()
    wd = {nm: nc.dram_tensor(nm, list(sh), F32, kind="ExternalInput").ap()
          for nm, sh in STAGE_DRAM["l1fb"]}
    c = Ctx(nc, stack)
    emit_setup(c)
    alloc_shared(c)
    emit_l1fb(c, x_in, x_out, wd)
    sout = c.sem("sout")

    def fwait(sy):
        sy.wait_ge(sout.h, sout.v)
    c.on("sync", fwait)
    c.flush()
    stack.close()
    _BUILD_CACHE_MOE["b"] = nc
    return nc


def run_l1f(xs, inp):
    ncA = build_l1fa()
    wA = {"l1f_rwT_d": np.ascontiguousarray(inp["L1_ffn_router_w"].T),
          "l1f_rb_d": np.ascontiguousarray(inp["L1_ffn_router_b"].reshape(8, 1))}
    res = run_bass_kernel_spmd(ncA, [dict(x_in=xs[ci], **wA) for ci in range(8)],
                               core_ids=list(range(8)))
    fls = [np.asarray(r["fl_out"]) for r in res.results]
    cnts = [np.asarray(r["cnt_out"]) for r in res.results]
    ncB = build_l1fb()
    maps = []
    for ci in range(8):
        cntp = np.zeros((8, 16), np.float32)
        for r in range(ci):
            cntp[r, :] = cnts[r][:, 0]
        maps.append(dict(x_in=xs[ci], l1f_fl_d=fls[ci], l1f_cntp_d=cntp,
                         l1f_W_d=inp["L1_ffn_W"], l1f_V_d=inp["L1_ffn_V"],
                         l1f_Wout_d=inp["L1_ffn_W_out"]))
    res = run_bass_kernel_spmd(ncB, maps, core_ids=list(range(8)))
    return [np.asarray(r["x_out"]) for r in res.results]


_OLD_RUN = run_stage


def run_stage(stage, xs, inp):
    if stage == "l1f":
        return run_l1f(xs, inp)
    return _OLD_RUN(stage, xs, inp)


def emit_l0a_solution(c, s3, SL, FL, PS, ps_m, ps_m2, psB, pb, o_fm, kt_, s0,
                      Xh, Xbh, XTh, qf, idnp, masks):

    def TQ(f):
        s3.v += 1
        c.on("tensor", f)
        return s3.v

    def VQ(f, n=1):
        s3.v += n
        c.on("vector", f)
        return s3.v

    triuI = masks[pb:pb + 64, 128:192]

    def FLc(i, j):
        return FL(i)[:, j:j + 1]

    def solve(Lsl, Ltsl, Rsl, OUTsl, OUTTsl):
        m0 = TQ(lambda t, g=s3.v: (
            t.wait_ge(s3.h, g),
            t.matmul(PS(ps_m), SL(Ltsl), SL(Rsl), start=True, stop=True,
                     tile_position=(pb, pb)).then_inc(s3.h, 1)))
        VQ(lambda v, m0=m0, e=s3.v + 1: (
            v.wait_ge(s3.h, m0),
            v.tensor_tensor(FL(3), SL(Rsl), PS(ps_m), ALU.subtract).then_inc(s3.h, 1),
            v.wait_ge(s3.h, e),
            v.tensor_copy(SL(18), FL(3)).then_inc(s3.h, 1)), n=2)
        curP, curPt = Lsl, Ltsl
        for lev in range(5):
            m1 = TQ(lambda t, g=s3.v, curP=curP, curPt=curPt: (
                t.wait_ge(s3.h, g),
                t.matmul(PS(ps_m), SL(curPt), SL(curP), start=True, stop=True,
                         tile_position=(pb, pb)).then_inc(s3.h, 1)))
            e1 = VQ(lambda v, m1=m1: (
                v.wait_ge(s3.h, m1),
                v.tensor_copy(SL(16), PS(ps_m)).then_inc(s3.h, 1)))
            m2 = TQ(lambda t, e1=e1: (
                t.wait_ge(s3.h, e1),
                t.transpose(PS(ps_m2), SL(16), idnp,
                            tile_position=(pb, pb)).then_inc(s3.h, 1)))
            e2 = VQ(lambda v, m2=m2: (
                v.wait_ge(s3.h, m2),
                v.tensor_copy(SL(17), PS(ps_m2)).then_inc(s3.h, 1)))
            m3 = TQ(lambda t, e2=e2: (
                t.wait_ge(s3.h, e2),
                t.matmul(PS(ps_m), SL(17), SL(18), start=True, stop=True,
                         tile_position=(pb, pb)).then_inc(s3.h, 1)))
            VQ(lambda v, m3=m3, e=s3.v + 1: (
                v.wait_ge(s3.h, m3),
                v.tensor_tensor(FL(3), FL(3), PS(ps_m), ALU.add).then_inc(s3.h, 1),
                v.wait_ge(s3.h, e),
                v.tensor_copy(SL(18), FL(3)).then_inc(s3.h, 1)), n=2)
            curP, curPt = 16, 17
        fo = VQ(lambda v, g=s3.v: (
            v.wait_ge(s3.h, g),
            v.tensor_copy(SL(OUTsl), FL(3)).then_inc(s3.h, 1)))
        mt = TQ(lambda t, fo=fo: (
            t.wait_ge(s3.h, fo),
            t.transpose(PS(ps_m2), SL(OUTsl), idnp,
                        tile_position=(pb, pb)).then_inc(s3.h, 1)))
        VQ(lambda v, mt=mt: (
            v.wait_ge(s3.h, mt),
            v.tensor_copy(SL(OUTTsl), PS(ps_m2)).then_inc(s3.h, 1)))

    solve(3, 13, 11, 5, 7)
    solve(4, 14, 12, 6, 8)
    mz = TQ(lambda t, g=s3.v: (
        t.wait_ge(s3.h, g),
        t.matmul(PS(ps_m), qf, XTh, start=True, stop=True,
                 tile_position=(pb, pb)).then_inc(s3.h, 1)))
    ez = VQ(lambda v, mz=mz: (
        v.wait_ge(s3.h, mz),
        v.tensor_copy(SL(2), PS(ps_m)).then_inc(s3.h, 1)))
    mzt = TQ(lambda t, ez=ez: (
        t.wait_ge(s3.h, ez),
        t.transpose(PS(ps_m2), SL(2), idnp,
                    tile_position=(pb, pb)).then_inc(s3.h, 1)))
    ezt = VQ(lambda v, mzt=mzt: (
        v.wait_ge(s3.h, mzt),
        v.tensor_copy(FL(4), PS(ps_m2)),
        v.tensor_copy(SL(9), PS(ps_m2)).then_inc(s3.h, 1)))
    m1m = TQ(lambda t, ezt=ezt: (
        t.wait_ge(s3.h, ezt),
        t.matmul(PS(ps_m), SL(7), SL(9), start=True, stop=True,
                 tile_position=(pb, pb)).then_inc(s3.h, 1)))
    e1m = VQ(lambda v, m1m=m1m: (
        v.wait_ge(s3.h, m1m),
        v.tensor_tensor(SL(10), PS(ps_m), triuI, ALU.mult).then_inc(s3.h, 1)))
    m2m = TQ(lambda t, e1m=e1m: (
        t.wait_ge(s3.h, e1m),
        t.matmul(PS(ps_m), SL(8), qf, start=True, stop=True,
                 tile_position=(pb, pb)).then_inc(s3.h, 1)))
    e2m = VQ(lambda v, m2m=m2m, e=s3.v + 1: (
        v.wait_ge(s3.h, m2m),
        v.tensor_tensor(FL(5), PS(ps_m), FL(1), ALU.mult).then_inc(s3.h, 1),
        v.wait_ge(s3.h, e),
        v.tensor_tensor(SL(15), FL(5), triuI, ALU.mult).then_inc(s3.h, 1)), n=2)
    mk = TQ(lambda t, e2m=e2m: (
        t.wait_ge(s3.h, e2m),
        t.matmul(PS(ps_m), SL(0), SL(10), start=True, stop=True,
                 tile_position=(pb, pb)),
        t.matmul(PS(psB), SL(0), SL(15), start=True, stop=True,
                 tile_position=(pb, pb)).then_inc(s3.h, 1)))
    osl = o_fm[pb:pb + 64, kt_ * CH + s0:kt_ * CH + s0 + 64]
    VQ(lambda v, mk=mk, e=s3.v + 1, e2=s3.v + 2, osl=osl: (
        v.wait_ge(s3.h, mk),
        v.tensor_tensor(FL(5), FL(4), PS(ps_m), ALU.subtract).then_inc(s3.h, 1),
        v.wait_ge(s3.h, e),
        v.tensor_tensor(FL(5), FL(5), FL(2), ALU.mult).then_inc(s3.h, 1),
        v.wait_ge(s3.h, e2),
        v.tensor_tensor(osl, FL(5), PS(psB), ALU.add).then_inc(s3.h, 1)), n=3)
    mr = TQ(lambda t, g=s3.v: (
        t.wait_ge(s3.h, g),
        t.matmul(PS(ps_m), SL(7), Xbh, start=True, stop=True,
                 tile_position=(pb, pb)).then_inc(s3.h, 1)))
    er = VQ(lambda v, mr=mr, e=s3.v + 1: (
        v.wait_ge(s3.h, mr),
        v.tensor_copy(SL(20), PS(ps_m)).then_inc(s3.h, 1),
        v.wait_ge(s3.h, e),
        v.tensor_scalar_mul(SL(19), SL(6), FLc(1, 63)).then_inc(s3.h, 1)), n=2)
    mk2 = TQ(lambda t, er=er: (
        t.wait_ge(s3.h, er),
        t.matmul(PS(ps_m), SL(0), SL(20), start=True, stop=True,
                 tile_position=(pb, pb)),
        t.matmul(PS(psB), SL(0), SL(19), start=True, stop=True,
                 tile_position=(pb, pb)).then_inc(s3.h, 1)))
    VQ(lambda v, mk2=mk2, e=s3.v + 1, e2=s3.v + 2, Xh=Xh: (
        v.wait_ge(s3.h, mk2),
        v.tensor_tensor(FL(5), Xh, PS(ps_m), ALU.subtract).then_inc(s3.h, 1),
        v.wait_ge(s3.h, e),
        v.tensor_scalar_mul(FL(5), FL(5), FLc(2, 63)).then_inc(s3.h, 1),
        v.wait_ge(s3.h, e2),
        v.tensor_tensor(Xh, FL(5), PS(psB), ALU.add).then_inc(s3.h, 1)), n=3)


def emit_l0a(c: Ctx, x_in, x_out, wd):
    CC = 64
    xn = c.sb("xn0", [128, KT * N], BF16)
    wbig = c.sb("wbig", [128, KT * D], BF16)
    pre_d = c.nc.dram_tensor("pre_scr", [3, KT, 128, N], BF16).ap()
    qh = c.sb("qh", [128, KT * CH], BF16)
    kh = c.sb("kh", [128, KT * CH], BF16)
    vh = c.sb("vh", [128, KT * CH], BF16)
    prew = c.sb("prew", [128, KT * (CH + 3)], BF16)
    cq32 = c.sb("cq32", [128, KT * CH], F32)
    lam = c.sb("lam", [16, CH], F32)
    bgr = c.sb("bgr", [16, CH], F32)
    wab = c.sb("wab", [128, KT * 32], BF16)
    abc = c.sb("abc", [16, 4], F32)
    convs = c.sb("convs", [128, KT * 4 * 3], F32)
    masks = c.sb("masks", [128, 192], F32)
    idnf16 = c.sb("idnf16", [16, 16], F32)
    lrowb = c.sb("lrowb", [1, 64], BF16)
    X_sb = c.sb("X_sb", [128, 8 * 64], F32)
    Xbf = c.sb("Xbf", [128, 64], BF16)
    XTbf = c.sb("XTbf", [128, 64], BF16)
    o_fm = c.sb("o_fm", [128, KT * CH], BF16)
    woT = c.sb("woT", [128, KT * 128], BF16)
    bo_c = c.sb("bo_c", [128, KT], F32)
    qkvb = c.sb("qkvb", [128, 3 * KT], F32)
    smat = c.sb("smat", [128, 24 * 64], BF16)
    sfm = c.sb("sfm", [128, 8 * 64], F32)
    lrow = c.sb("lrow", [1, 2 * CH], F32)
    lcol = c.sb("lcol", [128, 2], F32)
    ps_m = c.ps("ps_m", [128, 64])
    ps_m2 = c.ps("ps_m2", [128, 64], BF16)
    psA, psB = c.t["mm_psA"], c.t["mm_psB"]
    tmp = c.t["tmp"]
    sqb = c.t["sq"]
    strow = c.t["strow"]
    xc = c.t["xc"]

    sld = c.sem("l0aw")

    def fw(g):
        g.dma_start(wab[:, :].rearrange("p (k o) -> p k o", o=32),
                    wd["l0a_wabT_d"].rearrange("(k p) o -> p k o", p=128)).then_inc(sld.h, 16)
        g.dma_start(abc[0:16, 0:2], wd["l0a_abb_d"]).then_inc(sld.h, 16)
        g.dma_start(convs[:, :], wd["l0a_conv_d"]).then_inc(sld.h, 16)
        g.dma_start(masks[:, :], wd["l0a_tril_d"]).then_inc(sld.h, 16)
        g.dma_start(bo_c[:, :], wd["l0a_bo_d"]).then_inc(sld.h, 16)
        g.dma_start(qkvb[:, :], wd["l0a_qkvb_d"]).then_inc(sld.h, 16)
    sld.v += 96
    c.on("gpsimd", fw)

    def fz(v):
        v.memset(X_sb[:, :], 0.0)
        v.memset(prew[:, :], 0.0)
    c.on("vector", fz)
    emit_ln_all(c, "l0aln", x_in, xn)
    c.flush()
    c.dma_barrier(sld)

    def fneg(v):
        v.tensor_scalar_mul(abc[0:16, 2:3], abc[0:16, 0:1], -1.0)
        v.tensor_tensor(idnf16[:, :], c.t["riot"][0:16, 0:16],
                        c.t["ciot"][0:16, 0:1].broadcast_to([16, 16]), ALU.is_equal)
    c.on("vector", fneg)
    c.flush()

    se = c.sem("l0ase")
    sw = c.sem("l0asw")
    for iw, wname in enumerate(("l0a_wqT_d", "l0a_wkT_d", "l0a_wvT_d")):
        def fwl(g, wname=wname, gate=se.v):
            if gate:
                g.wait_ge(se.h, gate)
            g.dma_start(wbig[:, :].rearrange("p (k o) -> p k o", o=D),
                        wd[wname].rearrange("(k p) o -> p k o", p=128)).then_inc(sw.h, 16)
        sw.v += 16
        c.on("gpsimd", fwl)
        wv_ = sw.v
        for ci in range(NCH):
            for of in range(KT):
                def fmm(t, iw=iw, ci=ci, of=of, wv_=wv_, gate=se.v - 1):
                    t.wait_ge(sw.h, wv_)
                    if gate > 0:
                        t.wait_ge(se.h, gate)
                    ps = (psA, psB)[of % 2]
                    last = None
                    for k in range(KT):
                        last = t.matmul(ps[:, :CH], wbig[:, k * D + of * 128:k * D + (of + 1) * 128],
                                        xn[:, k * N + ci * CH:k * N + ci * CH + CH],
                                        start=(k == 0), stop=(k == KT - 1))
                    last.then_inc(se.h, 1)
                se.v += 1
                c.on("tensor", fmm)
                mmv = se.v

                def fev(s, iw=iw, ci=ci, of=of, mmv=mmv, dmv=sw.v):
                    s.wait_ge(se.h, mmv)
                    if of == 0:
                        s.wait_ge(sw.h, dmv)
                    ps = (psA, psB)[of % 2]
                    s.activation(cq32[:, of * CH:(of + 1) * CH], ps[:, :CH], AF.Silu,
                                 bias=qkvb[:, iw * KT + of:iw * KT + of + 1],
                                 scale=1.0).then_inc(se.h, 1)
                se.v += 1
                c.on("scalar", fev)
            chv = se.v

            def fst(g, iw=iw, ci=ci, chv=chv):
                g.wait_ge(se.h, chv)
                dst = pre_d[iw].rearrange("k p t -> p k t")[:, :, ci * CH:(ci + 1) * CH]
                g.dma_start(dst, cq32[:, :].rearrange("p (k t) -> p k t", t=CH)).then_inc(sw.h, 16)
            sw.v += 16
            c.on("gpsimd", fst)
        c.flush()

    s3 = c.sem("l0as3")
    for ci in range(NCH):
        def fgm(t, ci=ci, gate=s3.v, gate2=se.v):
            if gate:
                t.wait_ge(s3.h, gate)
            if gate2:
                t.wait_ge(se.h, gate2)
            for k in range(KT):
                t.matmul(psA[0:16, :CH], wab[:, k * 32:k * 32 + 16],
                         xn[:, k * N + ci * CH:k * N + ci * CH + CH],
                         start=(k == 0), stop=(k == KT - 1))
            last = None
            for k in range(KT):
                last = t.matmul(psB[0:16, :CH], wab[:, k * 32 + 16:(k + 1) * 32],
                                xn[:, k * N + ci * CH:k * N + ci * CH + CH],
                                start=(k == 0), stop=(k == KT - 1))
            last.then_inc(s3.h, 1)
        s3.v += 1
        c.on("tensor", fgm)
        gmv = s3.v

        def fge(s, ci=ci, gmv=gmv, e1=s3.v + 1, e2=s3.v + 2):
            s.wait_ge(s3.h, gmv)
            s.activation(lam[0:16, :CH], psA[0:16, :CH], AF.Abs, bias=abc[0:16, 0:1], scale=1.0)
            s.activation(bgr[0:16, :CH], psA[0:16, :CH], AF.Relu, bias=abc[0:16, 2:3],
                         scale=-1.0).then_inc(s3.h, 1)
            s.wait_ge(s3.h, e1)
            s.activation(lam[0:16, :CH], lam[0:16, :CH], AF.Exp, bias=0.0, scale=-1.0).then_inc(s3.h, 1)
            s.wait_ge(s3.h, e2)
            s.activation(lam[0:16, :CH], lam[0:16, :CH], AF.Ln, bias=1.0, scale=1.0).then_inc(s3.h, 1)
        s3.v += 3
        c.on("scalar", fge)
        gev = s3.v

        def fgv(v, ci=ci, gev=gev, e1=s3.v + 1, e2=s3.v + 2, e3=s3.v + 3):
            v.wait_ge(s3.h, gev)
            v.tensor_tensor(lam[0:16, :CH], lam[0:16, :CH], bgr[0:16, :CH], ALU.add).then_inc(s3.h, 1)
            v.wait_ge(s3.h, e1)
            v.tensor_scalar_mul(lam[0:16, :CH], lam[0:16, :CH], -1.0).then_inc(s3.h, 1)
            v.wait_ge(s3.h, e2)
            v.tensor_tensor_scan(lam[0:16, :CH], c.t["ones_f"][0:16, 0:1].broadcast_to([16, CH]),
                                 lam[0:16, :CH], 0.0, ALU.mult, ALU.add).then_inc(s3.h, 1)
        s3.v += 3
        c.on("vector", fgv)
        lamv = s3.v

        def fgb(s, ci=ci, gmv=gmv, lamv=lamv):
            s.wait_ge(s3.h, lamv)
            s.activation(bgr[0:16, :CH], psB[0:16, :CH], AF.Sigmoid,
                         bias=abc[0:16, 1:2], scale=1.0).then_inc(s3.h, 1)
        s3.v += 3 if False else 1
        c.on("scalar", fgb)
        c.flush()

        for iw in range(3):
            sldw = c.sem("l0aldw")

            def fld(g, iw=iw, ci=ci, gate=s3.v, gw=sw.v):
                if gate:
                    g.wait_ge(s3.h, gate)
                g.wait_ge(sw.h, gw)
                lo = ci * CH - 2
                w0 = CH + 3
                srcbase = pre_d[iw].rearrange("k p t -> p k t")
                if ci == 0:
                    g.dma_start(prew[:, :].rearrange("p (k t) -> p k t", t=w0)[:, :, 2:w0],
                                srcbase[:, :, 0:CH + 1]).then_inc(sldw.h, 16)
                elif ci == NCH - 1:
                    g.dma_start(prew[:, :].rearrange("p (k t) -> p k t", t=w0)[:, :, 0:CH + 2],
                                srcbase[:, :, lo:lo + CH + 2]).then_inc(sldw.h, 16)
                else:
                    g.dma_start(prew[:, :].rearrange("p (k t) -> p k t", t=w0),
                                srcbase[:, :, lo:lo + w0]).then_inc(sldw.h, 16)
            sldw.v += 16
            c.on("gpsimd", fld)
            ldv = sldw.v

            def fzz(v, iw=iw, ci=ci, ldv=ldv):
                v.wait_ge(sldw.h, ldv)
                if ci == 0:
                    for k in range(KT):
                        v.memset(prew[:, k * (CH + 3):k * (CH + 3) + 2], 0.0)
                if ci == NCH - 1:
                    for k in range(KT):
                        v.memset(prew[:, k * (CH + 3) + CH + 2:(k + 1) * (CH + 3)], 0.0)
                v.tensor_copy(tmp[:, 0:1], prew[:, 0:1]).then_inc(s3.h, 1)
            s3.v += 1
            c.on("vector", fzz)
            zzv = s3.v

            for k in range(KT):
                base = k * (CH + 3)

                def fcv(v, iw=iw, k=k, base=base, zzv=zzv, e0=s3.v):
                    v.wait_ge(s3.h, zzv)
                    if e0:
                        v.wait_ge(s3.h, e0)
                    v.tensor_scalar_mul(sqb[:, :CH], prew[:, base:base + CH],
                                        convs[:, iw * 32 + k * 4:iw * 32 + k * 4 + 1]).then_inc(s3.h, 1)
                    for j in range(1, 4):
                        v.wait_ge(s3.h, e0 + j)
                        v.scalar_tensor_tensor(sqb[:, :CH], prew[:, base + j:base + j + CH],
                                               convs[:, iw * 32 + k * 4 + j:iw * 32 + k * 4 + j + 1],
                                               sqb[:, :CH], ALU.mult, ALU.add).then_inc(s3.h, 1)
                s3.v += 4
                c.on("vector", fcv)
                cvv = s3.v

                def fsl(s, iw=iw, k=k, cvv=cvv):
                    s.wait_ge(s3.h, cvv)
                    s.activation(cq32[:, k * CH:(k + 1) * CH], sqb[:, :CH],
                                 AF.Silu).then_inc(s3.h, 1)
                s3.v += 1
                c.on("scalar", fsl)
            convv = s3.v
            if iw == 2:
                def fvc(v, convv=convv):
                    v.wait_ge(s3.h, convv)
                    v.tensor_copy(vh[:, :], cq32[:, :]).then_inc(s3.h, 1)
                s3.v += 1
                c.on("vector", fvc)
            else:
                dst = (qh, kh)[iw]
                def fsqA(s, convv=convv):
                    s.wait_ge(s3.h, convv)
                    last = None
                    for k in range(4):
                        last = s.activation(sqb[:, k * CH:(k + 1) * CH],
                                            cq32[:, k * CH:(k + 1) * CH], AF.Square)
                    last.then_inc(s3.h, 1)
                s3.v += 1
                c.on("scalar", fsqA)
                sqav = s3.v

                def fmmA(t, sqav=sqav):
                    t.wait_ge(s3.h, sqav)
                    last = None
                    for k in range(4):
                        last = t.matmul(c.t["ps_s"][0:1, :CH], c.t["ones_f"][:, :],
                                        sqb[:, k * CH:(k + 1) * CH],
                                        start=(k == 0), stop=False)
                    last.then_inc(s3.h, 1)
                s3.v += 1
                c.on("tensor", fmmA)
                mav = s3.v

                def fsqB(s, mav=mav):
                    s.wait_ge(s3.h, mav)
                    last = None
                    for k in range(4, KT):
                        last = s.activation(sqb[:, (k - 4) * CH:(k - 3) * CH],
                                            cq32[:, k * CH:(k + 1) * CH], AF.Square)
                    last.then_inc(s3.h, 1)
                s3.v += 1
                c.on("scalar", fsqB)
                sqbv = s3.v

                def fmmB(t, sqbv=sqbv):
                    t.wait_ge(s3.h, sqbv)
                    last = None
                    for k in range(4, KT):
                        last = t.matmul(c.t["ps_s"][0:1, :CH], c.t["ones_f"][:, :],
                                        sqb[:, (k - 4) * CH:(k - 3) * CH],
                                        start=False, stop=(k == KT - 1))
                    last.then_inc(s3.h, 1)
                s3.v += 1
                c.on("tensor", fmmB)
                mbv = s3.v

                def fns(s, mbv=mbv):
                    s.wait_ge(s3.h, mbv)
                    s.activation(strow[0:1, 0:CH], c.t["ps_s"][0:1, :CH], AF.Sqrt).then_inc(s3.h, 1)
                s3.v += 1
                c.on("scalar", fns)
                nsv = s3.v

                def fnr(v, nsv=nsv, e1=s3.v + 1, e2=s3.v + 2):
                    v.wait_ge(s3.h, nsv)
                    v.tensor_scalar_add(strow[0:1, 0:CH], strow[0:1, 0:CH], 1e-6).then_inc(s3.h, 1)
                    v.wait_ge(s3.h, e1)
                    v.reciprocal(strow[0:1, CH:2 * CH], strow[0:1, 0:CH]).then_inc(s3.h, 1)
                s3.v += 2
                c.on("vector", fnr)
                nrv = s3.v

                def fnb(t, nrv=nrv):
                    t.wait_ge(s3.h, nrv)
                    t.matmul(c.t["ps_b1"][:, :CH], c.t["onesr_f"][0:1, :],
                             strow[0:1, CH:2 * CH], start=True, stop=True).then_inc(s3.h, 1)
                s3.v += 1
                c.on("tensor", fnb)
                nbv = s3.v

                def fnn(v, dst=dst, nbv=nbv, e1=s3.v + 1):
                    v.wait_ge(s3.h, nbv)
                    v.tensor_copy(c.t["rstd_b"][:, :CH], c.t["ps_b1"][:, :CH]).then_inc(s3.h, 1)
                    v.wait_ge(s3.h, e1)
                    for k in range(KT):
                        v.tensor_tensor(dst[:, k * CH:(k + 1) * CH], cq32[:, k * CH:(k + 1) * CH],
                                        c.t["rstd_b"][:, :CH], ALU.mult)
                    v.tensor_copy(tmp[:, 0:1], dst[:, 0:1]).then_inc(s3.h, 1)
                s3.v += 2
                c.on("vector", fnn)
            c.flush()

        for cc in range(8):
            s0 = cc * 64
            for h in range(16):
                pj, hh = h // 2, h % 2
                pb = hh * 64
                kt_ = h // 2

                def PS(p_, lo=0, hi=64, pb=pb):
                    return p_[pb + lo:pb + hi, 0:64]

                def SL(i, pb=pb):
                    return smat[pb:pb + 64, i * 64:(i + 1) * 64]

                def FL(i, pb=pb):
                    return sfm[pb:pb + 64, i * 64:(i + 1) * 64]
                lcolb = lcol[pb:pb + 64, :]
                qf = qh[pb:pb + 64, kt_ * CH + s0:kt_ * CH + s0 + 64]
                kf = kh[pb:pb + 64, kt_ * CH + s0:kt_ * CH + s0 + 64]
                vf = vh[pb:pb + 64, kt_ * CH + s0:kt_ * CH + s0 + 64]
                idnp = c.t["idn"][pb:pb + 64, pb:pb + 64]
                Xh = X_sb[pb:pb + 64, pj * 64:(pj + 1) * 64]
                Xbh = Xbf[pb:pb + 64, :]
                XTh = XTbf[pb:pb + 64, :]

                def TQ(f):
                    s3.v += 1
                    c.on("tensor", f)
                    return s3.v

                def VQ(f, n=1):
                    s3.v += n
                    c.on("vector", f)
                    return s3.v

                def SQ(f, n=1):
                    s3.v += n
                    c.on("scalar", f)
                    return s3.v

                v0 = TQ(lambda t, h=h, s0=s0, gate=s3.v: (
                    t.wait_ge(s3.h, gate) if gate else None,
                    t.matmul(ps_m[0:1, 0:64], idnf16[0:16, h:h + 1], lam[0:16, s0:s0 + 64],
                             start=True, stop=True),
                    t.matmul(psB[0:1, 0:64], idnf16[0:16, h:h + 1], bgr[0:16, s0:s0 + 64],
                             start=True, stop=True).then_inc(s3.h, 1)))
                v1 = SQ(lambda s, v0=v0: (
                    s.wait_ge(s3.h, v0),
                    s.copy(lrow[0:1, 0:64], ps_m[0:1, 0:64]),
                    s.copy(lrow[0:1, CH:CH + 64], psB[0:1, 0:64]),
                    s.copy(lrowb[0:1, 0:64], psB[0:1, 0:64]).then_inc(s3.h, 1)))
                if cc > 0:
                    vb = TQ(lambda t, h=h, s0=s0, v1=v1: (
                        t.wait_ge(s3.h, v1),
                        t.matmul(ps_m[0:1, 0:1], idnf16[0:16, h:h + 1],
                                 lam[0:16, s0 - 1:s0], start=True, stop=True).then_inc(s3.h, 1)))
                    v1 = VQ(lambda v, vb=vb, e0=s3.v + 1: (
                        v.wait_ge(s3.h, vb),
                        v.tensor_copy(lrow[0:1, 64:65], ps_m[0:1, 0:1]).then_inc(s3.h, 1),
                        v.wait_ge(s3.h, e0),
                        v.tensor_scalar(lrow[0:1, 0:64], lrow[0:1, 0:64], lrow[0:1, 64:65],
                                        None, ALU.subtract).then_inc(s3.h, 1)), n=2)
                v2 = TQ(lambda t, v1=v1, pb=pb: (
                    t.wait_ge(s3.h, v1),
                    t.matmul(ps_m[pb:pb + 64, 0:1], lrow[0:1, 0:64], c.t["ones_f"][0:1, 0:1],
                             start=True, stop=True, tile_position=(0, pb)),
                    t.matmul(psB[pb:pb + 64, 0:64], c.t["onesr_f"][0:1, 0:64], lrow[0:1, 0:64],
                             start=True, stop=True, tile_position=(0, pb)).then_inc(s3.h, 1)))
                v3 = VQ(lambda v, v2=v2, e0=s3.v + 1, e1=s3.v + 2, e2=s3.v + 3,
                        lcolb=lcolb, PS=PS, FL=FL: (
                    v.wait_ge(s3.h, v2),
                    v.tensor_copy(lcolb[:, 0:1], PS(ps_m, 0, 64)[:, 0:1]),
                    v.tensor_copy(FL(4), PS(psB)).then_inc(s3.h, 1),
                    v.wait_ge(s3.h, e0),
                    v.tensor_scalar(FL(3), FL(4), lcolb[:, 0:1], None, ALU.subtract).then_inc(s3.h, 1),
                    v.wait_ge(s3.h, e1),
                    v.tensor_scalar_mul(FL(3), FL(3), -1.0).then_inc(s3.h, 1),
                    v.wait_ge(s3.h, e2),
                    v.tensor_scalar(FL(7), FL(3), 0.0, None, ALU.max),
                    v.tensor_scalar(FL(3), FL(3), 0.0, None, ALU.min).then_inc(s3.h, 1)), n=4)
                v4 = SQ(lambda s, v3=v3, e0=s3.v + 1, FL=FL: (
                    s.wait_ge(s3.h, v3),
                    s.activation(FL(0), FL(3), AF.Exp).then_inc(s3.h, 1),
                    s.wait_ge(s3.h, e0),
                    s.activation(FL(1), FL(7), AF.Exp, bias=0.0, scale=-1.0),
                    s.activation(FL(2), FL(4), AF.Exp).then_inc(s3.h, 1)), n=2)
                v5 = TQ(lambda t, v4=v4, pb=pb: (
                    t.wait_ge(s3.h, v4),
                    t.matmul(ps_m[pb:pb + 64, 0:1], lrow[0:1, CH:CH + 64], c.t["ones_f"][0:1, 0:1],
                             start=True, stop=True, tile_position=(0, pb)).then_inc(s3.h, 1)))
                v6 = VQ(lambda v, v5=v5, lcolb=lcolb, PS=PS: (
                    v.wait_ge(s3.h, v5),
                    v.tensor_copy(lcolb[:, 1:2], PS(ps_m)[:, 0:1]).then_inc(s3.h, 1)))
                v7 = TQ(lambda t, kf=kf, v6=v6, pb=pb, idnp=idnp: (
                    t.wait_ge(s3.h, v6),
                    t.matmul(ps_m[pb:pb + 64, 0:64], kf, kf, start=True, stop=True,
                             tile_position=(pb, pb)),
                    t.transpose(ps_m2[pb:pb + 64, 0:64], kf, idnp,
                                tile_position=(pb, pb)).then_inc(s3.h, 1)))
                v8 = VQ(lambda v, v7=v7, PS=PS, FL=FL, SL=SL: (
                    v.wait_ge(s3.h, v7),
                    v.tensor_copy(FL(5), PS(ps_m)),
                    v.tensor_copy(SL(0), PS(ps_m2)).then_inc(s3.h, 1)))
                v9 = TQ(lambda t, vf=vf, v8=v8, pb=pb, idnp=idnp: (
                    t.wait_ge(s3.h, v8),
                    t.transpose(ps_m2[pb:pb + 64, 0:64], vf, idnp,
                                tile_position=(pb, pb)).then_inc(s3.h, 1)))
                v9 = VQ(lambda v, v9=v9, PS=PS, SL=SL: (
                    v.wait_ge(s3.h, v9),
                    v.tensor_copy(SL(1), PS(ps_m2)).then_inc(s3.h, 1)))
                vbb = TQ(lambda t, v9=v9, pb=pb: (
                    t.wait_ge(s3.h, v9),
                    t.matmul(ps_m[pb:pb + 64, 0:64], c.t["onesr_b"][0:1, 0:64],
                             lrowb[0:1, 0:64], start=True, stop=True,
                             tile_position=(0, pb)).then_inc(s3.h, 1)))
                v10 = VQ(lambda v, vbb=vbb, e0=s3.v + 1, e1=s3.v + 2, e2=s3.v + 3,
                         e3=s3.v + 4, Xh=Xh, Xbh=Xbh, lcolb=lcolb, pb=pb, PS=PS, FL=FL, SL=SL: (
                    v.wait_ge(s3.h, vbb),
                    v.tensor_copy(FL(6), PS(ps_m)).then_inc(s3.h, 1),
                    v.wait_ge(s3.h, e0),
                    v.tensor_scalar_mul(FL(7), FL(5), lcolb[:, 1:2]).then_inc(s3.h, 1),
                    v.wait_ge(s3.h, e1),
                    v.tensor_tensor(SL(3), FL(7), masks[pb:pb + 64, 0:64], ALU.mult),
                    v.tensor_tensor(FL(7), FL(5), FL(6), ALU.mult).then_inc(s3.h, 1),
                    v.wait_ge(s3.h, e2),
                    v.tensor_tensor(SL(13), FL(7), masks[pb:pb + 64, 64:128], ALU.mult).then_inc(s3.h, 1),
                    v.wait_ge(s3.h, e3),
                    v.tensor_tensor(SL(4), SL(3), FL(0), ALU.mult),
                    v.tensor_tensor(SL(14), SL(13), FL(1), ALU.mult),
                    v.tensor_scalar_mul(SL(11), SL(0), lcolb[:, 1:2]),
                    v.tensor_scalar_mul(SL(12), SL(1), lcolb[:, 1:2]),
                    v.tensor_copy(Xbh, Xh).then_inc(s3.h, 1)), n=5)
                v11 = TQ(lambda t, v10=v10, pb=pb, idnp=idnp, Xbh=Xbh: (
                    t.wait_ge(s3.h, v10),
                    t.transpose(ps_m2[pb:pb + 64, 0:64], Xbh, idnp,
                                tile_position=(pb, pb)).then_inc(s3.h, 1)))
                v12 = VQ(lambda v, v11=v11, XTh=XTh, PS=PS: (
                    v.wait_ge(s3.h, v11),
                    v.tensor_copy(XTh, PS(ps_m2)).then_inc(s3.h, 1)))
                emit_l0a_solution(c, s3, SL, FL, PS, ps_m, ps_m2, psB, pb, o_fm,
                                  kt_, s0, Xh, Xbh, XTh, qf, idnp, masks)
            c.flush()
        sfin = c.sem("l0afin")
        sldx, ldv = load_x_chunk(c, x_in, ci, tag="l0alx%d" % ci)
        for of in range(KT):
            def fwo(g, of=of, gate=se.v):
                if gate:
                    g.wait_ge(se.h, gate)
                src3 = wd["l0a_woT_d"].rearrange("(k p) o -> p k o", p=128)
                g.dma_start(woT[:, :].rearrange("p (k o) -> p k o", o=128),
                            src3[:, :, of * 128:(of + 1) * 128]).then_inc(sw.h, 16)
            sw.v += 16
            c.on("gpsimd", fwo)
            wv2 = sw.v

            def fmm(t, of=of, wv2=wv2, gate=se.v - 1, dv=s3.v):
                t.wait_ge(sw.h, wv2)
                t.wait_ge(s3.h, dv)
                if gate > 0:
                    t.wait_ge(se.h, gate)
                ps = (psA, psB)[of % 2]
                last = None
                for k in range(KT):
                    last = t.matmul(ps[:, :CH], woT[:, k * 128:(k + 1) * 128],
                                    o_fm[:, k * CH:(k + 1) * CH],
                                    start=(k == 0), stop=(k == KT - 1))
                last.then_inc(se.h, 1)
            se.v += 1
            c.on("tensor", fmm)
            mmv = se.v
            slot = (tmp[:, :CH], sqb[:, :CH])[of % 2]

            def fse2(s, of=of, mmv=mmv, slot=slot, vprev=sfin.v):
                s.wait_ge(se.h, mmv)
                if vprev > 1:
                    s.wait_ge(sfin.h, vprev - 1)
                ps = (psA, psB)[of % 2]
                s.activation(slot, ps[:, :CH], AF.Identity, bias=bo_c[:, of:of + 1],
                             scale=1.0).then_inc(se.h, 1)
            se.v += 1
            c.on("scalar", fse2)
            sev = se.v

            def fvv(v, of=of, ci=ci, sev=sev, slot=slot, ldv=ldv):
                v.wait_ge(se.h, sev)
                v.wait_ge(c.sem("l0alx%d" % ci).h, ldv)
                sl = xc[:, of * CH:(of + 1) * CH]
                v.tensor_tensor(sl, sl, slot, ALU.add).then_inc(sfin.h, 1)
            sfin.v += 1
            c.on("vector", fvv)
        store_x_chunk(c, x_out, ci, sfin, sfin.v)
        c.flush()


STAGE_DRAM["l0a"] = [("l0a_wqT_d", [D, D]), ("l0a_wkT_d", [D, D]), ("l0a_wvT_d", [D, D]),
                     ("l0a_woT_d", [D, D]), ("l0a_wabT_d", [D, 32]),
                     ("l0a_abb_d", [16, 2]), ("l0a_conv_d", [128, 96]),
                     ("l0a_tril_d", [128, 192]), ("l0a_bo_d", [128, 8]),
                     ("l0a_qkvb_d", [128, 24])]


def _stage_l0a(inp):
    def colk(v, kt):
        return np.ascontiguousarray(v.reshape(kt, 128).T)
    cw = np.zeros((128, 96), np.float32)
    for iw, nm in enumerate(("conv_q", "conv_k", "conv_v")):
        w4 = inp["L0_attn_" + nm][:, 0, :]
        for k in range(8):
            cw[:, iw * 32 + k * 4:iw * 32 + (k + 1) * 4] = w4[k * 128:(k + 1) * 128, :]
    m = np.zeros((128, 192), np.float32)
    t1 = np.tril(np.ones((64, 64), np.float32), -1)
    t2 = np.triu(np.ones((64, 64), np.float32), 1)
    t3 = np.triu(np.ones((64, 64), np.float32), 0)
    for half in range(2):
        m[half * 64:(half + 1) * 64, 0:64] = t1
        m[half * 64:(half + 1) * 64, 64:128] = t2
        m[half * 64:(half + 1) * 64, 128:192] = t3
    qkvb = np.zeros((128, 24), np.float32)
    for iw, nm in enumerate(("bq", "bk", "bv")):
        qkvb[:, iw * 8:(iw + 1) * 8] = colk(inp["L0_attn_" + nm], 8)
    wab = np.concatenate([inp["L0_attn_Wa"].T, inp["L0_attn_Wb"].T], axis=1)
    abb = np.stack([inp["L0_attn_ba"], inp["L0_attn_bb"]], axis=1)
    return {"l0a_wqT_d": np.ascontiguousarray(inp["L0_attn_Wq"].T),
            "l0a_wkT_d": np.ascontiguousarray(inp["L0_attn_Wk"].T),
            "l0a_wvT_d": np.ascontiguousarray(inp["L0_attn_Wv"].T),
            "l0a_woT_d": np.ascontiguousarray(inp["L0_attn_Wo"].T),
            "l0a_wabT_d": np.ascontiguousarray(wab),
            "l0a_abb_d": np.ascontiguousarray(abb.astype(np.float32)),
            "l0a_conv_d": cw, "l0a_tril_d": m,
            "l0a_bo_d": colk(inp["L0_attn_bo"], 8),
            "l0a_qkvb_d": qkvb}


_OLD_STAGE4 = stage_weights


def stage_weights(stage, inp):
    if stage == "l0a":
        return _stage_l0a(inp)
    return _OLD_STAGE4(stage, inp)


_OLD_BUILD4 = build_stage


def build_stage(stage):
    EMITTERS.setdefault("l0a", emit_l0a)
    return _OLD_BUILD4(stage)


# version 04
# speedup vs baseline: 1.0001x; 1.0001x over previous
import numpy as np
from contextlib import ExitStack
import concourse.bass as bass
import concourse.mybir as mybir
from concourse.bass_utils import run_bass_kernel_spmd

F32 = mybir.dt.float32
BF16 = mybir.dt.bfloat16
AF = mybir.ActivationFunctionType
ALU = mybir.AluOpType
AX = mybir.AxisListType

D = 1024
N = 2048
KT = D // 128
CH = 512
NCH = N // CH
ENGINES = ("tensor", "vector", "scalar", "gpsimd", "sync")


class Sem:
    __slots__ = ("h", "v")
    def __init__(self, h):
        self.h = h
        self.v = 0


class Ctx:
    def __init__(self, nc, stack):
        self.nc = nc
        self.stack = stack
        self.q = {e: [] for e in ENGINES}
        self._sems = {}
        self._blk = 0
        self.t = {}

    def sem(self, name) -> Sem:
        if name not in self._sems:
            st = getattr(self, "semstack", None) or self.stack
            self._sems[name] = Sem(st.enter_context(self.nc.semaphore(name)))
        return self._sems[name]

    def on(self, eng, f):
        self.q[eng].append(f)

    def flush(self):
        qs = {e: self.q[e] for e in ENGINES if self.q[e]}
        self.q = {e: [] for e in ENGINES}
        if not qs:
            return
        self._blk += 1
        with self.nc.Block("b%d" % self._blk) as blk:
            for ename, fs in qs.items():
                getattr(blk, ename)(lambda engine, fs=fs: [f(engine) for f in fs])

    def dma_barrier(self, *sems):
        def f(g, sems=sems):
            for s in sems:
                g.wait_ge(s.h, s.v)
        self.on("gpsimd", f)
        self.flush()

    def sb(self, name, shape, dtype):
        t = self.stack.enter_context(self.nc.sbuf_tensor(name, list(shape), dtype))
        self.t[name] = t
        return t

    def ps(self, name, shape, dtype=F32):
        t = self.stack.enter_context(self.nc.psum_tensor(name, list(shape), dtype))
        self.t[name] = t
        return t


CONST_VALS = [1e-5, 1e-6, float(np.finfo(np.float32).eps), 0.0, 1.0, -1.0, 0.5,
              -30.0, 8.0, 0.125, 2048.0, -1e30]


def emit_setup(c: Ctx):
    ones_f = c.sb("ones_f", [128, 1], F32)
    onesr_f = c.sb("onesr_f", [1, 128], F32)
    onesr_b = c.sb("onesr_b", [1, 128], BF16)
    ones_b = c.sb("ones_b", [128, 1], BF16)
    idn = c.sb("idn", [128, 128], BF16)
    riot = c.sb("riot", [128, 128], F32)
    ciot = c.sb("ciot", [128, 1], F32)
    cpool = c.sb("cpool", [128, max(16, len(CONST_VALS))], F32)

    def g(g):
        g.memset(ones_f[:, :], 1.0)
        g.memset(onesr_f[:, :], 1.0)
        g.memset(onesr_b[:, :], 1.0)
        g.memset(ones_b[:, :], 1.0)
        for i, v in enumerate(CONST_VALS):
            g.memset(cpool[:, i:i + 1], v)
        g.iota(riot[:, :], [[1, 128]], channel_multiplier=0, allow_small_or_imprecise_dtypes=True)
        g.iota(ciot[:, :], [[0, 1]], channel_multiplier=1, allow_small_or_imprecise_dtypes=True)
    c.on("gpsimd", g)
    c.flush()
    for i, v in enumerate(CONST_VALS):
        c.nc.const_aps.register(F32, v, cpool[:, i:i + 1])

    def v(v):
        v.tensor_tensor(idn[:, :], riot[:, :], ciot[:, :].broadcast_to([128, 128]), ALU.is_equal)
    c.on("vector", v)
    c.flush()


def alloc_shared(c: Ctx, sq_cols=KT * CH):
    c.sb("xc", [128, KT * CH], F32)
    c.sb("sq", [128, sq_cols], F32)
    c.sb("strow", [1, 4 * CH], F32)
    c.sb("rstd_b", [128, CH], F32)
    c.sb("negm_b", [128, CH], F32)
    c.sb("tmp", [128, CH], F32)
    c.ps("ps_s", [1, CH])
    c.ps("ps_q", [1, CH])
    c.ps("ps_b1", [128, CH])
    c.ps("ps_b2", [128, CH])
    c.ps("mm_psA", [128, CH])
    c.ps("mm_psB", [128, CH])


def load_x_chunk(c: Ctx, x_dram, ci, n_tok=N, w=CH, dst=None, tag="lx"):
    sld = c.sem(tag)
    xc = c.t["xc"] if dst is None else dst
    off = ci * CH

    def f(sy, off=off, w=w):
        src = x_dram.rearrange("k p t -> p k t")[:, :, off:off + w]
        sy.dma_start(xc[:, :].rearrange("p (k t) -> p k t", k=KT)[:, :, :w], src).then_inc(sld.h, 16)
    sld.v += 16
    c.on("sync", f)
    return sld, sld.v


def emit_ln_chunk(c: Ctx, tag, ci, xn, n_tok=N, wait=None, xn_ktile_stride=None):
    T = c.t
    xc, sq, strow = T["xc"], T["sq"], T["strow"]
    ps_s, ps_q, ps_b1, ps_b2 = T["ps_s"], T["ps_q"], T["ps_b1"], T["ps_b2"]
    rstd_b, negm_b, tmp = T["rstd_b"], T["negm_b"], T["tmp"]
    ones_f, onesr_f = T["ones_f"], T["onesr_f"]
    ssq, smm, srow, ssct, srow2, sbc, sev = (c.sem(tag + s) for s in
                                             ("sq", "mm", "rw", "sc", "r2", "bc", "ev"))
    off = ci * CH
    w = CH
    stride = n_tok if xn_ktile_stride is None else xn_ktile_stride
    xoff = off if xn_ktile_stride is None else 0

    nslot = sq.shape[1] // CH
    if nslot >= KT:
        def fsq(s, wait=wait):
            if wait is not None:
                s.wait_ge(wait[0].h, wait[1])
            last = None
            for k in range(KT):
                last = s.activation(sq[:, k * CH:k * CH + w], xc[:, k * CH:k * CH + w], AF.Square)
            last.then_inc(ssq.h, 1)
        ssq.v += 1
        c.on("scalar", fsq)

        sqv, row_prev = ssq.v, srow.v

        def fmm(t, sqv=sqv, row_prev=row_prev, wait=wait):
            if wait is not None:
                t.wait_ge(wait[0].h, wait[1])
            t.wait_ge(ssq.h, sqv)
            if row_prev:
                t.wait_ge(srow.h, row_prev)
            for k in range(KT):
                t.matmul(ps_s[0:1, :w], ones_f[:, :], xc[:, k * CH:k * CH + w],
                         start=(k == 0), stop=(k == KT - 1))
            last = None
            for k in range(KT):
                last = t.matmul(ps_q[0:1, :w], ones_f[:, :], sq[:, k * CH:k * CH + w],
                                start=(k == 0), stop=(k == KT - 1))
            last.then_inc(smm.h, 1)
        smm.v += 1
        c.on("tensor", fmm)
    else:
        def fsq1(s, wait=wait, mmprev=smm.v):
            if wait is not None:
                s.wait_ge(wait[0].h, wait[1])
            if mmprev:
                s.wait_ge(smm.h, mmprev)
            last = None
            for k in range(4):
                last = s.activation(sq[:, k * CH:k * CH + w], xc[:, k * CH:k * CH + w], AF.Square)
            last.then_inc(ssq.h, 1)
        ssq.v += 1
        c.on("scalar", fsq1)
        sqv1, row_prev = ssq.v, srow.v

        def fmm1(t, sqv1=sqv1, row_prev=row_prev, wait=wait):
            if wait is not None:
                t.wait_ge(wait[0].h, wait[1])
            t.wait_ge(ssq.h, sqv1)
            if row_prev:
                t.wait_ge(srow.h, row_prev)
            for k in range(KT):
                t.matmul(ps_s[0:1, :w], ones_f[:, :], xc[:, k * CH:k * CH + w],
                         start=(k == 0), stop=(k == KT - 1))
            last = None
            for k in range(4):
                last = t.matmul(ps_q[0:1, :w], ones_f[:, :], sq[:, k * CH:k * CH + w],
                                start=(k == 0), stop=False)
            last.then_inc(smm.h, 1)
        smm.v += 1
        c.on("tensor", fmm1)
        mv1 = smm.v

        def fsq2(s, mv1=mv1):
            s.wait_ge(smm.h, mv1)
            last = None
            for k in range(4, KT):
                last = s.activation(sq[:, (k - 4) * CH:(k - 4) * CH + w],
                                    xc[:, k * CH:k * CH + w], AF.Square)
            last.then_inc(ssq.h, 1)
        ssq.v += 1
        c.on("scalar", fsq2)
        sqv2 = ssq.v

        def fmm2(t, sqv2=sqv2):
            t.wait_ge(ssq.h, sqv2)
            last = None
            for k in range(4, KT):
                last = t.matmul(ps_q[0:1, :w], ones_f[:, :], sq[:, (k - 4) * CH:(k - 4) * CH + w],
                                start=False, stop=(k == KT - 1))
            last.then_inc(smm.h, 1)
        smm.v += 1
        c.on("tensor", fmm2)

    mmv = smm.v

    def frow(v, mmv=mmv):
        v.wait_ge(smm.h, mmv)
        v.tensor_scalar_mul(strow[0:1, 0 * CH:0 * CH + w], ps_s[0:1, :w], 1.0 / D)
        v.tensor_tensor(strow[0:1, 1 * CH:1 * CH + w],
                        strow[0:1, 0 * CH:0 * CH + w],
                        strow[0:1, 0 * CH:0 * CH + w], ALU.mult)
        v.scalar_tensor_tensor(strow[0:1, 1 * CH:1 * CH + w],
                               ps_q[0:1, :w], 1.0 / D,
                               strow[0:1, 1 * CH:1 * CH + w],
                               ALU.mult, ALU.subtract).then_inc(srow.h, 1)
    srow.v += 1
    c.on("vector", frow)

    rowv = srow.v

    def fsqr(s, rowv=rowv):
        s.wait_ge(srow.h, rowv)
        s.activation(strow[0:1, 1 * CH:1 * CH + w],
                     strow[0:1, 1 * CH:1 * CH + w], AF.Sqrt,
                     bias=1e-5, scale=1.0).then_inc(ssct.h, 1)
    ssct.v += 1
    c.on("scalar", fsqr)

    sctv = ssct.v

    def frow2(v, sctv=sctv, r2a=srow2.v + 1):
        v.wait_ge(ssct.h, sctv)
        v.reciprocal(strow[0:1, 2 * CH:2 * CH + w],
                     strow[0:1, 1 * CH:1 * CH + w]).then_inc(srow2.h, 1)
        v.wait_ge(srow2.h, r2a)
        v.scalar_tensor_tensor(strow[0:1, 3 * CH:3 * CH + w],
                               strow[0:1, 0 * CH:0 * CH + w], -1.0,
                               strow[0:1, 2 * CH:2 * CH + w],
                               ALU.mult, ALU.mult).then_inc(srow2.h, 1)
    srow2.v += 2
    c.on("vector", frow2)

    r2v, ev_prev = srow2.v, sev.v

    def fbc(t, r2v=r2v, ev_prev=ev_prev):
        t.wait_ge(srow2.h, r2v)
        if ev_prev:
            t.wait_ge(sev.h, ev_prev)
        t.matmul(ps_b1[:, :w], onesr_f[0:1, :], strow[0:1, 2 * CH:2 * CH + w],
                 start=True, stop=True)
        t.matmul(ps_b2[:, :w], onesr_f[0:1, :], strow[0:1, 3 * CH:3 * CH + w],
                 start=True, stop=True).then_inc(sbc.h, 1)
    sbc.v += 1
    c.on("tensor", fbc)

    bcv = sbc.v

    def fev(v, bcv=bcv):
        v.wait_ge(sbc.h, bcv)
        v.tensor_copy(rstd_b[:, :w], ps_b1[:, :w])
        last = v.tensor_copy(negm_b[:, :w], ps_b2[:, :w])
        if xn is not None:
            for k in range(KT):
                v.tensor_tensor(tmp[:, :w], xc[:, k * CH:k * CH + w], rstd_b[:, :w], ALU.mult)
                last = v.tensor_tensor(xn[:, k * stride + xoff:k * stride + xoff + w],
                                       tmp[:, :w], negm_b[:, :w], ALU.add)
        last.then_inc(sev.h, 1)
    sev.v += 1
    c.on("vector", fev)
    return sev, sev.v


def emit_ln_all(c: Ctx, tag, x_dram, xn):
    for ci in range(NCH):
        sld, v = load_x_chunk(c, x_dram, ci, tag=tag + "lx%d" % ci)
        emit_ln_chunk(c, tag, ci, xn, wait=(sld, v))
        c.flush()


def emit_proj(c: Ctx, tag, lhsT, rhs, n_k, n_of, chunks, epil, psA=None, psB=None):
    smm = c.sem(tag + "mm")
    sev = c.sem(tag + "ev")
    psA = psA or c.t["mm_psA"]
    psB = psB or c.t["mm_psB"]
    i = 0
    for of in range(n_of):
        for ci in range(len(chunks)):
            off, w = chunks[ci]
            ps = (psA, psB)[i % 2]
            ev_need = sev.v - 1

            def fmm(t, of=of, ci=ci, ps=ps, w=w, ev_need=ev_need):
                if ev_need > 0:
                    t.wait_ge(sev.h, ev_need)
                last = None
                for k in range(n_k):
                    last = t.matmul(ps[:, :w], lhsT(k, of), rhs(k, ci),
                                    start=(k == 0), stop=(k == n_k - 1))
                last.then_inc(smm.h, 1)
            smm.v += 1
            c.on("tensor", fmm)
            epil(of, ci, ps, smm.v, smm, sev)
            sev.v += 1
            i += 1
    assert sev.v == smm.v


def dma_weight(c: Ctx, tag, dram_ap, sbuf_t, rows, cols):
    s = c.sem(tag)

    def f(g):
        src = dram_ap.rearrange("(k p) o -> p k o", p=128)
        dst = sbuf_t[:, 0:(rows // 128) * cols].rearrange("p (k o) -> p k o", o=cols)
        g.dma_start(dst, src).then_inc(s.h, 16)
    s.v += 16
    c.on("gpsimd", f)
    return s, s.v


def store_x_chunk(c: Ctx, x_out, ci, wait_sem, wait_v, w=CH):
    sout = c.sem("sout")
    xc = c.t["xc"]
    off = ci * CH

    def f(sy, off=off, w=w, wait_v=wait_v):
        sy.wait_ge(wait_sem.h, wait_v)
        dst = x_out.rearrange("k p t -> p k t")[:, :, off:off + w]
        sy.dma_start(dst, xc[:, :].rearrange("p (k t) -> p k t", k=KT)[:, :, :w]).then_inc(sout.h, 16)
    sout.v += 16
    c.on("sync", f)


def emit_l0f(c: Ctx, x_in, x_out, wd):
    w_s = c.sb("l0f_w", [128, KT * D], BF16)
    a1_s = c.sb("l0f_a1", [128, KT * D], BF16)
    a2_s = c.sb("l0f_a2", [128, KT * D], BF16)
    b_s = c.sb("l0f_b", [128, 8], F32)
    xn = c.sb("xn", [128, KT * N], BF16)
    h1 = c.sb("h1", [128, KT * N], BF16)
    h2 = c.sb("h2", [128, KT * N], BF16)
    tmp2 = c.sb("tmp2", [128, 2 * CH], F32)

    dma_weight(c, "wl1", wd["l0f_w_d"], w_s, D, D)
    dma_weight(c, "wl2", wd["l0f_a1_d"], a1_s, D, D)
    dma_weight(c, "wl3", wd["l0f_a2_d"], a2_s, D, D)
    sb_ = c.sem("wl4")

    def fb(sy):
        sy.dma_start(b_s[:, :], wd["l0f_b_d"]).then_inc(sb_.h, 16)
    sb_.v += 16
    c.on("sync", fb)

    emit_ln_all(c, "ln", x_in, xn)
    c.flush()
    c.dma_barrier(c.sem("wl1"), c.sem("wl2"), c.sem("wl3"), c.sem("wl4"))

    def wsl(t):
        return lambda k, of: t[:, k * D + of * 128: k * D + (of + 1) * 128]

    def xsl(t):
        return lambda k, ci: t[:, k * N + ci * CH: (k * N + ci * CH) + CH]

    chunks = [(i * CH, CH) for i in range(NCH)]

    def cp_epil(dst):
        def epil(of, ci, ps, mmv, smm, sev):
            def f(s, of=of, ci=ci, ps=ps, mmv=mmv):
                s.wait_ge(smm.h, mmv)
                s.copy(dst[:, of * N + ci * CH: of * N + ci * CH + CH],
                       ps[:, :CH]).then_inc(sev.h, 1)
            c.on("scalar", f)
        return epil

    emit_proj(c, "f1", wsl(w_s), xsl(xn), KT, KT, chunks, cp_epil(h1))
    c.flush()
    emit_proj(c, "f2", wsl(a1_s), xsl(h1), KT, KT, chunks, cp_epil(h2))
    c.flush()

    smm = c.sem("f3mm")
    sth = c.sem("f3th")
    sv = c.sem("f3v")
    psA, psB = c.t["mm_psA"], c.t["mm_psB"]
    for ci in range(NCH):
        sld, ldv = load_x_chunk(c, x_in, ci, tag="f3lx%d" % ci)
        for of in range(KT):
            i = ci * KT + of
            ps = (psA, psB)[i % 2]
            ev_need = sth.v - 1

            def fmm(t, of=of, ci=ci, ps=ps, ev_need=ev_need):
                if ev_need > 0:
                    t.wait_ge(sth.h, ev_need)
                last = None
                for k in range(KT):
                    last = t.matmul(ps[:, :CH], a2_s[:, k * D + of * 128:k * D + (of + 1) * 128],
                                    h2[:, k * N + ci * CH:k * N + ci * CH + CH],
                                    start=(k == 0), stop=(k == KT - 1))
                last.then_inc(smm.h, 1)
            smm.v += 1
            c.on("tensor", fmm)
            mmv = smm.v
            vprev = sv.v
            slot = tmp2[:, (i % 2) * CH:(i % 2) * CH + CH]

            def fs(s, of=of, ps=ps, mmv=mmv, vprev=vprev, slot=slot):
                s.wait_ge(smm.h, mmv)
                if vprev > 1:
                    s.wait_ge(sv.h, vprev - 1)
                s.activation(slot, ps[:, :CH], AF.Tanh,
                             bias=b_s[:, of:of + 1], scale=1.0).then_inc(sth.h, 1)
            sth.v += 1
            c.on("scalar", fs)
            thv = sth.v

            def fv(v, of=of, ci=ci, thv=thv, ldv=ldv, slot=slot):
                v.wait_ge(sth.h, thv)
                v.wait_ge(c.sem("f3lx%d" % ci).h, ldv)
                sl = c.t["xc"][:, of * CH:(of + 1) * CH]
                v.tensor_tensor(sl, sl, slot, ALU.add).then_inc(sv.h, 1)
            sv.v += 1
            c.on("vector", fv)
        store_x_chunk(c, x_out, ci, sv, sv.v)
        c.flush()


def emit_l3f(c: Ctx, x_in, x_out, wd):
    FF = 4096
    FT = FF // 128
    w1b = c.sb("w1b", [128, KT * FF], BF16)
    w2b = c.sb("w2b", [128, FT * D], BF16)
    wstage = c.t["xc"]
    ternA = c.t["sq"]
    xnc = c.sb("xnc", [128, KT * CH], BF16)
    h1c = c.sb("h1c", [128, 8 * CH], BF16)
    outacc = c.t["sq"]
    b1_s = c.sb("b1_s", [128, FT], F32)
    b2_s = c.sb("b2_s", [128, KT], F32)
    acc1 = c.sb("acc1", [128, KT + FT + 2], F32)
    srow1 = c.sb("trow", [1, 16], F32)
    thr = c.sb("thr", [128, 4], F32)
    ps2A = c.ps("ps2A", [128, CH])
    ps2B = c.ps("ps2B", [128, CH])

    sq = c.t["sq"]
    sdw = c.sem("l3dw")
    sbias = c.sem("l3bias")

    def fbias(sy):
        sy.dma_start(b1_s[:, :], wd["l3f_b1_d"]).then_inc(sbias.h, 16)
        sy.dma_start(b2_s[:, :], wd["l3f_b2_d"]).then_inc(sbias.h, 16)
    sbias.v += 32
    c.on("sync", fbias)

    sred = c.sem("l3red")
    npieces = KT + FT
    for p in range(npieces):
        if p < KT:
            src = wd["l3f_w1t_d"][p * 128:(p + 1) * 128, :]
            wcols = FF
        else:
            k2 = p - KT
            src = wd["l3f_w2t_d"][k2 * 128:(k2 + 1) * 128, :]
            wcols = D
        redprev = sred.v

        def fdma(g, p=p, src=src, wcols=wcols, redprev=redprev):
            if redprev:
                g.wait_ge(sred.h, redprev)
            g.dma_start(wstage[:, :wcols], src).then_inc(sdw.h, 16)
        sdw.v += 16
        c.on("gpsimd", fdma)
        dv = sdw.v

        def fred(v, p=p, dv=dv, wcols=wcols):
            v.wait_ge(sdw.h, dv)
            v.reduce_sum(acc1[:, p:p + 1], wstage[:, :wcols], axis=AX.X,
                         apply_absolute_value=True).then_inc(sred.h, 1)
        sred.v += 1
        c.on("vector", fred)
    c.flush()

    ps_s = c.t["ps_s"]
    stsc = c.sem("l3sc")

    def fsc_t(t):
        t.matmul(ps_s[0:1, 0:KT], c.t["ones_f"][:, :], acc1[:, 0:KT], start=True, stop=True)
        t.matmul(ps_s[0:1, KT:KT + FT], c.t["ones_f"][:, :], acc1[:, KT:KT + FT],
                 start=True, stop=True).then_inc(stsc.h, 1)
    stsc.v += 1
    c.on("tensor", fsc_t)
    scv = stsc.v

    def fsc_v(v, scv=scv):
        v.wait_ge(stsc.h, scv)
        v.reduce_sum(srow1[0:1, 0:1], ps_s[0:1, 0:KT], axis=AX.X)
        v.reduce_sum(srow1[0:1, 1:2], ps_s[0:1, KT:KT + FT], axis=AX.X)
        v.tensor_scalar_mul(srow1[0:1, 2:3], srow1[0:1, 0:1], 0.5 / (D * FF))
        v.tensor_scalar_mul(srow1[0:1, 3:4], srow1[0:1, 1:2], 0.5 / (D * FF))
        v.tensor_scalar_max(srow1[0:1, 2:3], srow1[0:1, 2:3], 0.5e-8)
        v.tensor_scalar_max(srow1[0:1, 3:4], srow1[0:1, 3:4], 0.5e-8)
        v.tensor_scalar_mul(srow1[0:1, 4:5], srow1[0:1, 2:3], -1.0)
        v.tensor_scalar_mul(srow1[0:1, 5:6], srow1[0:1, 3:4], -1.0).then_inc(stsc.h, 1)
    stsc.v += 1
    c.on("vector", fsc_v)
    scv2 = stsc.v

    def fsc_t2(t, scv2=scv2):
        t.wait_ge(stsc.h, scv2)
        t.matmul(c.t["ps_b1"][:, 0:4], c.t["onesr_f"][0:1, :], srow1[0:1, 2:6],
                 start=True, stop=True).then_inc(stsc.h, 1)
    stsc.v += 1
    c.on("tensor", fsc_t2)
    scv3 = stsc.v

    def fsc_v2(v, scv3=scv3):
        v.wait_ge(stsc.h, scv3)
        v.tensor_copy(thr[:, :], c.t["ps_b1"][:, 0:4]).then_inc(stsc.h, 1)
    stsc.v += 1
    c.on("vector", fsc_v2)
    c.flush()

    stn = c.sem("l3tn")
    for p in range(npieces):
        if p < KT:
            src = wd["l3f_w1t_d"][p * 128:(p + 1) * 128, :]
            wcols, tc, ntc = FF, 0, 2
            dst = w1b[:, p * FF:(p + 1) * FF]
        else:
            k2 = p - KT
            src = wd["l3f_w2t_d"][k2 * 128:(k2 + 1) * 128, :]
            wcols, tc, ntc = D, 1, 3
            dst = w2b[:, k2 * D:(k2 + 1) * D]
        tprev = stn.v

        def fdma(g, src=src, wcols=wcols, tprev=tprev):
            if tprev:
                g.wait_ge(stn.h, tprev)
            g.dma_start(wstage[:, :wcols], src).then_inc(sdw.h, 16)
        sdw.v += 16
        c.on("gpsimd", fdma)
        dv = sdw.v

        def ftn(v, dv=dv, wcols=wcols, tc=tc, ntc=ntc, dst=dst):
            v.wait_ge(sdw.h, dv)
            v.tensor_scalar(ternA[:, :wcols], wstage[:, :wcols], thr[:, ntc:ntc + 1], None, ALU.is_le)
            v.scalar_tensor_tensor(dst, wstage[:, :wcols], thr[:, tc:tc + 1],
                                   ternA[:, :wcols], ALU.is_ge, ALU.subtract).then_inc(stn.h, 1)
        stn.v += 1
        c.on("vector", ftn)
    c.flush()

    c.dma_barrier(sbias)
    psA, psB = c.t["mm_psA"], c.t["mm_psB"]
    for ci in range(NCH):
        sld, ldv = load_x_chunk(c, x_in, ci, tag="l3lx%d" % ci)
        lnsem, lnv = emit_ln_chunk(c, "l3ln", ci, xnc, wait=(sld, ldv), xn_ktile_stride=CH)
        c.flush()
        for slab in range(4):
            smm = c.sem("l3m%d_%d" % (ci, slab))
            sg = c.sem("l3g%d_%d" % (ci, slab))
            for of1 in range(8):
                fft = slab * 8 + of1
                ps = (psA, psB)[of1 % 2]
                gprev = sg.v - 1

                def fmm(t, fft=fft, ps=ps, gprev=gprev):
                    if gprev > 0:
                        t.wait_ge(sg.h, gprev)
                    last = None
                    for k in range(KT):
                        last = t.matmul(ps[:, :CH],
                                        w1b[:, k * FF + fft * 128:k * FF + (fft + 1) * 128],
                                        xnc[:, k * CH:(k + 1) * CH],
                                        start=(k == 0), stop=(k == KT - 1))
                    last.then_inc(smm.h, 1)
                smm.v += 1
                c.on("tensor", fmm)
                mmv = smm.v

                def fgelu(s, fft=fft, of1=of1, ps=ps, mmv=mmv):
                    s.wait_ge(smm.h, mmv)
                    s.activation(h1c[:, of1 * CH:(of1 + 1) * CH], ps[:, :CH], AF.Gelu,
                                 bias=b1_s[:, fft:fft + 1], scale=1.0).then_inc(sg.h, 1)
                sg.v += 1
                c.on("scalar", fgelu)
            sv2 = c.sem("l3v%d_%d" % (ci, slab))
            gfin = sg.v
            for of2 in range(KT):
                ps = (ps2A, ps2B)[of2 % 2]
                vprev = sv2.v - 1

                def fmm2(t, of2=of2, slab=slab, ps=ps, vprev=vprev, gfin=gfin):
                    t.wait_ge(sg.h, gfin)
                    if vprev > 0:
                        t.wait_ge(sv2.h, vprev)
                    last = None
                    for k2 in range(8):
                        gk = slab * 8 + k2
                        last = t.matmul(ps[:, :CH],
                                        w2b[:, gk * D + of2 * 128:gk * D + (of2 + 1) * 128],
                                        h1c[:, k2 * CH:(k2 + 1) * CH],
                                        start=(k2 == 0), stop=(k2 == 7))
                    last.then_inc(smm.h, 1)
                smm.v += 1
                c.on("tensor", fmm2)
                mmv = smm.v

                def facc(v, of2=of2, slab=slab, ps=ps, mmv=mmv):
                    v.wait_ge(smm.h, mmv)
                    dst = outacc[:, of2 * CH:(of2 + 1) * CH]
                    if slab == 0:
                        v.tensor_copy(dst, ps[:, :CH]).then_inc(sv2.h, 1)
                    else:
                        v.tensor_tensor(dst, dst, ps[:, :CH], ALU.add).then_inc(sv2.h, 1)
                sv2.v += 1
                c.on("vector", facc)
            c.flush()
        sfin = c.sem("l3fin%d" % ci)
        for of in range(KT):
            def ffin(v, of=of, ci=ci):
                sl = c.t["xc"][:, of * CH:(of + 1) * CH]
                v.tensor_tensor(sl, sl, outacc[:, of * CH:(of + 1) * CH], ALU.add)
                v.tensor_scalar(sl, sl, b2_s[:, of:of + 1], None, ALU.add).then_inc(sfin.h, 1)
            sfin.v += 1
            c.on("vector", ffin)
        store_x_chunk(c, x_out, ci, sfin, sfin.v)
        c.flush()


STAGE_DRAM = {
    "l0f": [("l0f_w_d", [D, D]), ("l0f_a1_d", [D, D]), ("l0f_a2_d", [D, D]),
            ("l0f_b_d", [128, 8])],
    "l3f": [("l3f_w1t_d", [D, 4096]), ("l3f_w2t_d", [4096, D]),
            ("l3f_b1_d", [128, 32]), ("l3f_b2_d", [128, 8])],
}

EMITTERS = {}


def _reg():
    EMITTERS["l0f"] = emit_l0f
    EMITTERS["l3f"] = emit_l3f


def col128(v):
    n = v.shape[0]
    return np.ascontiguousarray(v.reshape(n // 128, 128).T)


def stage_weights(stage, inp):
    if stage == "l0f":
        return {"l0f_w_d": inp["L0_ffn_weight"],
                "l0f_a1_d": inp["L0_ffn_aux_weight1"],
                "l0f_a2_d": inp["L0_ffn_aux_weight2"],
                "l0f_b_d": col128(inp["L0_ffn_bias"])}
    if stage == "l3f":
        return {"l3f_w1t_d": np.ascontiguousarray(inp["L3_ffn_W1"].T),
                "l3f_w2t_d": np.ascontiguousarray(inp["L3_ffn_W2"].T),
                "l3f_b1_d": col128(inp["L3_ffn_b1"]),
                "l3f_b2_d": col128(inp["L3_ffn_b2"])}
    raise KeyError(stage)


_BUILD_CACHE = {}


DEBUG_DUMPS = []


def build_stage(stage):
    if stage in _BUILD_CACHE:
        return _BUILD_CACHE[stage]
    if "l0f" not in EMITTERS:
        _reg()
    stack = ExitStack()
    nc = bass.Bass("TRN2", target_bir_lowering=False)
    x_in = nc.dram_tensor("x_in", [KT, 128, N], F32, kind="ExternalInput").ap()
    x_out = nc.dram_tensor("x_out", [KT, 128, N], F32, kind="ExternalOutput").ap()
    wd = {nm: nc.dram_tensor(nm, list(sh), F32, kind="ExternalInput").ap()
          for nm, sh in STAGE_DRAM[stage]}
    c = Ctx(nc, stack)
    emit_setup(c)
    alloc_shared(c, sq_cols=(4 * CH if stage == "l2f" else KT * CH))
    EMITTERS[stage](c, x_in, x_out, wd)
    import os as _os2
    for nm in _os2.environ.get("DBG_DUMP", "").split(","):
        if nm and nm in c.t:
            tt = c.t[nm]
            sh = [tt.shape[0], tt.shape[1]]
            dmp = nc.dram_tensor("dbg_" + nm, sh, F32, kind="ExternalOutput").ap()
            sd = c.sem("sout")

            def fd(g, tt=tt, dmp=dmp, sd=sd):
                g.dma_start(dmp, tt[:, :]).then_inc(sd.h, 16)
            sd.v += 16
            c.on("gpsimd", fd)
            c.flush()
    sout = c.sem("sout")

    def fw(sy):
        sy.wait_ge(sout.h, sout.v)
    c.on("sync", fw)
    c.flush()
    stack.close()
    _BUILD_CACHE[stage] = nc
    return nc


def x_to_dev(xb):
    return np.ascontiguousarray(xb.T.reshape(KT, 128, N))


def x_from_dev(xd):
    return np.ascontiguousarray(xd.reshape(D, N).T)


def run_stage(stage, xs, inp):
    nc = build_stage(stage)
    w = stage_weights(stage, inp)
    maps = [dict(x_in=xs[ci], **w) for ci in range(8)]
    res = run_bass_kernel_spmd(nc, maps, core_ids=list(range(8)))
    return [np.asarray(r["x_out"]) for r in res.results]


STAGE_ORDER = ["l0a", "l0f", "l1a", "l1f", "l2a", "l2f", "l3a", "l3f"]


def kernel(**inputs):
    xs = [x_to_dev(inputs["L0_attn_x"][b]) for b in range(8)]
    for stage in STAGE_ORDER:
        xs = run_stage(stage, xs, inputs)
    return np.stack([x_from_dev(x) for x in xs])


def emit_mha(c: Ctx, kind, x_in, x_out, wd):
    nc = c.nc
    is_l1 = kind == "l1a"
    xn = c.sb("xn", [128, KT * N], BF16)
    wqk = c.sb("wqk", [128, KT * 2048], BF16)
    wv = c.sb("wv", [128, KT * D], BF16)
    wo = c.sb("wo", [128, KT * D], BF16)
    q_sb = c.sb("q_sb", [128, N], BF16)
    k_sb = c.sb("k_sb", [128, N], BF16)
    qkf = c.t["xc"][:, 0:N]
    rot_sb = c.t["sq"][:, N:2 * N]
    v_tm = c.sb("v_tm", [128, 16 * 128], BF16)
    s_sb = c.t["sq"][:, 0:N]
    p_sb = c.sb("p_sb", [128, N], BF16)
    v_sb = p_sb
    pT_sb = c.sb("pT_sb", [128, 16 * 128], BF16)
    z_sb = c.sb("z_sb", [64, N], BF16) if is_l1 else None
    zbuf = c.sb("zbuf", [128, KT * N], BF16)
    cmask = c.sb("cmask", [128, 4 * CH], BF16)
    mrow = c.sb("mrow", [128, 4], F32)
    ps_tr = c.ps("ps_tr", [128, 512], BF16)
    ps_z = c.ps("ps_z", [64, 128])
    if is_l1:
        qb = c.sb("qb", [128, 8], F32)
        kb = c.sb("kb", [128, 8], F32)
        vb = c.sb("vb", [128, 8], F32)
    else:
        cos2 = c.sb("cos2", [128, N], BF16)
        sin2 = c.sb("sin2", [128, N], BF16)

    import os as _os
    if _os.environ.get("MHA_NOW1", "0") != "1":
        dma_weight(c, kind + "w1", wd[kind + "_wqk_d"], wqk, D, 2048)
    dma_weight(c, kind + "w2", wd[kind + "_wv_d"], wv, D, D)
    dma_weight(c, kind + "w3", wd[kind + "_wo_d"], wo, D, D)
    sx = c.sem(kind + "aux")

    if _os.environ.get("MHA_NOAUX", "0") == "1":
        def faux(g):
            g.wait_ge(sx.h, 0)
        c.on("gpsimd", faux)
        sx.v += 0
        emit_ln_all(c, kind + "ln", x_in, xn)
        c.flush()
        c.dma_barrier(c.sem(kind + "w2"), c.sem(kind + "w3"))
        _skip_aux = True
    else:
        _skip_aux = False

    def faux(g):
        g.dma_start(cmask[:, :], wd["cmask_d"]).then_inc(sx.h, 16)
        if is_l1:
            g.dma_start(qb[:, :], wd["l1a_qb_d"]).then_inc(sx.h, 16)
            g.dma_start(kb[:, :], wd["l1a_kb_d"]).then_inc(sx.h, 16)
            g.dma_start(vb[:, :], wd["l1a_vb_d"]).then_inc(sx.h, 16)
        else:
            g.dma_start(cos2[:, :], wd["l3a_cos_d"]).then_inc(sx.h, 16)
            g.dma_start(sin2[:, :], wd["l3a_sin_d"]).then_inc(sx.h, 16)
    if not _skip_aux:
        sx.v += 64 if is_l1 else 48
        c.on("gpsimd", faux)
        emit_ln_all(c, kind + "ln", x_in, xn)
        c.flush()
        if _os.environ.get("MHA_NOW1", "0") != "1":
            c.dma_barrier(c.sem(kind + "w1"), c.sem(kind + "w2"), c.sem(kind + "w3"), sx)
        else:
            c.dma_barrier(c.sem(kind + "w2"), c.sem(kind + "w3"), sx)

    psA, psB = c.t["mm_psA"], c.t["mm_psB"]

    def proj_pair(tagp, colbase, dst_sb, bias_col, j):
        sp = c.sem(kind + "pp")
        se = c.sem(kind + "pe")
        se_start = se.v
        wsb = wv if colbase is None else wqk
        for ci in range(NCH):
            ps = (psA, psB)[ci % 2]
            eneed = se.v - 1

            def fmm(t, ci=ci, ps=ps, eneed=eneed, j=j, colbase=colbase, wsb=wsb):
                if eneed > 0:
                    t.wait_ge(se.h, eneed)
                cb = j * 128 if colbase is None else colbase
                stride = D if colbase is None else 2048
                last = None
                for k in range(KT):
                    last = t.matmul(ps[:, :CH], wsb[:, k * stride + cb:k * stride + cb + 128],
                                    xn[:, k * N + ci * CH:k * N + ci * CH + CH],
                                    start=(k == 0), stop=(k == KT - 1))
                last.then_inc(sp.h, 1)
            sp.v += 1
            c.on("tensor", fmm)
            mmv = sp.v

            if is_l1:
                def fev(s, ci=ci, ps=ps, mmv=mmv, se_start=se_start):
                    s.wait_ge(sp.h, mmv)
                    if se_start:
                        s.wait_ge(se.h, se_start)
                    s.activation(dst_sb[:, ci * CH:(ci + 1) * CH], ps[:, :CH], AF.Identity,
                                 bias=bias_col, scale=1.0).then_inc(se.h, 1)
                se.v += 1
                c.on("scalar", fev)
            else:
                def fev(s, ci=ci, ps=ps, mmv=mmv, se_start=se_start):
                    s.wait_ge(sp.h, mmv)
                    if se_start:
                        s.wait_ge(se.h, se_start)
                    s.copy(qkf[:, ci * CH:(ci + 1) * CH], ps[:, :CH]).then_inc(se.h, 1)
                se.v += 1
                c.on("scalar", fev)
        if not is_l1 and dst_sb is not v_sb:
            qev = se.v

            def frope(v, qev=qev, dst_sb=dst_sb):
                v.wait_ge(se.h, qev)
                v.tensor_copy(rot_sb[0:32, :], qkf[32:64, :])
                v.tensor_copy(rot_sb[32:64, :], qkf[0:32, :])
                v.tensor_copy(rot_sb[64:96, :], qkf[96:128, :])
                v.tensor_copy(rot_sb[96:128, :], qkf[64:96, :])
                v.tensor_tensor(rot_sb[:, :], rot_sb[:, :], sin2[:, :], ALU.mult)
                v.tensor_tensor(qkf[:, :], qkf[:, :], cos2[:, :], ALU.mult)
                v.tensor_tensor(dst_sb[:, :], qkf[:, :], rot_sb[:, :], ALU.add).then_inc(se.h, 1)
            se.v += 1
            c.on("vector", frope)
        elif not is_l1:
            qev = se.v

            def fvcp(v, qev=qev):
                v.wait_ge(se.h, qev)
                v.tensor_copy(v_sb[:, :], qkf[:, :]).then_inc(se.h, 1)
            se.v += 1
            c.on("vector", fvcp)
        return se

    import os
    npairs = int(os.environ.get("MHA_PAIRS", "8"))
    nqt = int(os.environ.get("MHA_QT", "16"))
    skip_sm = os.environ.get("MHA_SKIP_SM", "0") == "1"
    for j in range(npairs):
        if is_l1:
            sev_ = proj_pair(kind + "v%d" % j, None, v_sb, vb[:, j:j + 1], j)
        else:
            sev_ = proj_pair(kind + "v%d" % j, None, v_sb, None, j)
        str_ = c.sem(kind + "tr")
        ste = c.sem(kind + "te")
        evfin = sev_.v
        for g in range(4):
            teneed = ste.v

            def ftr(t, g=g, evfin=evfin, teneed=teneed, sev_=sev_):
                t.wait_ge(sev_.h, evfin)
                if teneed > 0:
                    t.wait_ge(ste.h, teneed)
                last = None
                for ii in range(4):
                    i = g * 4 + ii
                    last = t.transpose(ps_tr[:, ii * 128:(ii + 1) * 128],
                                       v_sb[:, i * 128:(i + 1) * 128], c.t["idn"][:, :])
                last.then_inc(str_.h, 1)
            str_.v += 1
            c.on("tensor", ftr)
            trv = str_.v

            def fte(v, g=g, trv=trv):
                v.wait_ge(str_.h, trv)
                v.tensor_copy(v_tm[:, g * 512:(g + 1) * 512], ps_tr[:, :]).then_inc(ste.h, 1)
            ste.v += 1
            c.on("vector", fte)
        c.flush()
        if is_l1:
            proj_pair(kind + "q%d" % j, j * 256, q_sb, qb[:, j:j + 1], j)
            proj_pair(kind + "k%d" % j, j * 256 + 128, k_sb, kb[:, j:j + 1], j)
        else:
            proj_pair(kind + "q%d" % j, j * 256, q_sb, None, j)
            proj_pair(kind + "k%d" % j, j * 256 + 128, k_sb, None, j)
        c.flush()

        for hh in range(2):
            h = 2 * j + hh
            base = hh * 64
            for qt in range(nqt):
                nkc = qt // 4 + 1
                nkt = qt + 1
                nk = nkt * 128
                ssc = c.sem(kind + "sc")
                sse = c.sem(kind + "se")
                for kc in range(nkc):
                    ps = (psA, psB)[kc % 2]
                    eneed = sse.v - 1

                    def fsc(t, qt=qt, kc=kc, ps=ps, eneed=eneed, base=base):
                        if eneed > 0:
                            t.wait_ge(sse.h, eneed)
                        t.matmul(ps[:, :CH], q_sb[base:base + 64, qt * 128:(qt + 1) * 128],
                                 k_sb[base:base + 64, kc * CH:(kc + 1) * CH],
                                 start=True, stop=True).then_inc(ssc.h, 1)
                    ssc.v += 1
                    c.on("tensor", fsc)
                    scv = ssc.v
                    diag = (kc == nkc - 1)

                    def fse(v, qt=qt, kc=kc, ps=ps, scv=scv, diag=diag):
                        v.wait_ge(ssc.h, scv)
                        dst = s_sb[:, kc * CH:(kc + 1) * CH]
                        if diag:
                            dl = (qt * 128 - kc * CH) // 128
                            v.tensor_tensor(dst, ps[:, :CH],
                                            cmask[:, dl * CH:(dl + 1) * CH], ALU.add).then_inc(sse.h, 1)
                        else:
                            v.tensor_copy(dst, ps[:, :CH]).then_inc(sse.h, 1)
                    sse.v += 1
                    c.on("vector", fse)
                sev = sse.v

                def fsm1(v, qt=qt, nk=nk, sev=sev, e1=sse.v + 1):
                    v.wait_ge(sse.h, sev)
                    v.memset(mrow[:, 2:3], 0.0)
                    v.reduce_max(mrow[:, 0:1], s_sb[:, :nk], axis=AX.X).then_inc(sse.h, 1)
                    v.wait_ge(sse.h, e1)
                    v.tensor_scalar_mul(mrow[:, 1:2], mrow[:, 0:1], -0.125).then_inc(sse.h, 1)
                sse.v += 2
                c.on("vector", fsm1)
                sev2 = sse.v

                def fsm2(s, qt=qt, nk=nk, sev2=sev2):
                    s.wait_ge(sse.h, sev2)
                    s.activation(p_sb[:, :nk], s_sb[:, :nk], AF.Exp,
                                 bias=mrow[:, 1:2], scale=0.125,
                                 accum_out=mrow[:, 2:3]).then_inc(sse.h, 1)
                sse.v += 1
                c.on("scalar", fsm2)
                sev3 = sse.v

                def fsm3(v, sev3=sev3, nk=nk, ra=sse.v + 1):
                    v.wait_ge(sse.h, sev3)
                    v.reciprocal(mrow[:, 3:4], mrow[:, 2:3]).then_inc(sse.h, 1)
                    v.wait_ge(sse.h, ra)
                    if is_l1:
                        v.tensor_scalar(p_sb[:, :nk], p_sb[:, :nk], mrow[:, 3:4], 0.125,
                                        ALU.mult, ALU.mult).then_inc(sse.h, 1)
                    else:
                        v.tensor_scalar_mul(p_sb[:, :nk], p_sb[:, :nk], mrow[:, 3:4]).then_inc(sse.h, 1)
                sse.v += 2
                c.on("vector", fsm3)
                stp = c.sem(kind + "tp")
                stq = c.sem(kind + "tq")
                pscale = sse.v
                ngrp = (nkt + 3) // 4
                for g in range(ngrp):
                    gw = min(4, nkt - g * 4)
                    tqneed = stq.v

                    def ftr(t, g=g, gw=gw, pscale=pscale, tqneed=tqneed, stp=stp, stq=stq):
                        t.wait_ge(sse.h, pscale)
                        if tqneed > 0:
                            t.wait_ge(stq.h, tqneed)
                        last = None
                        for ii in range(gw):
                            i = g * 4 + ii
                            last = t.transpose(ps_tr[:, ii * 128:(ii + 1) * 128],
                                               p_sb[:, i * 128:(i + 1) * 128], c.t["idn"][:, :])
                        last.then_inc(stp.h, 1)
                    stp.v += 1
                    c.on("tensor", ftr)
                    tpv = stp.v

                    def fte(v, g=g, gw=gw, tpv=tpv, stp=stp, stq=stq):
                        v.wait_ge(stp.h, tpv)
                        v.tensor_copy(pT_sb[:, g * 512:g * 512 + gw * 128],
                                      ps_tr[:, :gw * 128]).then_inc(stq.h, 1)
                    stq.v += 1
                    c.on("vector", fte)
                saz = c.sem(kind + "az")
                tqfin = stq.v
                azprev = saz.v

                def fav(t, qt=qt, nkt=nkt, hh=hh, tqfin=tqfin, azprev=azprev, stq=stq, saz=saz):
                    t.wait_ge(stq.h, tqfin)
                    if azprev:
                        t.wait_ge(saz.h, azprev)
                    last = None
                    for i in range(nkt):
                        last = t.matmul(ps_z[:, :], v_tm[:, i * 128 + hh * 64:i * 128 + hh * 64 + 64],
                                        pT_sb[:, i * 128:(i + 1) * 128],
                                        start=(i == 0), stop=(i == nkt - 1))
                    last.then_inc(saz.h, 1)
                saz.v += 1
                c.on("tensor", fav)
                azv = saz.v

                def fze(v, qt=qt, h=h, azv=azv, saz=saz):
                    v.wait_ge(saz.h, azv)
                    if is_l1:
                        v.tensor_copy(z_sb[:, qt * 128:(qt + 1) * 128], ps_z[:, :]).then_inc(saz.h, 1)
                    else:
                        v.tensor_copy(zbuf[(h % 2) * 64:(h % 2) * 64 + 64,
                                           (h // 2) * N + qt * 128:(h // 2) * N + (qt + 1) * 128],
                                      ps_z[:, :]).then_inc(saz.h, 1)
                saz.v += 1
                c.on("vector", fze)
            if is_l1:
                def fscat(v, h=h):
                    for g in range(16):
                        v.tensor_copy(zbuf[(g % 2) * 64:(g % 2) * 64 + 64,
                                           (g // 2) * N + h * 128:(g // 2) * N + (h + 1) * 128],
                                      z_sb[:, g::16])
                c.on("vector", fscat)
            c.flush()

    smm = c.sem(kind + "omm")
    sv = c.sem(kind + "ov")
    _noproj = _os.environ.get("MHA_NOPROJ", "0") == "1"
    if _os.environ.get("MHA_MEMSET_Z", "0") == "1":
        def fmz(g):
            g.memset(zbuf[:, :], 0.0)
        c.on("gpsimd", fmz)
        c.flush()
    for ci in range(NCH):
        sld, ldv = load_x_chunk(c, x_in, ci, tag=kind + "olx%d" % ci)
        if _noproj:
            store_x_chunk(c, x_out, ci, sld, ldv)
            c.flush()
            continue
        for of in range(KT):
            i = ci * KT + of
            ps = (psA, psB)[i % 2]
            vneed = sv.v - 1

            def fmm(t, of=of, ci=ci, ps=ps, vneed=vneed):
                if vneed > 0:
                    t.wait_ge(sv.h, vneed)
                last = None
                for k in range(KT):
                    last = t.matmul(ps[:, :CH], wo[:, k * D + of * 128:k * D + (of + 1) * 128],
                                    zbuf[:, k * N + ci * CH:k * N + ci * CH + CH],
                                    start=(k == 0), stop=(k == KT - 1))
                last.then_inc(smm.h, 1)
            smm.v += 1
            c.on("tensor", fmm)
            mmv = smm.v
            slot = (c.t["tmp"][:, :CH], c.t["sq"][:, :CH])[i % 2]

            def fs(s, ps=ps, mmv=mmv, vprev=sv.v, slot=slot):
                s.wait_ge(smm.h, mmv)
                if vprev > 1:
                    s.wait_ge(sv.h, vprev - 1)
                s.copy(slot, ps[:, :CH]).then_inc(smm.h, 1)
            smm.v += 1
            c.on("scalar", fs)
            scv = smm.v

            def fv(v, of=of, ci=ci, scv=scv, ldv=ldv, slot=slot):
                v.wait_ge(smm.h, scv)
                v.wait_ge(c.sem(kind + "olx%d" % ci).h, ldv)
                sl = c.t["xc"][:, of * CH:(of + 1) * CH]
                v.tensor_tensor(sl, sl, slot, ALU.add).then_inc(sv.h, 1)
            sv.v += 1
            c.on("vector", fv)
        store_x_chunk(c, x_out, ci, sv, sv.v)
        c.flush()


def emit_l1a(c, x_in, x_out, wd):
    emit_mha(c, "l1a", x_in, x_out, wd)


def emit_l3a(c, x_in, x_out, wd):
    emit_mha(c, "l3a", x_in, x_out, wd)


STAGE_DRAM["l1a"] = [("l1a_wqk_d", [D, 2048]), ("l1a_wv_d", [D, D]), ("l1a_wo_d", [D, D]),
                     ("l1a_qb_d", [128, 8]), ("l1a_kb_d", [128, 8]), ("l1a_vb_d", [128, 8]),
                     ("cmask_d", [128, 4 * CH])]
STAGE_DRAM["l3a"] = [("l3a_wqk_d", [D, 2048]), ("l3a_wv_d", [D, D]), ("l3a_wo_d", [D, D]),
                     ("l3a_cos_d", [128, N]), ("l3a_sin_d", [128, N]), ("cmask_d", [128, 4 * CH])]


def _causal_mask():
    m = np.zeros((128, 4 * CH), np.float32)
    p = np.arange(128)[:, None]
    f = np.arange(CH)[None, :]
    for dl in range(4):
        m[:, dl * CH:(dl + 1) * CH] = np.where(f <= dl * 128 + p, 0.0, -1e30)
    return m


def _rope_tables():
    DH = 64
    inv_freq = 1.0 / 10000.0 ** (np.arange(0, DH, 2, dtype=np.float32) / DH)
    t = np.arange(N, dtype=np.float32)
    freqs = np.outer(t, inv_freq)
    emb = np.concatenate([freqs, freqs], -1)
    cos = np.cos(emb).T.astype(np.float32)
    sin = np.sin(emb).T.astype(np.float32)
    cos2 = np.concatenate([cos, cos], 0)
    sins = np.concatenate([-sin[0:32], sin[32:64]], 0)
    sin2 = np.concatenate([sins, sins], 0)
    return np.ascontiguousarray(cos2), np.ascontiguousarray(sin2)


def _stage_l1a(inp):
    Wq, Wk, Wv = inp["L1_attn_Wq"], inp["L1_attn_Wk"], inp["L1_attn_Wv"]
    bq, bk, bv = inp["L1_attn_bq"], inp["L1_attn_bk"], inp["L1_attn_bv"]
    wqk = np.zeros((D, 2048), np.float32)
    qb = np.zeros((128, 8), np.float32)
    kb = np.zeros((128, 8), np.float32)
    for jj in range(8):
        wqk[:, jj * 256 + 0:jj * 256 + 64] = Wq[2 * jj].T
        wqk[:, jj * 256 + 64:jj * 256 + 128] = Wq[2 * jj + 1].T
        wqk[:, jj * 256 + 128:jj * 256 + 192] = Wk[2 * jj].T
        wqk[:, jj * 256 + 192:jj * 256 + 256] = Wk[2 * jj + 1].T
        qb[0:64, jj] = bq[2 * jj]
        qb[64:128, jj] = bq[2 * jj + 1]
        kb[0:64, jj] = bk[2 * jj]
        kb[64:128, jj] = bk[2 * jj + 1]
    wv = np.ascontiguousarray(Wv.reshape(1024, 1024).T)
    vb = np.ascontiguousarray(bv.reshape(1024).reshape(8, 128).T)
    return {"l1a_wqk_d": np.ascontiguousarray(wqk), "l1a_wv_d": wv,
            "l1a_wo_d": np.ascontiguousarray(inp["L1_attn_proj"]),
            "l1a_qb_d": qb, "l1a_kb_d": kb, "l1a_vb_d": vb, "cmask_d": _causal_mask()}


def _stage_l3a(inp):
    Wq, Wk = inp["L3_attn_Wq"], inp["L3_attn_Wk"]
    wqk = np.zeros((D, 2048), np.float32)
    for jj in range(8):
        wqk[:, jj * 256 + 0:jj * 256 + 128] = Wq.T[:, jj * 128:(jj + 1) * 128]
        wqk[:, jj * 256 + 128:jj * 256 + 256] = Wk.T[:, jj * 128:(jj + 1) * 128]
    cosqk, sinqk = _rope_tables()
    return {"l3a_wqk_d": np.ascontiguousarray(wqk),
            "l3a_wv_d": np.ascontiguousarray(inp["L3_attn_Wv"].T),
            "l3a_wo_d": np.ascontiguousarray(inp["L3_attn_Wo"].T),
            "l3a_cos_d": cosqk, "l3a_sin_d": sinqk, "cmask_d": _causal_mask()}


_OLD_STAGE = stage_weights


def stage_weights(stage, inp):
    if stage == "l1a":
        return _stage_l1a(inp)
    if stage == "l3a":
        return _stage_l3a(inp)
    return _OLD_STAGE(stage, inp)


def _reg2():
    _reg()
    EMITTERS["l1a"] = emit_l1a
    EMITTERS["l3a"] = emit_l3a


_OLD_BUILD = build_stage


def build_stage(stage):
    if not EMITTERS:
        _reg2()
    EMITTERS.setdefault("l1a", emit_l1a)
    EMITTERS.setdefault("l3a", emit_l3a)
    return _OLD_BUILD(stage)


def emit_l2f(c: Ctx, x_in, x_out, wd):
    nc = c.nc
    s = c.sb("s_strm", [128, KT * N], F32)
    NP1 = N + 1
    sbf = c.sb("sbf", [128, KT * NP1], BF16)
    ubuf = c.sb("ubuf", [128, 16384], BF16)
    q_sb = ubuf[:, 0:2048]
    k_sb = ubuf[:, 2048:4096]
    p_sb = ubuf[:, 4096:5120]
    pT_sb = ubuf[:, 5120:6144]
    v_tm = ubuf[:, 6144:8192]
    v_sb = ubuf[:, 4096:6144]
    zl = ubuf[:, 8192:16384]
    wpair = c.sb("wpair", [128, KT * 384], BF16)
    wscr = c.sb("wscr", [128, 4864], BF16)
    wo2 = wscr[:, 0:4096]
    w1s = wscr[:, 0:2048]
    w2s = wscr[:, 2048:4864]
    cosk = c.sb("cosk", [128, N], BF16)
    sink = c.sb("sink", [128, N], BF16)
    etab = c.sb("etab", [128, N], BF16)
    cmask = c.sb("cmask2", [128, 4 * CH], BF16)
    pmask = c.sb("pmask", [128, 4 * CH], BF16)
    mrow = c.sb("mrow2", [128, 4], F32)
    smallp = c.sb("smallp", [128, 2 * 44 + 2 * 8 + 2 * 16 + 2 + 2], F32)
    b1p = smallp[:, 0:88]
    b2p = smallp[:, 88:104]
    lgp = smallp[:, 104:120]
    lbp = smallp[:, 120:136]
    qsp = smallp[:, 136:138]
    ksp = smallp[:, 138:140]
    ps_tr = c.ps("ps_tr2", [128, 512], BF16)
    ps_z = c.ps("ps_z2", [64, 128])
    psA, psB = c.t["mm_psA"], c.t["mm_psB"]
    ps_s, ps_q, ps_b1, ps_b2 = c.t["ps_s"], c.t["ps_q"], c.t["ps_b1"], c.t["ps_b2"]
    rb_sb, qn32, tmp = c.t["rstd_b"], c.t["negm_b"], c.t["tmp"]
    sqb = c.t["sq"]
    strow = c.t["strow"]

    sx = c.sem("l2faux")
    import os as _os7
    _noaux2 = _os7.environ.get("L2F_NOAUX2", "0") == "1"

    def faux(g):
        g.dma_start(cosk[:, :], wd["l2f_cosk_d"]).then_inc(sx.h, 16)
        g.dma_start(sink[:, :], wd["l2f_sink_d"]).then_inc(sx.h, 16)
        g.dma_start(etab[:, :], wd["l2f_etab_d"]).then_inc(sx.h, 16)
        g.dma_start(cmask[:, :], wd["cmask_d"]).then_inc(sx.h, 16)
        g.dma_start(pmask[:, :], wd["l2f_pmask_d"]).then_inc(sx.h, 16)
    if not _noaux2:
        sx.v += 80
        c.on("gpsimd", faux)
    sy_ = c.sem("l2fsp")

    def fsp(sy):
        sy.dma_start(b1p[:, :], wd["l2f_b1p_d"]).then_inc(sy_.h, 16)
        sy.dma_start(b2p[:, :], wd["l2f_b2_d"]).then_inc(sy_.h, 16)
        sy.dma_start(lgp[:, :], wd["l2f_lg_d"]).then_inc(sy_.h, 16)
        sy.dma_start(lbp[:, :], wd["l2f_lb_d"]).then_inc(sy_.h, 16)
        sy.dma_start(qsp[:, :], wd["l2f_qs_d"]).then_inc(sy_.h, 16)
        sy.dma_start(ksp[:, :], wd["l2f_ks_d"]).then_inc(sy_.h, 16)
    if not _noaux2:
        sy_.v += 96
        c.on("sync", fsp)

    import os as _os9
    if _os9.environ.get("DBG_MEMSET", "0") == "1":
        def fmz(g):
            g.memset(ubuf[:, :], 0.0)
            g.memset(wpair[:, :], 0.0)
            g.memset(wscr[:, :], 0.0)
            g.memset(sbf[:, :], 0.0)
            g.memset(s[:, :], 0.0)
            g.memset(c.t["sq"][:, :], 0.0)
            g.memset(tmp[:, :], 0.0)
            g.memset(mrow[:, :], 0.0)
            g.memset(strow[0:1, :], 0.0)
        c.on("gpsimd", fmz)
        c.flush()
    _noln = _os9.environ.get("L2F_NOLN", "0") == "1"
    _nox = _os9.environ.get("L2F_NOX", "0") == "1"
    for ci in range([] if _nox else range(NCH)) if False else (range(0) if _nox else range(NCH)):
        sld, v_ = load_x_chunk(c, x_in, ci, tag="l2flx%d" % ci)
        if _noln:
            def fcp(vv, ci=ci, sld=sld, v_=v_):
                vv.wait_ge(sld.h, v_)
                for k in range(KT):
                    vv.tensor_copy(s[:, k * N + ci * CH:k * N + ci * CH + CH],
                                   c.t["xc"][:, k * CH:(k + 1) * CH])
            c.on("vector", fcp)
        else:
            emit_ln_chunk(c, "l2fln", ci, s, wait=(sld, v_))
        c.flush()
    c.dma_barrier(sx, sy_)

    def shslice(k, off, w):
        base = k * NP1 + (1 if k < 4 else 0) + off
        return sbf[:, base:base + w]

    def shift_emit(tag):
        first = not hasattr(c, "_l2f_sbf_init")
        c._l2f_sbf_init = True

        def f(vv, first=first):
            for k in range(KT):
                if first:
                    vv.memset(sbf[:, k * NP1:k * NP1 + 1], 0.0)
                vv.tensor_copy(sbf[:, k * NP1 + 1:(k + 1) * NP1], s[:, k * N:(k + 1) * N])
        c.on("vector", f)
        c.flush()

    import os as _os4
    _rlevel = int(_os4.environ.get("L2F_ROPE_LEVEL", "5"))

    def lnorm_rope(pj, isq, lyr, sp, emit_mm):
        dst = q_sb if isq else k_sb
        cosT, sinT = cosk, sink
        scol = qsp[:, lyr:lyr + 1] if isq else ksp[:, lyr:lyr + 1]
        se = c.sem("l2fqe")
        rot2 = sqb[:, 0:CH]
        prev_nmv = [0]
        prev_bcrd = [0]
        psfree = {}
        for ci in range(NCH):
            ps = (psA, psB)[ci % 2]
            mmv = emit_mm(ci, ps, psfree.get(ci - 2, 0))
            if _rlevel <= 1:
                def fev0(sc, ps=ps, mmv=mmv, ci=ci):
                    sc.wait_ge(sp.h, mmv)
                    sc.copy(dst[:, ci * CH:(ci + 1) * CH], ps[:, :CH]).then_inc(se.h, 1)
                se.v += 1
                psfree[ci] = se.v
                c.on("scalar", fev0)
                continue

            def fsq(sc, ps=ps, mmv=mmv, ci=ci, pn=prev_nmv[0]):
                sc.wait_ge(sp.h, mmv)
                if pn:
                    sc.wait_ge(se.h, pn)
                sc.activation(tmp[:, :CH], ps[:, :CH], AF.Square).then_inc(se.h, 1)
            se.v += 1
            c.on("scalar", fsq)
            sqv = se.v

            if _rlevel <= 2:
                def fev0(sc, ps=ps, ci=ci, sqv=sqv):
                    sc.wait_ge(se.h, sqv)
                    sc.copy(dst[:, ci * CH:(ci + 1) * CH], ps[:, :CH]).then_inc(se.h, 7)
                se.v += 7
                psfree[ci] = se.v
                c.on("scalar", fev0)
                continue

            def fnm(t, sqv=sqv, ci=ci):
                t.wait_ge(se.h, sqv)
                t.matmul(ps_s[0:1, :CH], c.t["ones_f"][0:64, :], tmp[0:64, :CH],
                         start=True, stop=True)
                t.matmul(ps_q[0:1, :CH], c.t["ones_f"][64:128, :], tmp[64:128, :CH],
                         start=True, stop=True).then_inc(se.h, 1)
            se.v += 1
            c.on("tensor", fnm)
            nmv = se.v

            def fsc2(sc, nmv=nmv):
                sc.wait_ge(se.h, nmv)
                sc.activation(strow[0:1, 0:CH], ps_s[0:1, :CH], AF.Sqrt)
                sc.activation(strow[0:1, CH:2 * CH], ps_q[0:1, :CH], AF.Sqrt).then_inc(se.h, 1)
            se.v += 1
            c.on("scalar", fsc2)
            scv = se.v
            prev_nmv[0] = scv

            def frw(vv, scv=scv, ra=se.v + 1):
                vv.wait_ge(se.h, scv)
                vv.tensor_scalar_max(strow[0:1, 0:CH], strow[0:1, 0:CH], 1e-12)
                vv.tensor_scalar_max(strow[0:1, CH:2 * CH], strow[0:1, CH:2 * CH], 1e-12)
                vv.reciprocal(strow[0:1, 2 * CH:3 * CH], strow[0:1, 0:CH]).then_inc(se.h, 1)
                vv.wait_ge(se.h, ra)
                vv.reciprocal(strow[0:1, 3 * CH:4 * CH], strow[0:1, CH:2 * CH]).then_inc(se.h, 1)
            se.v += 2
            c.on("vector", frw)
            rwv = se.v

            if _rlevel <= 3:
                def fev0(sc, ps=ps, ci=ci, rwv=rwv):
                    sc.wait_ge(se.h, rwv)
                    sc.copy(dst[:, ci * CH:(ci + 1) * CH], ps[:, :CH]).then_inc(se.h, 3)
                se.v += 3
                psfree[ci] = se.v
                c.on("scalar", fev0)
                continue

            def fbc(t, rwv=rwv, pb=prev_bcrd[0]):
                t.wait_ge(se.h, rwv)
                if pb:
                    t.wait_ge(se.h, pb)
                t.matmul(ps_b1[0:64, :CH], c.t["onesr_f"][0:1, 0:64],
                         strow[0:1, 2 * CH:3 * CH], start=True, stop=True)
                t.matmul(ps_b1[64:128, :CH], c.t["onesr_f"][0:1, 0:64],
                         strow[0:1, 3 * CH:4 * CH], start=True, stop=True,
                         tile_position=(0, 64)).then_inc(se.h, 1)
            se.v += 1
            c.on("tensor", fbc)
            bcv = se.v

            if _rlevel <= 4:
                def fev0(sc, ps=ps, ci=ci, bcv=bcv):
                    sc.wait_ge(se.h, bcv)
                    sc.copy(dst[:, ci * CH:(ci + 1) * CH], ps[:, :CH]).then_inc(se.h, 2)
                se.v += 2
                psfree[ci] = se.v
                c.on("scalar", fev0)
                continue

            prev_bcrd[0] = se.v + 2
            psfree[ci] = se.v + 2
            def fvx(vv, ps=ps, ci=ci, bcv=bcv, ra=se.v + 1, e2=se.v + 2, e3=se.v + 3, e4=se.v + 4):
                vv.wait_ge(se.h, bcv)
                vv.tensor_copy(rb_sb[:, :CH], ps_b1[:, :CH]).then_inc(se.h, 1)
                vv.wait_ge(se.h, ra)
                vv.tensor_tensor(qn32[:, :CH], ps[:, :CH], rb_sb[:, :CH], ALU.mult).then_inc(se.h, 1)
                vv.wait_ge(se.h, e2)
                vv.tensor_scalar_mul(qn32[:, :CH], qn32[:, :CH], scol)
                vv.tensor_copy(rot2[0:32, :], qn32[32:64, :CH])
                vv.tensor_copy(rot2[32:64, :], qn32[0:32, :CH])
                vv.tensor_copy(rot2[64:96, :], qn32[96:128, :CH])
                vv.tensor_copy(rot2[96:128, :], qn32[64:96, :CH]).then_inc(se.h, 1)
                vv.wait_ge(se.h, e3)
                vv.tensor_tensor(rot2[:, :], rot2[:, :], sinT[:, ci * CH:(ci + 1) * CH], ALU.mult)
                vv.tensor_tensor(qn32[:, :CH], qn32[:, :CH], cosT[:, ci * CH:(ci + 1) * CH], ALU.mult).then_inc(se.h, 1)
                vv.wait_ge(se.h, e4)
                if isq:
                    vv.tensor_tensor(qn32[:, :CH], qn32[:, :CH], rot2[:, :], ALU.add).then_inc(se.h, 1)
                    vv.wait_ge(se.h, e4 + 1)
                    vv.tensor_tensor(dst[:, ci * CH:(ci + 1) * CH], qn32[:, :CH],
                                     etab[:, ci * CH:(ci + 1) * CH], ALU.mult).then_inc(se.h, 1)
                else:
                    vv.tensor_tensor(dst[:, ci * CH:(ci + 1) * CH], qn32[:, :CH],
                                     rot2[:, :], ALU.add).then_inc(se.h, 2)
            se.v += 6
            c.on("vector", fvx)
        return se

    import os as _os3
    _nlyr = int(_os3.environ.get("L2F_LAYERS", "2"))
    _skip_att = _os3.environ.get("L2F_SKIP_ATT", "0") == "1"
    _skip_ffn = _os3.environ.get("L2F_SKIP_FFN", "0") == "1"
    for lyr in range(_nlyr):
        swq = c.sem("l2fwq")
        if _os3.environ.get("L2F_NOSHIFT", "0") == "1":
            pass
        else:
            shift_emit("sh%d" % lyr)

        sp = c.sem("l2fpp")
        spe = c.sem("l2fpe")
        _npj = 0 if _skip_att else int(_os3.environ.get("L2F_PAIRS", "4"))
        _skip_qk = _os3.environ.get("L2F_SKIP_QK", "0") == "1"
        _skip_loc = _os3.environ.get("L2F_SKIP_LOC", "0") == "1"
        _nofwp = _os3.environ.get("L2F_NOFWP", "0") == "1"
        for pj in range(_npj):
            wp_gate = sp.v

            def fwp(g, lyr=lyr, pj=pj, wp_gate=wp_gate):
                if wp_gate:
                    g.wait_ge(sp.h, wp_gate)
                srcw = wd["l2f_wp_d"][lyr, pj].rearrange("(k p) o -> p k o", p=128)
                g.dma_start(wpair[:, :].rearrange("p (k o) -> p k o", o=384),
                            srcw).then_inc(swq.h, 16)
            swq.v += 16
            c.on("gpsimd", fwp)
            if _nofwp:
                c.flush()
            wpv = swq.v
            for ci in range(NCH):
                ps = (psA, psB)[ci % 2]
                eneed = spe.v - 1

                def fmm(t, ci=ci, ps=ps, pj=pj, eneed=eneed, wpv=wpv):
                    t.wait_ge(swq.h, wpv)
                    if eneed > 0:
                        t.wait_ge(spe.h, eneed)
                    last = None
                    for k in range(KT):
                        last = t.matmul(ps[:, :CH],
                                        wpair[:, k * 384 + 256:k * 384 + 384],
                                        shslice(k, ci * CH, CH),
                                        start=(k == 0), stop=(k == KT - 1))
                    last.then_inc(sp.h, 1)
                sp.v += 1
                c.on("tensor", fmm)
                mmv = sp.v

                def fev(sc, ci=ci, ps=ps, mmv=mmv):
                    sc.wait_ge(sp.h, mmv)
                    sc.copy(v_sb[:, ci * CH:(ci + 1) * CH], ps[:, :CH]).then_inc(spe.h, 1)
                spe.v += 1
                c.on("scalar", fev)
            str_ = c.sem("l2ftr")
            ste = c.sem("l2fte")
            evfin = spe.v
            for g_ in range(4):
                teneed = ste.v

                def ftr(t, g_=g_, evfin=evfin, teneed=teneed):
                    t.wait_ge(spe.h, evfin)
                    if teneed > 0:
                        t.wait_ge(ste.h, teneed)
                    last = None
                    for ii in range(4):
                        i = g_ * 4 + ii
                        last = t.transpose(ps_tr[:, ii * 128:(ii + 1) * 128],
                                           v_sb[:, i * 128:(i + 1) * 128], c.t["idn"][:, :])
                    last.then_inc(str_.h, 1)
                str_.v += 1
                c.on("tensor", ftr)
                trv = str_.v

                def fte(vv, g_=g_, trv=trv):
                    vv.wait_ge(str_.h, trv)
                    vv.tensor_copy(v_tm[:, g_ * 512:(g_ + 1) * 512], ps_tr[:, :]).then_inc(ste.h, 1)
                ste.v += 1
                c.on("vector", fte)
            c.flush()
            _qkl = {"0": [], "q": [True], "k": [False], "qk": [True, False]}[_os3.environ.get("L2F_QKMODE", "qk") if not _skip_qk else "0"]
            for isq in _qkl:
                boff2 = 0 if isq else 128

                def emit_mm(ci, ps, psfree_ev, boff2=boff2, pj=pj, wpv=swq.v):
                    def fmm(t, ci=ci, ps=ps, psfree_ev=psfree_ev, boff2=boff2, pj=pj, wpv=wpv):
                        t.wait_ge(swq.h, wpv)
                        if psfree_ev:
                            t.wait_ge(c.sem("l2fqe").h, psfree_ev)
                        last = None
                        for k in range(KT):
                            last = t.matmul(ps[:, :CH],
                                            wpair[:, k * 384 + boff2:k * 384 + boff2 + 128],
                                            shslice(k, ci * CH, CH),
                                            start=(k == 0), stop=(k == KT - 1))
                        last.then_inc(sp.h, 1)
                    sp.v += 1
                    c.on("tensor", fmm)
                    return sp.v
                lnorm_rope(pj, isq, lyr, sp, emit_mm)
                c.flush()
            if _os3.environ.get("L2F_HALT", "") == "afterqk":
                break
            for hh in range(0 if (_skip_loc or _skip_qk) else 2):
                h = 2 * pj + hh
                base = hh * 64
                ssc = c.sem("l2fsc")
                sse = c.sem("l2fse")
                for wi in range(4):
                    for qt in range(4):
                        kcs = ([wi - 1, wi] if wi > 0 else [wi])
                        nkt0 = (wi - 1 if wi > 0 else wi) * 4
                        nloc = len(kcs) * 4
                        for li, kc in enumerate(kcs):
                            ps = (psA, psB)[li % 2]
                            eneed = sse.v - 1

                            def fsc(t, wi=wi, qt=qt, kc=kc, ps=ps, eneed=eneed, base=base):
                                if eneed > 0:
                                    t.wait_ge(sse.h, eneed)
                                t.matmul(ps[:, :CH],
                                         q_sb[base:base + 64, wi * CH + qt * 128:wi * CH + (qt + 1) * 128],
                                         k_sb[base:base + 64, kc * CH:(kc + 1) * CH],
                                         start=True, stop=True).then_inc(ssc.h, 1)
                            ssc.v += 1
                            c.on("tensor", fsc)
                            scv = ssc.v
                            cur = (kc == wi)

                            def fse(vv, li=li, qt=qt, ps=ps, scv=scv, cur=cur):
                                vv.wait_ge(ssc.h, scv)
                                dst2 = sqb[:, li * CH:(li + 1) * CH]
                                msk = cmask if cur else pmask
                                vv.tensor_tensor(dst2, ps[:, :CH],
                                                 msk[:, qt * CH:(qt + 1) * CH], ALU.add).then_inc(sse.h, 1)
                            sse.v += 1
                            c.on("vector", fse)
                        nk = nloc * 128
                        sev = sse.v

                        def fsm1(vv, nk=nk, sev=sev, e1=sse.v + 1):
                            vv.wait_ge(sse.h, sev)
                            vv.memset(mrow[:, 2:3], 0.0)
                            vv.reduce_max(mrow[:, 0:1], sqb[:, :nk], axis=AX.X).then_inc(sse.h, 1)
                            vv.wait_ge(sse.h, e1)
                            vv.tensor_scalar_mul(mrow[:, 1:2], mrow[:, 0:1], -1.0).then_inc(sse.h, 1)
                        sse.v += 2
                        c.on("vector", fsm1)
                        sev2 = sse.v

                        def fsm2(sc, nk=nk, sev2=sev2):
                            sc.wait_ge(sse.h, sev2)
                            sc.activation(p_sb[:, :nk], sqb[:, :nk], AF.Exp,
                                          bias=mrow[:, 1:2], scale=1.0,
                                          accum_out=mrow[:, 2:3]).then_inc(sse.h, 1)
                        sse.v += 1
                        c.on("scalar", fsm2)
                        sev3 = sse.v

                        def fsm3(vv, sev3=sev3, nk=nk, ra=sse.v + 1):
                            vv.wait_ge(sse.h, sev3)
                            vv.reciprocal(mrow[:, 3:4], mrow[:, 2:3]).then_inc(sse.h, 1)
                            vv.wait_ge(sse.h, ra)
                            vv.tensor_scalar_mul(p_sb[:, :nk], p_sb[:, :nk],
                                                 mrow[:, 3:4]).then_inc(sse.h, 1)
                        sse.v += 2
                        c.on("vector", fsm3)
                        stp = c.sem("l2ftp")
                        stq = c.sem("l2ftq")
                        pscale = sse.v
                        ngrp = nloc // 4
                        for g_ in range(ngrp):
                            tqneed = stq.v

                            def ftr(t, g_=g_, pscale=pscale, tqneed=tqneed):
                                t.wait_ge(sse.h, pscale)
                                if tqneed > 0:
                                    t.wait_ge(stq.h, tqneed)
                                last = None
                                for ii in range(4):
                                    i = g_ * 4 + ii
                                    last = t.transpose(ps_tr[:, ii * 128:(ii + 1) * 128],
                                                       p_sb[:, i * 128:(i + 1) * 128], c.t["idn"][:, :])
                                last.then_inc(stp.h, 1)
                            stp.v += 1
                            c.on("tensor", ftr)
                            tpv = stp.v

                            def fte(vv, g_=g_, tpv=tpv):
                                vv.wait_ge(stp.h, tpv)
                                vv.tensor_copy(pT_sb[:, g_ * 512:(g_ + 1) * 512],
                                               ps_tr[:, :]).then_inc(stq.h, 1)
                            stq.v += 1
                            c.on("vector", fte)
                        saz = c.sem("l2faz")
                        tqfin = stq.v
                        azprev = saz.v

                        def fav(t, wi=wi, nloc=nloc, nkt0=nkt0, hh=hh, tqfin=tqfin, azprev=azprev):
                            t.wait_ge(stq.h, tqfin)
                            if azprev:
                                t.wait_ge(saz.h, azprev)
                            last = None
                            for i in range(nloc):
                                akt = nkt0 + i
                                last = t.matmul(ps_z[:, :],
                                                v_tm[:, akt * 128 + hh * 64:akt * 128 + hh * 64 + 64],
                                                pT_sb[:, i * 128:(i + 1) * 128],
                                                start=(i == 0), stop=(i == nloc - 1))
                            last.then_inc(saz.h, 1)
                        saz.v += 1
                        c.on("tensor", fav)
                        azv = saz.v

                        def fze(vv, wi=wi, qt=qt, h=h, azv=azv):
                            vv.wait_ge(saz.h, azv)
                            vv.tensor_copy(zl[(h % 2) * 64:(h % 2) * 64 + 64,
                                              (h // 2) * N + wi * CH + qt * 128:(h // 2) * N + wi * CH + (qt + 1) * 128],
                                           ps_z[:, :]).then_inc(saz.h, 1)
                        saz.v += 1
                        c.on("vector", fze)
                c.flush()
        if _skip_att:
            pass
        if _os3.environ.get("L2F_HALT", "") == "afterqk":
            break
        swo = c.sem("l2fwo")
        wo_gate = sp.v

        def fwo(g, lyr=lyr, wo_gate=wo_gate):
            g.wait_ge(sp.h, wo_gate)
            src2 = wd["l2f_wo_d"][lyr].rearrange("(k p) o -> p k o", p=128)
            g.dma_start(wo2[:, :].rearrange("p (k o) -> p k o", o=D), src2).then_inc(swo.h, 16)
        swo.v += 16
        c.on("gpsimd", fwo)
        c.dma_barrier(swo)
        smm = c.sem("l2fom")
        svv = c.sem("l2fov")
        for ci in range(0 if _skip_att else NCH):
            for of in range(KT):
                i = ci * KT + of
                ps = (psA, psB)[i % 2]
                vneed = svv.v - 1

                def fmm(t, of=of, ci=ci, ps=ps, vneed=vneed):
                    if vneed > 0:
                        t.wait_ge(svv.h, vneed)
                    last = None
                    for k in range(4):
                        last = t.matmul(ps[:, :CH], wo2[:, k * D + of * 128:k * D + (of + 1) * 128],
                                        zl[:, k * N + ci * CH:k * N + ci * CH + CH],
                                        start=(k == 0), stop=(k == 3))
                    last.then_inc(smm.h, 1)
                smm.v += 1
                c.on("tensor", fmm)
                mmv = smm.v
                slot = (tmp[:, :CH], sqb[:, :CH])[i % 2]

                def fs(sc, ps=ps, mmv=mmv, vprev=svv.v, slot=slot):
                    sc.wait_ge(smm.h, mmv)
                    if vprev > 1:
                        sc.wait_ge(svv.h, vprev - 1)
                    sc.copy(slot, ps[:, :CH]).then_inc(smm.h, 1)
                smm.v += 1
                c.on("scalar", fs)
                scv = smm.v

                def fv(vv, of=of, ci=ci, scv=scv, slot=slot):
                    vv.wait_ge(smm.h, scv)
                    sl = s[:, of * N + ci * CH:of * N + ci * CH + CH]
                    vv.tensor_tensor(sl, sl, slot, ALU.add).then_inc(svv.h, 1)
                svv.v += 1
                c.on("vector", fv)
        c.flush()

        if _skip_ffn:
            continue
        shift_emit("sh2_%d" % lyr)
        sff = c.sem("l2fff")
        sfe = c.sem("l2ffe")
        sga = c.sem("l2fga")
        sw1 = c.sem("l2fw1")
        sw2 = c.sem("l2fw2")
        for ci in range(NCH):
            lnh = ubuf[:, 0:4096]

            def fsq(sc, ci=ci, mmprev=sff.v):
                if mmprev:
                    sc.wait_ge(sff.h, mmprev)
                last = None
                for k in range(4):
                    last = sc.activation(sqb[:, k * CH:(k + 1) * CH],
                                         shslice(k, ci * CH, CH), AF.Square)
                last.then_inc(sff.h, 1)
            sff.v += 1
            c.on("scalar", fsq)
            sqv = sff.v

            def fmm(t, ci=ci, sqv=sqv):
                t.wait_ge(sff.h, sqv)
                for k in range(KT):
                    t.matmul(ps_s[0:1, :CH], c.t["ones_b"][:, :], shslice(k, ci * CH, CH),
                             start=(k == 0), stop=(k == KT - 1))
                last = None
                for k in range(4):
                    last = t.matmul(ps_q[0:1, :CH], c.t["ones_f"][:, :], sqb[:, k * CH:(k + 1) * CH],
                                    start=(k == 0), stop=False)
                last.then_inc(sff.h, 1)
            sff.v += 1
            c.on("tensor", fmm)
            mv1 = sff.v

            def fsq2(sc, ci=ci, mv1=mv1):
                sc.wait_ge(sff.h, mv1)
                last = None
                for k in range(4, KT):
                    last = sc.activation(sqb[:, (k - 4) * CH:(k - 3) * CH],
                                         shslice(k, ci * CH, CH), AF.Square)
                last.then_inc(sff.h, 1)
            sff.v += 1
            c.on("scalar", fsq2)
            sqv2 = sff.v

            def fmm2b(t, sqv2=sqv2):
                t.wait_ge(sff.h, sqv2)
                last = None
                for k in range(4, KT):
                    last = t.matmul(ps_q[0:1, :CH], c.t["ones_f"][:, :],
                                    sqb[:, (k - 4) * CH:(k - 3) * CH],
                                    start=False, stop=(k == KT - 1))
                last.then_inc(sff.h, 1)
            sff.v += 1
            c.on("tensor", fmm2b)
            mmv = sff.v

            def frow(vv, mmv=mmv):
                vv.wait_ge(sff.h, mmv)
                vv.tensor_scalar_mul(strow[0:1, 0:CH], ps_s[0:1, :CH], 1.0 / D)
                vv.tensor_tensor(strow[0:1, CH:2 * CH], strow[0:1, 0:CH], strow[0:1, 0:CH], ALU.mult)
                vv.scalar_tensor_tensor(strow[0:1, CH:2 * CH], ps_q[0:1, :CH], 1.0 / D,
                                        strow[0:1, CH:2 * CH], ALU.mult, ALU.subtract).then_inc(sff.h, 1)
            sff.v += 1
            c.on("vector", frow)
            rwv = sff.v

            def fsqr(sc, rwv=rwv):
                sc.wait_ge(sff.h, rwv)
                sc.activation(strow[0:1, CH:2 * CH], strow[0:1, CH:2 * CH], AF.Sqrt,
                              bias=1e-5, scale=1.0).then_inc(sff.h, 1)
            sff.v += 1
            c.on("scalar", fsqr)
            sqv2 = sff.v

            def frow2(vv, sqv2=sqv2, ra=sff.v + 1):
                vv.wait_ge(sff.h, sqv2)
                vv.reciprocal(strow[0:1, 2 * CH:3 * CH], strow[0:1, CH:2 * CH]).then_inc(sff.h, 1)
                vv.wait_ge(sff.h, ra)
                vv.scalar_tensor_tensor(strow[0:1, 3 * CH:4 * CH], strow[0:1, 0:CH], -1.0,
                                        strow[0:1, 2 * CH:3 * CH], ALU.mult, ALU.mult).then_inc(sff.h, 1)
            sff.v += 2
            c.on("vector", frow2)
            rw2 = sff.v

            def fbc(t, rw2=rw2):
                t.wait_ge(sff.h, rw2)
                t.matmul(ps_b1[:, :CH], c.t["onesr_f"][0:1, :], strow[0:1, 2 * CH:3 * CH],
                         start=True, stop=True)
                t.matmul(ps_b2[:, :CH], c.t["onesr_f"][0:1, :], strow[0:1, 3 * CH:4 * CH],
                         start=True, stop=True).then_inc(sff.h, 1)
            sff.v += 1
            c.on("tensor", fbc)
            bcv = sff.v

            def fnorm(vv, ci=ci, bcv=bcv, lyr=lyr, base=sff.v):
                vv.wait_ge(sff.h, bcv)
                vv.tensor_copy(rb_sb[:, :CH], ps_b1[:, :CH])
                vv.tensor_copy(qn32[:, :CH], ps_b2[:, :CH]).then_inc(sff.h, 1)
                vv.wait_ge(sff.h, base + 1)
                ev = base + 1
                for k in range(KT):
                    sl_ = (tmp[:, :CH], rot2 := sqb[:, 0:CH])[k % 2] if False else tmp[:, :CH]
                    vv.tensor_tensor(sl_, shslice(k, ci * CH, CH), rb_sb[:, :CH], ALU.mult).then_inc(sff.h, 1)
                    ev += 1
                    vv.wait_ge(sff.h, ev)
                    vv.tensor_tensor(sl_, sl_, qn32[:, :CH], ALU.add).then_inc(sff.h, 1)
                    ev += 1
                    vv.wait_ge(sff.h, ev)
                    vv.tensor_scalar(lnh[:, k * CH:(k + 1) * CH], sl_,
                                     lgp[:, lyr * 8 + k:lyr * 8 + k + 1],
                                     lbp[:, lyr * 8 + k:lyr * 8 + k + 1],
                                     ALU.mult, ALU.add).then_inc(sff.h, 1)
                    ev += 1
                    if k < KT - 1:
                        vv.wait_ge(sff.h, ev)
            sff.v += 1 + 3 * KT
            c.on("vector", fnorm)
            lnv = sff.v

            act = ubuf[:, 4096:4096 + 22 * CH]
            if ci == 0 and lyr == 0:
                c._l2f_m1v = {}
                c._l2f_acv = {}
                c._l2f_s2v = {}
            for of1 in range(22):
                slab = 0
                w1gate = c._l2f_m1v.get(("m1", lyr, ci, of1 - 1)) or c._l2f_m1v.get(("m1p",))

                def fw1(g, of1=of1, slab=slab, lyr=lyr, w1gate=w1gate):
                    if w1gate:
                        g.wait_ge(sff.h, w1gate)
                    src = wd["l2f_w1p_d"][lyr].rearrange("(k p) o -> p k o", p=128)
                    g.dma_start(w1s[:, slab:slab + 1024].rearrange("p (k o) -> p k o", o=128),
                                src[:, :, of1 * 128:(of1 + 1) * 128]).then_inc(sw1.h, 16)
                    g.dma_start(w1s[:, slab + 1024:slab + 2048].rearrange("p (k o) -> p k o", o=128),
                                src[:, :, 2816 + of1 * 128:2816 + (of1 + 1) * 128]).then_inc(sw1.h, 16)
                sw1.v += 32
                c.on("gpsimd", fw1)
                w1v = sw1.v
                psgate = c._l2f_acv.get(("ac", lyr, ci, of1 - 1)) or c._l2f_acv.get(("acp",))

                def fmm1(t, of1=of1, slab=slab, w1v=w1v, lnv=lnv, psgate=psgate):
                    t.wait_ge(sw1.h, w1v)
                    t.wait_ge(sff.h, lnv)
                    if psgate:
                        t.wait_ge(sff.h, psgate)
                    for k in range(KT):
                        t.matmul(psA[:, :CH], w1s[:, slab + k * 128:slab + (k + 1) * 128],
                                 lnh[:, k * CH:(k + 1) * CH], start=(k == 0), stop=(k == KT - 1))
                    last = None
                    for k in range(KT):
                        last = t.matmul(psB[:, :CH], w1s[:, slab + 1024 + k * 128:slab + 1024 + (k + 1) * 128],
                                        lnh[:, k * CH:(k + 1) * CH], start=(k == 0), stop=(k == KT - 1))
                    last.then_inc(sff.h, 1)
                sff.v += 1
                c.on("tensor", fmm1)
                m1v = sff.v
                c._l2f_m1v[("m1", lyr, ci, of1)] = m1v
                scr_gate = sga.v

                def fact(sc, of1=of1, lyr=lyr, m1v=m1v, scr_gate=scr_gate):
                    sc.wait_ge(sff.h, m1v)
                    if scr_gate:
                        sc.wait_ge(sga.h, scr_gate)
                    sc.activation(tmp[:, :CH], psB[:, :CH], AF.Gelu,
                                  bias=b1p[:, lyr * 44 + 22 + of1:lyr * 44 + 22 + of1 + 1],
                                  scale=1.0)
                    sc.activation(qn32[:, :CH], psA[:, :CH], AF.Identity,
                                  bias=b1p[:, lyr * 44 + of1:lyr * 44 + of1 + 1],
                                  scale=1.0).then_inc(sff.h, 1)
                sff.v += 1
                c.on("scalar", fact)
                acv = sff.v
                c._l2f_acv[("ac", lyr, ci, of1)] = acv

                def fmul(vv, of1=of1, acv=acv):
                    vv.wait_ge(sff.h, acv)
                    vv.tensor_tensor(act[:, of1 * CH:(of1 + 1) * CH], qn32[:, :CH],
                                     tmp[:, :CH], ALU.mult).then_inc(sga.h, 1)
                sga.v += 1
                c.on("vector", fmul)
            c._l2f_m1v[("m1p",)] = c._l2f_m1v[("m1", lyr, ci, 21)]
            c._l2f_acv[("acp",)] = c._l2f_acv[("ac", lyr, ci, 21)]
            gfin = sga.v
            for of2 in range(KT):
                slab2 = 0
                w2gate = c._l2f_s2v.get(("s2", lyr, ci, of2 - 1)) or c._l2f_s2v.get(("s2p",))

                def fw2(g, of2=of2, slab2=slab2, lyr=lyr, w2gate=w2gate):
                    if w2gate:
                        g.wait_ge(sff.h, w2gate)
                    src = wd["l2f_w2p_d"][lyr].rearrange("(k p) o -> p k o", p=128)
                    g.dma_start(w2s[:, slab2:slab2 + 2816].rearrange("p (k o) -> p k o", o=128),
                                src[:, :, of2 * 128:(of2 + 1) * 128]).then_inc(sw2.h, 16)
                sw2.v += 16
                c.on("gpsimd", fw2)
                w2v = sw2.v
                psgate2 = c._l2f_s2v.get(("s2", lyr, ci, of2 - 1)) or c._l2f_s2v.get(("s2p",))

                def fmm2(t, of2=of2, slab2=slab2, w2v=w2v, gfin=gfin, psgate2=psgate2):
                    t.wait_ge(sw2.h, w2v)
                    t.wait_ge(sga.h, gfin)
                    if psgate2:
                        t.wait_ge(sff.h, psgate2)
                    last = None
                    for k2 in range(22):
                        last = t.matmul(psA[:, :CH] if of2 % 2 == 0 else psB[:, :CH],
                                        w2s[:, slab2 + k2 * 128:slab2 + (k2 + 1) * 128],
                                        act[:, k2 * CH:(k2 + 1) * CH],
                                        start=(k2 == 0), stop=(k2 == 21))
                    last.then_inc(sff.h, 1)
                sff.v += 1
                c.on("tensor", fmm2)
                m2v = sff.v
                slot = (tmp[:, :CH], qn32[:, :CH])[of2 % 2]
                scr2 = sga.v - 1

                def fs2(sc, of2=of2, m2v=m2v, lyr=lyr, slot=slot, scr2=scr2):
                    sc.wait_ge(sff.h, m2v)
                    if scr2 > gfin:
                        sc.wait_ge(sga.h, scr2)
                    sc.activation(slot, (psA if of2 % 2 == 0 else psB)[:, :CH], AF.Identity,
                                  bias=b2p[:, lyr * 8 + of2:lyr * 8 + of2 + 1],
                                  scale=1.0).then_inc(sff.h, 1)
                sff.v += 1
                c.on("scalar", fs2)
                s2v = sff.v
                c._l2f_s2v[("s2", lyr, ci, of2)] = s2v

                def fv2(vv, of2=of2, ci=ci, s2v=s2v, slot=slot):
                    vv.wait_ge(sff.h, s2v)
                    sl = s[:, of2 * N + ci * CH:of2 * N + ci * CH + CH]
                    vv.tensor_tensor(sl, sl, slot, ALU.add).then_inc(sga.h, 1)
                sga.v += 1
                c.on("vector", fv2)
            c._l2f_s2v[("s2p",)] = c._l2f_s2v[("s2", lyr, ci, 7)]
            c.flush()

    sfin = c.sem("l2ffin")
    if _os9.environ.get("L2F_NOFINAL", "0") == "1":
        sout_ = c.sem("sout")

        def fst(sy):
            sy.dma_start(x_out.rearrange("k p t -> p k t"),
                         s[:, :].rearrange("p (k t) -> p k t", t=N)).then_inc(sout_.h, 16)
        sout_.v += 16
        c.on("sync", fst)
        c.flush()
        return
    for ci in range(NCH):
        sld, ldv = load_x_chunk(c, x_in, ci, tag="l2fflx%d" % ci)

        def fadd(vv, ci=ci, ldv=ldv):
            vv.wait_ge(c.sem("l2fflx%d" % ci).h, ldv)
            last = None
            for k in range(KT):
                sl = c.t["xc"][:, k * CH:(k + 1) * CH]
                last = vv.tensor_tensor(sl, sl, s[:, k * N + ci * CH:k * N + ci * CH + CH], ALU.add)
            last.then_inc(sfin.h, 1)
        sfin.v += 1
        c.on("vector", fadd)
        store_x_chunk(c, x_out, ci, sfin, sfin.v)
        c.flush()


STAGE_DRAM["l2f"] = [("l2f_wp_d", [2, 4, D, 384]), ("l2f_wo_d", [2, 512, D]),
                     ("l2f_w1p_d", [2, D, 5632]), ("l2f_w2p_d", [2, 2816, D]),
                     ("l2f_b1p_d", [128, 88]), ("l2f_b2_d", [128, 16]),
                     ("l2f_lg_d", [128, 16]), ("l2f_lb_d", [128, 16]),
                     ("l2f_qs_d", [128, 2]), ("l2f_ks_d", [128, 2]),
                     ("l2f_cosk_d", [128, N]), ("l2f_sink_d", [128, N]),
                     ("l2f_etab_d", [128, N]),
                     ("l2f_pmask_d", [128, 4 * CH]), ("cmask_d", [128, 4 * CH])]


def _stage_l2f(inp):
    DHD, WIN = 64, 512
    half_idx = np.arange(0, DHD, 2, dtype=np.float32)
    inv_freq = 1.0 / 10000.0 ** (half_idx / DHD)
    tt = np.arange(N, dtype=np.float32)
    freqs = tt[:, None] * inv_freq[None, :]
    freqs = np.concatenate([freqs, freqs], -1)
    base = (half_idx + 0.4 * DHD) / (1.4 * DHD)
    power = (tt - N // 2) / float(WIN // 2)
    scale = base[None, :] ** power[:, None]
    scale = np.concatenate([scale, scale], -1)
    cos = np.cos(freqs); sin = np.sin(freqs)
    ck = (cos / scale).T.astype(np.float32)
    sk = (sin / scale).T.astype(np.float32)
    et = (8.0 * scale * scale).T.astype(np.float32)
    def fold(s_):
        return np.concatenate([-s_[0:32], s_[32:64]], 0)
    cosk = np.ascontiguousarray(np.concatenate([ck, ck], 0))
    sink = np.ascontiguousarray(np.concatenate([fold(sk), fold(sk)], 0))
    etab = np.ascontiguousarray(np.concatenate([et, et], 0))
    pm = np.zeros((128, 4 * CH), np.float32)
    p = np.arange(128)[:, None]
    f = np.arange(CH)[None, :]
    for qt in range(4):
        pm[:, qt * CH:(qt + 1) * CH] = np.where(f >= qt * 128 + p, 0.0, -1e30)
    w1 = inp["L2_ffn_w1"]; w2 = inp["L2_ffn_w2"]; b1 = inp["L2_ffn_b1"]
    HID = 2730
    w1p = np.zeros((2, D, 5632), np.float32)
    w1p[:, :, :HID] = w1[:, :, :HID]
    w1p[:, :, 2816:2816 + HID] = w1[:, :, HID:]
    w2p = np.zeros((2, 2816, D), np.float32)
    w2p[:, :HID, :] = w2
    b1p = np.zeros((128, 88), np.float32)
    for l in range(2):
        a = np.zeros(2816, np.float32); a[:HID] = b1[l, :HID]
        g = np.zeros(2816, np.float32); g[:HID] = b1[l, HID:]
        b1p[:, l * 44:l * 44 + 22] = a.reshape(22, 128).T
        b1p[:, l * 44 + 22:l * 44 + 44] = g.reshape(22, 128).T
    def two(vname):
        v = inp[vname]
        return np.ascontiguousarray(np.stack([v[0].reshape(8, 128).T, v[1].reshape(8, 128).T], 1).reshape(128, 16))
    b2p = np.zeros((128, 16), np.float32)
    lgp = np.zeros((128, 16), np.float32)
    lbp = np.zeros((128, 16), np.float32)
    for l in range(2):
        b2p[:, l * 8:(l + 1) * 8] = inp["L2_ffn_b2"][l].reshape(8, 128).T
        lgp[:, l * 8:(l + 1) * 8] = inp["L2_ffn_ln_g"][l].reshape(8, 128).T
        lbp[:, l * 8:(l + 1) * 8] = inp["L2_ffn_ln_b"][l].reshape(8, 128).T
    qsp = np.zeros((128, 2), np.float32)
    ksp = np.zeros((128, 2), np.float32)
    for l in range(2):
        qsp[:, l] = np.tile(inp["L2_ffn_q_scale"][l], 2)
        ksp[:, l] = np.tile(inp["L2_ffn_k_scale"][l], 2)
    wqkv_in = inp["L2_ffn_wqkv"]
    wp = np.zeros((2, 4, D, 384), np.float32)
    for l in range(2):
        for pj in range(4):
            wp[l, pj, :, 0:128] = wqkv_in[l][:, pj * 128:(pj + 1) * 128]
            wp[l, pj, :, 128:256] = wqkv_in[l][:, 512 + pj * 128:512 + (pj + 1) * 128]
            wp[l, pj, :, 256:384] = wqkv_in[l][:, 1024 + pj * 128:1024 + (pj + 1) * 128]
    return {"l2f_wp_d": wp,
            "l2f_wo_d": np.ascontiguousarray(inp["L2_ffn_wo"]),
            "l2f_w1p_d": w1p, "l2f_w2p_d": w2p,
            "l2f_b1p_d": b1p, "l2f_b2_d": b2p, "l2f_lg_d": lgp, "l2f_lb_d": lbp,
            "l2f_qs_d": qsp, "l2f_ks_d": ksp,
            "l2f_cosk_d": cosk, "l2f_sink_d": sink, "l2f_etab_d": etab,
            "l2f_pmask_d": pm, "cmask_d": _causal_mask()}


_OLD_STAGE2 = stage_weights


def stage_weights(stage, inp):
    if stage == "l2f":
        return _stage_l2f(inp)
    return _OLD_STAGE2(stage, inp)


_OLD_BUILD2 = build_stage


def build_stage(stage):
    EMITTERS.setdefault("l2f", emit_l2f)
    return _OLD_BUILD2(stage)


def emit_l2a(c: Ctx, x_in, x_out, wd):
    DI, DS = 2048, 64
    DIT = DI // 128
    xr = c.sb("xr", [128, KT * CH], BF16)
    winslab = c.sb("winslab", [128, KT * 128], BF16)
    xm_sb = c.sb("xm_sb", [128, DIT * (CH + 3)], BF16)
    z_sb2 = c.sb("z_sb2", [128, DIT * CH], BF16)
    u_bf = c.sb("u_bf", [128, DIT * CH], BF16)
    dl32 = c.sb("dl32", [128, DIT * CH], BF16)
    w32 = c.sb("w32", [128, DIT * CH], BF16)
    y32 = c.sb("y32", [128, DIT * CH], F32)
    wxT = c.sb("wxT", [128, DIT * 192], BF16)
    wdtT = c.sb("wdtT", [64, 2048], BF16)
    wouts = c.sb("wouts", [128, DIT * 128], BF16)
    a32 = c.sb("a32", [128, DIT * 64], F32)
    hcar = c.sb("hcar", [128, DIT * 64], F32)
    convw = c.sb("convw", [128, DIT * 4], F32)
    convb = c.sb("convb", [128, DIT], F32)
    bdt = c.sb("bdt", [128, DIT], F32)
    dpcol = c.sb("dpcol", [128, DIT], F32)
    nwcol = c.sb("nwcol", [128, KT], F32)
    dtr_b = c.sb("dtr_b", [64, CH], BF16)
    brow_s = c.sb("brow_s", [1, CH], BF16)
    crow_s = c.sb("crow_s", [1, CH], BF16)
    bb_n = c.sb("bb_n", [128, CH], BF16)
    cc_n = c.sb("cc_n", [128, CH], BF16)
    scr = c.sb("scr", [128, CH], F32)
    ps_w = c.ps("ps_w", [128, CH])
    psA, psB = c.t["mm_psA"], c.t["mm_psB"]
    ps_s, ps_b1 = c.t["ps_s"], c.t["ps_b1"]
    tmp = c.t["tmp"]
    sqb = c.t["sq"]
    xc = c.t["xc"]
    xn32 = w32

    sld = c.sem("l2aw")

    def fw(g):
        g.dma_start(wxT[:, :].rearrange("p (k o) -> p k o", o=192),
                    wd["l2a_wxT_d"].rearrange("(k p) o -> p k o", p=128)).then_inc(sld.h, 16)
        g.dma_start(wdtT[:, :], wd["l2a_wdtT_d"]).then_inc(sld.h, 16)
        g.dma_start(a32[:, :], wd["l2a_alog_d"]).then_inc(sld.h, 16)
        g.dma_start(convw[:, :], wd["l2a_convw_d"]).then_inc(sld.h, 16)
        g.dma_start(convb[:, :], wd["l2a_convb_d"]).then_inc(sld.h, 16)
        g.dma_start(bdt[:, :], wd["l2a_bdt_d"]).then_inc(sld.h, 16)
        g.dma_start(dpcol[:, :], wd["l2a_dp_d"]).then_inc(sld.h, 16)
        g.dma_start(nwcol[:, :], wd["l2a_nw_d"]).then_inc(sld.h, 16)
    sld.v += 128
    c.on("gpsimd", fw)
    c.dma_barrier(sld)

    def fsetup(v):
        v.memset(hcar[:, :], 0.0)
        v.memset(xm_sb[:, :], 0.0)
        v.memset(y32[:, :], 0.0)
    c.on("vector", fsetup)

    def fa(s):
        s.activation(a32[:, :], a32[:, :], AF.Exp)
    c.on("scalar", fa)
    c.flush()

    def fa2(v):
        v.tensor_scalar_mul(a32[:, :], a32[:, :], -1.0)
    c.on("vector", fa2)
    c.flush()

    se = c.sem("l2ase")
    sw = c.sem("l2asw")
    for ci in range(NCH):
        sldx, ldv = load_x_chunk(c, x_in, ci, tag="l2alx%d" % ci)
        emit_ln_chunk(c, "l2aln", ci, xn32, wait=(sldx, ldv), xn_ktile_stride=CH)
        c.flush()

        def fr1(s, base=se.v):
            last = None
            for k in range(KT):
                last = s.activation(sqb[:, k * CH:(k + 1) * CH],
                                    xn32[:, k * CH:(k + 1) * CH], AF.Square)
            last.then_inc(se.h, 1)
        se.v += 1
        c.on("scalar", fr1)
        e1 = se.v

        def fr2(t, e1=e1):
            t.wait_ge(se.h, e1)
            last = None
            for k in range(KT):
                last = t.matmul(ps_s[0:1, :CH], c.t["ones_f"][:, :], sqb[:, k * CH:(k + 1) * CH],
                                start=(k == 0), stop=(k == KT - 1))
            last.then_inc(se.h, 1)
        se.v += 1
        c.on("tensor", fr2)
        e2 = se.v

        def fr3(s, e2=e2):
            s.wait_ge(se.h, e2)
            s.activation(c.t["strow"][0:1, 0:CH], ps_s[0:1, :CH], AF.Sqrt,
                         bias=float(np.finfo(np.float32).eps), scale=1.0 / D).then_inc(se.h, 1)
        se.v += 1
        c.on("scalar", fr3)
        e3 = se.v

        def fr4(v, e3=e3, r1=se.v + 1):
            v.wait_ge(se.h, e3)
            v.reciprocal(c.t["strow"][0:1, CH:2 * CH], c.t["strow"][0:1, 0:CH]).then_inc(se.h, 1)
        se.v += 1
        c.on("vector", fr4)
        e4 = se.v

        def fr5(t, e4=e4):
            t.wait_ge(se.h, e4)
            t.matmul(ps_b1[:, :CH], c.t["onesr_f"][0:1, :], c.t["strow"][0:1, CH:2 * CH],
                     start=True, stop=True).then_inc(se.h, 1)
        se.v += 1
        c.on("tensor", fr5)
        e5 = se.v

        def fr6(v, ci=ci, e5=e5, r1=se.v + 1):
            v.wait_ge(se.h, e5)
            v.tensor_copy(c.t["rstd_b"][:, :CH], ps_b1[:, :CH]).then_inc(se.h, 1)
            v.wait_ge(se.h, r1)
            for k in range(KT):
                v.tensor_tensor(tmp[:, :CH], xn32[:, k * CH:(k + 1) * CH],
                                c.t["rstd_b"][:, :CH], ALU.mult).then_inc(se.h, 1)
                v.wait_ge(se.h, r1 + 1 + 2 * k)
                v.tensor_scalar_mul(xr[:, k * CH:(k + 1) * CH], tmp[:, :CH],
                                    nwcol[:, k:k + 1]).then_inc(se.h, 1)
                if k < KT - 1:
                    v.wait_ge(se.h, r1 + 2 + 2 * k)
        se.v += 1 + 2 * KT
        c.on("vector", fr6)
        xr_done = se.v

        for of in range(32):
            prevmm = se.v

            def fwl2(g, of=of, prevmm=prevmm):
                if prevmm:
                    g.wait_ge(se.h, prevmm)
                src2 = wd["l2a_winT_d"].rearrange("(k p) o -> p k o", p=128)
                g.dma_start(winslab[:, :].rearrange("p (k o) -> p k o", o=128),
                            src2[:, :, of * 128:(of + 1) * 128]).then_inc(sw.h, 16)
            sw.v += 16
            c.on("gpsimd", fwl2)
            wv = sw.v
            evgate = se.v - 1

            def fmm(t, of=of, wv=wv, xr_done=xr_done, evgate=evgate):
                t.wait_ge(sw.h, wv)
                t.wait_ge(se.h, xr_done)
                if evgate > xr_done:
                    t.wait_ge(se.h, evgate)
                ps = (psA, psB)[of % 2]
                last = None
                for k in range(KT):
                    last = t.matmul(ps[:, :CH], winslab[:, k * 128:(k + 1) * 128],
                                    xr[:, k * CH:(k + 1) * CH],
                                    start=(k == 0), stop=(k == KT - 1))
                last.then_inc(se.h, 1)
            se.v += 1
            c.on("tensor", fmm)
            mmv = se.v

            def fev(s, of=of, ci=ci, mmv=mmv):
                s.wait_ge(se.h, mmv)
                ps = (psA, psB)[of % 2]
                if of < DIT:
                    s.copy(xm_sb[:, of * (CH + 3) + 3:(of + 1) * (CH + 3)], ps[:, :CH]).then_inc(se.h, 1)
                else:
                    s.copy(z_sb2[:, (of - DIT) * CH:(of - DIT + 1) * CH], ps[:, :CH]).then_inc(se.h, 1)
            se.v += 1
            c.on("scalar", fev)
        xz_done = se.v
        c.flush()

        for k in range(DIT):
            base = k * (CH + 3)

            def fcv(v, k=k, base=base, e0=se.v):
                v.tensor_scalar_mul(scr[:, :CH], xm_sb[:, base:base + CH],
                                    convw[:, k * 4:k * 4 + 1]).then_inc(se.h, 1)
                for j in range(1, 4):
                    v.wait_ge(se.h, e0 + j)
                    v.scalar_tensor_tensor(scr[:, :CH], xm_sb[:, base + j:base + j + CH],
                                           convw[:, k * 4 + j:k * 4 + j + 1], scr[:, :CH],
                                           ALU.mult, ALU.add).then_inc(se.h, 1)
            se.v += 4
            c.on("vector", fcv)
            cvv = se.v

            def fsl(s, k=k, cvv=cvv):
                s.wait_ge(se.h, cvv)
                s.activation(scr[:, :CH], scr[:, :CH], AF.Silu,
                             bias=convb[:, k:k + 1], scale=1.0).then_inc(se.h, 1)
            se.v += 1
            c.on("scalar", fsl)
            slv = se.v

            def fub(v, k=k, slv=slv):
                v.wait_ge(se.h, slv)
                v.tensor_copy(u_bf[:, k * CH:(k + 1) * CH], scr[:, :CH]).then_inc(se.h, 1)
            se.v += 1
            c.on("vector", fub)
            ubv = se.v

            def fhl(v, k=k, base=base, ubv=ubv):
                v.wait_ge(se.h, ubv)
                v.tensor_copy(xm_sb[:, base:base + 3],
                              xm_sb[:, base + CH:base + CH + 3]).then_inc(se.h, 1)
            se.v += 1
            c.on("vector", fhl)
        u_done = se.v
        c.flush()

        def fxd(t, u_done=u_done):
            last = None
            for k in range(DIT):
                last = t.matmul(psA[:, :CH], wxT[:, k * 192:k * 192 + 128],
                                u_bf[:, k * CH:(k + 1) * CH], start=(k == 0), stop=(k == DIT - 1))
            for k in range(DIT):
                last = t.matmul(ps_w[0:64, :CH], wxT[:, k * 192 + 128:k * 192 + 192],
                                u_bf[:, k * CH:(k + 1) * CH], start=(k == 0), stop=(k == DIT - 1))
            last.then_inc(se.h, 1)
        se.v += 1
        c.on("tensor", fxd)
        xdv = se.v

        def fxe(s, xdv=xdv):
            s.wait_ge(se.h, xdv)
            s.copy(dtr_b[0:64, :CH], psA[0:64, :CH])
            s.copy(bb_n[64:128, :CH], psA[64:128, :CH])
            s.copy(bb_n[0:64, :CH], ps_w[0:64, :CH]).then_inc(se.h, 1)
        se.v += 1
        c.on("scalar", fxe)
        c.flush()

        spv_hist = {}
        for of in range(DIT):
            def fdm(t, of=of, gate=spv_hist.get(of - 2, 0)):
                if gate:
                    t.wait_ge(se.h, gate)
                t.matmul(psA[:, :CH] if of % 2 == 0 else psB[:, :CH],
                         wdtT[:, of * 128:(of + 1) * 128], dtr_b[:, :CH],
                         start=True, stop=True).then_inc(se.h, 1)
            se.v += 1
            c.on("tensor", fdm)
            dmv = se.v

            def fsp_(s, of=of, dmv=dmv, dlgate=spv_hist.get(("dl", of - 1), 0), eA=se.v + 1):
                s.wait_ge(se.h, dmv)
                if dlgate:
                    s.wait_ge(se.h, dlgate)
                ps = (psA, psB)[of % 2]
                s.activation(scr[:, :CH], ps[:, :CH], AF.Abs, bias=bdt[:, of:of + 1], scale=1.0)
                s.activation(tmp[:, :CH], ps[:, :CH], AF.Relu, bias=bdt[:, of:of + 1], scale=1.0)
                s.activation(scr[:, :CH], scr[:, :CH], AF.Exp, bias=0.0,
                             scale=-1.0).then_inc(se.h, 1)
                s.wait_ge(se.h, eA)
                s.activation(scr[:, :CH], scr[:, :CH], AF.Ln, bias=1.0,
                             scale=1.0).then_inc(se.h, 1)
            se.v += 2
            c.on("scalar", fsp_)
            spv = se.v
            spv_hist[of] = spv

            def fdl(v, of=of, spv=spv, e0=se.v + 1):
                v.wait_ge(se.h, spv)
                v.tensor_tensor(dl32[:, of * CH:(of + 1) * CH], scr[:, :CH], tmp[:, :CH],
                                ALU.add).then_inc(se.h, 1)
                v.wait_ge(se.h, e0)
                v.tensor_tensor(w32[:, of * CH:(of + 1) * CH], dl32[:, of * CH:(of + 1) * CH],
                                u_bf[:, of * CH:(of + 1) * CH], ALU.mult).then_inc(se.h, 1)
            se.v += 2
            spv_hist[("dl", of)] = se.v
            c.on("vector", fdl)
        delta_done = se.v
        c.flush()

        sy2 = c.sem("l2asy")
        for n in range(DS):
            def fsel(t, n=n, prev=sy2.v):
                if prev:
                    t.wait_ge(sy2.h, prev)
                t.matmul(ps_w[0:1, :CH], c.t["idn"][64:128, 64 + n:64 + n + 1],
                         bb_n[64:128, :CH], start=True, stop=True)
                t.matmul(ps_b1[0:1, :CH], c.t["idn"][0:64, n:n + 1],
                         bb_n[0:64, :CH], start=True, stop=True).then_inc(sy2.h, 1)
            sy2.v += 1
            c.on("tensor", fsel)
            selv = sy2.v

            def fsee(s, n=n, selv=selv):
                s.wait_ge(sy2.h, selv)
                s.copy(brow_s[0:1, :CH], ps_w[0:1, :CH])
                s.copy(crow_s[0:1, :CH], ps_b1[0:1, :CH]).then_inc(sy2.h, 1)
            sy2.v += 1
            c.on("scalar", fsee)
            seev = sy2.v

            def fbcm(t, n=n, seev=seev):
                t.wait_ge(sy2.h, seev)
                t.matmul(ps_w[:, :CH], c.t["onesr_b"][0:1, :], brow_s[0:1, :CH],
                         start=True, stop=True)
                t.matmul(ps_b1[:, :CH], c.t["onesr_b"][0:1, :], crow_s[0:1, :CH],
                         start=True, stop=True).then_inc(sy2.h, 1)
            sy2.v += 1
            c.on("tensor", fbcm)
            bcmv = sy2.v

            def fbce(s, n=n, bcmv=bcmv):
                s.wait_ge(sy2.h, bcmv)
                s.copy(cc_n[:, :CH], ps_b1[:, :CH])
                s.copy(scr[:, :CH], ps_w[:, :CH]).then_inc(sy2.h, 1)
            sy2.v += 1
            c.on("scalar", fbce)
            bcev = sy2.v
            for k in range(DIT):
                def fda(s, n=n, k=k, bcev=bcev, prevv=sy2.v):
                    s.wait_ge(sy2.h, bcev)
                    if prevv > bcev:
                        s.wait_ge(sy2.h, prevv)
                    s.activation(tmp[:, :CH], dl32[:, k * CH:(k + 1) * CH], AF.Exp,
                                 scale=a32[:, k * 64 + n:k * 64 + n + 1]).then_inc(sy2.h, 1)
                sy2.v += 1
                c.on("scalar", fda)
                dav = sy2.v

                def fscan(v, n=n, k=k, dav=dav, e0=sy2.v + 1, e1=sy2.v + 2, e2=sy2.v + 3):
                    v.wait_ge(sy2.h, dav)
                    v.tensor_tensor(sqb[:, :CH], w32[:, k * CH:(k + 1) * CH],
                                    scr[:, :CH], ALU.mult).then_inc(sy2.h, 1)
                    v.wait_ge(sy2.h, e0)
                    v.tensor_tensor_scan(sqb[:, CH:2 * CH], tmp[:, :CH], sqb[:, 0:CH],
                                         hcar[:, k * 64 + n:k * 64 + n + 1],
                                         ALU.mult, ALU.add).then_inc(sy2.h, 1)
                    v.wait_ge(sy2.h, e1)
                    v.tensor_copy(hcar[:, k * 64 + n:k * 64 + n + 1], sqb[:, 2 * CH - 1:2 * CH])
                    v.tensor_tensor(sqb[:, 2 * CH:3 * CH], sqb[:, CH:2 * CH], cc_n[:, :CH],
                                    ALU.mult).then_inc(sy2.h, 1)
                    v.wait_ge(sy2.h, e2)
                    v.tensor_tensor(y32[:, k * CH:(k + 1) * CH], y32[:, k * CH:(k + 1) * CH],
                                    sqb[:, 2 * CH:3 * CH], ALU.add).then_inc(sy2.h, 1)
                sy2.v += 4
                c.on("vector", fscan)
        scan_done = sy2.v
        c.flush()

        for k in range(DIT):
            def fyd(v, k=k, e0=se.v + 1):
                v.scalar_tensor_tensor(y32[:, k * CH:(k + 1) * CH], u_bf[:, k * CH:(k + 1) * CH],
                                       dpcol[:, k:k + 1], y32[:, k * CH:(k + 1) * CH],
                                       ALU.mult, ALU.add).then_inc(se.h, 1)
            se.v += 1
            c.on("vector", fyd)
            ydv = se.v

            def fsz(s, k=k, ydv=ydv):
                s.wait_ge(se.h, ydv)
                s.activation(scr[:, :CH], z_sb2[:, k * CH:(k + 1) * CH], AF.Silu).then_inc(se.h, 1)
            se.v += 1
            c.on("scalar", fsz)
            szv = se.v

            def fyg(v, k=k, szv=szv):
                v.wait_ge(se.h, szv)
                v.tensor_tensor(u_bf[:, k * CH:(k + 1) * CH], y32[:, k * CH:(k + 1) * CH],
                                scr[:, :CH], ALU.mult).then_inc(se.h, 1)
            se.v += 1
            c.on("vector", fyg)
        ygate_done = se.v
        c.flush()
        def fyz(v):
            v.memset(y32[:, :], 0.0)
        c.on("vector", fyz)

        sldx2, ldv2 = load_x_chunk(c, x_in, ci, tag="l2alx2_%d" % ci)
        emit_ln_chunk(c, "l2aln2", ci, xn32, wait=(sldx2, ldv2), xn_ktile_stride=CH)
        c.flush()

        sfin = c.sem("l2afin")
        for of in range(KT):
            prevmm = se.v

            def fwo(g, of=of, prevmm=prevmm):
                if prevmm:
                    g.wait_ge(se.h, prevmm)
                src2 = wd["l2a_woutT_d"].rearrange("(k p) o -> p k o", p=128)
                g.dma_start(wouts[:, :].rearrange("p (k o) -> p k o", o=128),
                            src2[:, :, of * 128:(of + 1) * 128]).then_inc(sw.h, 16)
            sw.v += 16
            c.on("gpsimd", fwo)
            wv = sw.v
            vgate = sfin.v - 1

            def fmm(t, of=of, wv=wv, vgate=vgate):
                t.wait_ge(sw.h, wv)
                if vgate > 0:
                    t.wait_ge(sfin.h, vgate)
                ps = (psA, psB)[of % 2]
                last = None
                for k in range(DIT):
                    last = t.matmul(ps[:, :CH], wouts[:, k * 128:(k + 1) * 128],
                                    u_bf[:, k * CH:(k + 1) * CH],
                                    start=(k == 0), stop=(k == DIT - 1))
                last.then_inc(se.h, 1)
            se.v += 1
            c.on("tensor", fmm)
            mmv = se.v
            slot = (tmp[:, :CH], scr[:, :CH])[of % 2]

            def fse(s, of=of, mmv=mmv, slot=slot, vprev=sfin.v):
                s.wait_ge(se.h, mmv)
                if vprev > 1:
                    s.wait_ge(sfin.h, vprev - 1)
                ps = (psA, psB)[of % 2]
                s.copy(slot, ps[:, :CH]).then_inc(se.h, 1)
            se.v += 1
            c.on("scalar", fse)
            sev_ = se.v

            def fvv(v, of=of, ci=ci, sev_=sev_, slot=slot, e0=sfin.v + 1):
                v.wait_ge(se.h, sev_)
                sl = xc[:, of * CH:(of + 1) * CH]
                v.tensor_tensor(sl, sl, slot, ALU.add).then_inc(sfin.h, 1)
                v.wait_ge(sfin.h, e0)
                v.tensor_tensor(sl, sl, xn32[:, of * CH:(of + 1) * CH], ALU.add).then_inc(sfin.h, 1)
            sfin.v += 2
            c.on("vector", fvv)
        store_x_chunk(c, x_out, ci, sfin, sfin.v)
        c.flush()


STAGE_DRAM["l2a"] = [("l2a_winT_d", [D, 4096]), ("l2a_wxT_d", [DIT_ := 2048, 192]),
                     ("l2a_wdtT_d", [64, 2048]), ("l2a_woutT_d", [2048, D]),
                     ("l2a_alog_d", [128, 16 * 64]), ("l2a_convw_d", [128, 16 * 4]),
                     ("l2a_convb_d", [128, 16]), ("l2a_bdt_d", [128, 16]),
                     ("l2a_dp_d", [128, 16]), ("l2a_nw_d", [128, 8])]


def _stage_l2a(inp):
    def col128s(v):
        n = v.shape[0]
        return np.ascontiguousarray(v.reshape(n // 128, 128).T)
    alog = inp["L2_attn_A_log"]
    a = np.zeros((128, 16 * 64), np.float32)
    for k in range(16):
        a[:, k * 64:(k + 1) * 64] = alog[k * 128:(k + 1) * 128, :]
    cw = inp["L2_attn_conv_w"][:, 0, :]
    cwl = np.zeros((128, 64), np.float32)
    for k in range(16):
        cwl[:, k * 4:(k + 1) * 4] = cw[k * 128:(k + 1) * 128, :]
    return {"l2a_winT_d": np.ascontiguousarray(inp["L2_attn_W_in"].T),
            "l2a_wxT_d": np.ascontiguousarray(inp["L2_attn_W_x"].T),
            "l2a_wdtT_d": np.ascontiguousarray(inp["L2_attn_W_dt"].T),
            "l2a_woutT_d": np.ascontiguousarray(inp["L2_attn_W_out"].T),
            "l2a_alog_d": a, "l2a_convw_d": cwl,
            "l2a_convb_d": col128s(inp["L2_attn_conv_b"]),
            "l2a_bdt_d": col128s(inp["L2_attn_b_dt"]),
            "l2a_dp_d": col128s(inp["L2_attn_Dp"]),
            "l2a_nw_d": col128s(inp["L2_attn_norm_w"]),
            }


_OLD_STAGE3 = stage_weights


def stage_weights(stage, inp):
    if stage == "l2a":
        return _stage_l2a(inp)
    return _OLD_STAGE3(stage, inp)


_OLD_BUILD3 = build_stage


def build_stage(stage):
    EMITTERS.setdefault("l2a", emit_l2a)
    return _OLD_BUILD3(stage)


def emit_l1fa(c: Ctx, x_in, wd, fl_out, cnt_out):
    xn = c.sb("xnA", [128, KT * N], BF16)
    rwT = c.sb("rwT", [128, KT * 8], BF16)
    rb = c.sb("rb", [128, 1], F32)
    lg_sb = c.sb("lg_sb", [8, N], F32)
    idnf = c.sb("idnf", [128, 128], F32)
    ltm = c.sb("ltm", [128, 8], F32)
    fcols = c.sb("fcols", [128, 32], F32)
    mrow3 = c.sb("mrow3", [128, 6], F32)
    fl_sb = c.sb("fl_sb", [8, 4 * N], F32)
    cnt_sb = c.sb("cnt_sb", [16, 1], F32)
    ps_t2 = c.ps("ps_t2", [128, 32])
    ps_t = c.t["ps_b1"]
    ps_c = c.t["ps_b2"]
    psA, psB = c.t["mm_psA"], c.t["mm_psB"]

    sld = c.sem("l1faw")

    def fw(g):
        g.dma_start(rwT[:, :].rearrange("p (k o) -> p k o", o=8),
                    wd["l1f_rwT_d"].rearrange("(k p) o -> p k o", p=128)).then_inc(sld.h, 16)
        g.dma_start(rb[0:8, :], wd["l1f_rb_d"]).then_inc(sld.h, 16)
    sld.v += 32
    c.on("gpsimd", fw)

    def fidn(v):
        v.tensor_tensor(idnf[:, :], c.t["riot"][:, :],
                        c.t["ciot"][:, :].broadcast_to([128, 128]), ALU.is_equal)
    c.on("vector", fidn)
    emit_ln_all(c, "l1faln", x_in, xn)
    c.flush()
    c.dma_barrier(sld)

    se = c.sem("l1fase")
    for ci in range(NCH):
        def fmm(t, ci=ci, gate=se.v - 1):
            if gate > 0:
                t.wait_ge(se.h, gate)
            ps = (psA, psB)[ci % 2]
            last = None
            for k in range(KT):
                last = t.matmul(ps[0:8, :CH], rwT[:, k * 8:(k + 1) * 8],
                                xn[:, k * N + ci * CH:k * N + ci * CH + CH],
                                start=(k == 0), stop=(k == KT - 1))
            last.then_inc(se.h, 1)
        se.v += 1
        c.on("tensor", fmm)
        mmv = se.v

        def fev(s, ci=ci, mmv=mmv):
            s.wait_ge(se.h, mmv)
            ps = (psA, psB)[ci % 2]
            s.activation(lg_sb[0:8, ci * CH:(ci + 1) * CH], ps[0:8, :CH], AF.Identity,
                         bias=rb[0:8, 0:1], scale=1.0).then_inc(se.h, 1)
        se.v += 1
        c.on("scalar", fev)
    c.flush()

    s2 = c.sem("l1fas2")
    for ti in range(16):
        def ftr(t, ti=ti, gate=s2.v):
            if gate:
                t.wait_ge(s2.h, gate)
            t.transpose(ps_t2[0:128, 0:8], lg_sb[0:8, ti * 128:(ti + 1) * 128],
                        idnf[0:8, 0:8]).then_inc(s2.h, 1)
        s2.v += 1
        c.on("tensor", ftr)
        trv = s2.v

        def fl1(v, ti=ti, trv=trv, e1=s2.v + 1, e2_=s2.v + 2, e3=s2.v + 3, e4=s2.v + 4,
                e5=s2.v + 5, e6=s2.v + 6):
            v.wait_ge(s2.h, trv)
            v.tensor_copy(ltm[:, 0:8], ps_t2[:, 0:8]).then_inc(s2.h, 1)
            v.wait_ge(s2.h, e1)
            v.reduce_max(mrow3[:, 0:1], ltm[:, 0:8], axis=AX.X).then_inc(s2.h, 1)
            v.wait_ge(s2.h, e2_)
            v.tensor_scalar_mul(mrow3[:, 1:2], mrow3[:, 0:1], -1.0)
            v.tensor_scalar(fcols[:, 16:24], ltm[:, 0:8], mrow3[:, 0:1], None,
                            ALU.is_equal).then_inc(s2.h, 1)
            v.wait_ge(s2.h, e3)
            v.tensor_scalar(fcols[:, 0:8], fcols[:, 16:24], -1e30, None, ALU.mult).then_inc(s2.h, 1)
            v.wait_ge(s2.h, e4)
            v.tensor_tensor(fcols[:, 8:16], ltm[:, 0:8], fcols[:, 0:8], ALU.add).then_inc(s2.h, 1)
            v.wait_ge(s2.h, e5)
            v.reduce_max(mrow3[:, 2:3], fcols[:, 8:16], axis=AX.X).then_inc(s2.h, 1)
            v.wait_ge(s2.h, e6)
            v.tensor_scalar(fcols[:, 24:32], fcols[:, 8:16], mrow3[:, 2:3], None,
                            ALU.is_equal).then_inc(s2.h, 1)
        s2.v += 7
        c.on("vector", fl1)
        l1v = s2.v

        def fl2(s, ti=ti, l1v=l1v):
            s.wait_ge(s2.h, l1v)
            s.activation(mrow3[:, 3:4], mrow3[:, 2:3], AF.Exp,
                         bias=mrow3[:, 1:2], scale=1.0).then_inc(s2.h, 1)
        s2.v += 1
        c.on("scalar", fl2)
        l2v = s2.v

        def fl3(v, ti=ti, l2v=l2v, e1=s2.v + 1, e2_=s2.v + 2, e3=s2.v + 3):
            v.wait_ge(s2.h, l2v)
            v.tensor_scalar_add(mrow3[:, 4:5], mrow3[:, 3:4], 1.0).then_inc(s2.h, 1)
            v.wait_ge(s2.h, e1)
            v.reciprocal(mrow3[:, 4:5], mrow3[:, 4:5]).then_inc(s2.h, 1)
            v.wait_ge(s2.h, e2_)
            v.tensor_tensor(mrow3[:, 5:6], mrow3[:, 3:4], mrow3[:, 4:5], ALU.mult).then_inc(s2.h, 1)
            v.wait_ge(s2.h, e3)
            v.tensor_scalar_mul(fcols[:, 0:8], fcols[:, 16:24], mrow3[:, 4:5])
            v.tensor_scalar_mul(fcols[:, 8:16], fcols[:, 24:32], mrow3[:, 5:6]).then_inc(s2.h, 1)
        s2.v += 4
        c.on("vector", fl3)
        l3v = s2.v

        for j in range(4):
            def ftrj(t, ti=ti, j=j, l3v=l3v, gate=s2.v):
                t.wait_ge(s2.h, l3v)
                if gate > l3v:
                    t.wait_ge(s2.h, gate)
                t.transpose(ps_t[0:8, 0:128], fcols[:, j * 8:(j + 1) * 8],
                            idnf[:, :]).then_inc(s2.h, 1)
            s2.v += 1
            c.on("tensor", ftrj)
            tv = s2.v

            def fevj(s, ti=ti, j=j, tv=tv):
                s.wait_ge(s2.h, tv)
                s.copy(fl_sb[0:8, j * N + ti * 128:j * N + (ti + 1) * 128],
                       ps_t[0:8, 0:128]).then_inc(s2.h, 1)
            s2.v += 1
            c.on("scalar", fevj)
        def fcnt(t, ti=ti, gate=s2.v):
            t.wait_ge(s2.h, gate)
            t.matmul(ps_c[0:16, 0:1], fcols[:, 16:32], c.t["ones_f"][:, :],
                     start=(ti == 0), stop=(ti == 15)).then_inc(s2.h, 1)
        s2.v += 1
        c.on("tensor", fcnt)
    cntv = s2.v

    def fce(v, cntv=cntv):
        v.wait_ge(s2.h, cntv)
        v.tensor_copy(cnt_sb[0:16, :], ps_c[0:16, 0:1]).then_inc(s2.h, 1)
    s2.v += 1
    c.on("vector", fce)
    c.flush()

    sout = c.sem("sout")

    def fst(sy):
        sy.dma_start(fl_out, fl_sb[0:8, :]).then_inc(sout.h, 16)
        sy.dma_start(cnt_out, cnt_sb[0:16, :]).then_inc(sout.h, 16)
    sout.v += 32
    c.on("sync", fst)
    c.flush()


def emit_l1fb(c: Ctx, x_in, x_out, wd):
    CAP = 2048.0
    xn = c.sb("xnB", [128, KT * CH], BF16)
    fl = c.sb("fl", [8, 4 * N], BF16)
    cntp = c.sb("cntp", [8, 16], F32)
    offr = c.sb("offr", [1, 16], F32)
    grow = c.sb("grow", [8, 2 * N], F32)
    gtot = c.sb("gtot", [8, N], F32)
    wslab = c.sb("wslab", [128, KT * 2048], BF16)
    vslab = c.sb("vslab", [128, KT * 2048], BF16)
    oslab = c.sb("oslab", [128, 16 * 128], BF16)
    outacc = c.sb("outacc2", [128, KT * CH], F32)
    a_sc = c.t["tmp"]
    b_sc = c.t["rstd_b"]
    g_bc = c.t["negm_b"]
    act2 = c.sb("act2", [128, 16 * CH], BF16)
    idnf2 = c.sb("idnf2t", [128, 128], F32)
    c.t["idnf2"] = idnf2
    ps_o = c.ps("ps_o", [1, CH])
    psA, psB = c.t["mm_psA"], c.t["mm_psB"]

    sld = c.sem("l1fbw")

    def fw(g):
        g.dma_start(fl[0:8, :], wd["l1f_fl_d"]).then_inc(sld.h, 16)
        g.dma_start(cntp[0:8, :], wd["l1f_cntp_d"]).then_inc(sld.h, 16)
    sld.v += 32
    c.on("gpsimd", fw)

    def fidn2(v):
        v.tensor_tensor(idnf2[:, :], c.t["riot"][:, :],
                        c.t["ciot"][:, :].broadcast_to([128, 128]), ALU.is_equal)
    c.on("vector", fidn2)
    c.flush()
    c.dma_barrier(sld)

    se = c.sem("l1fbse")

    def foff(t):
        t.matmul(c.t["ps_s"][0:1, 0:16], c.t["ones_f"][0:8, :], cntp[0:8, :],
                 start=True, stop=True).then_inc(se.h, 1)
    se.v += 1
    c.on("tensor", foff)
    ofv = se.v

    def foe(s, ofv=ofv):
        s.wait_ge(se.h, ofv)
        s.copy(offr[0:1, :], c.t["ps_s"][0:1, 0:16]).then_inc(se.h, 1)
    se.v += 1
    c.on("scalar", foe)
    fv_ = se.v

    def fot(v, fv_=fv_):
        v.wait_ge(se.h, fv_)
        v.tensor_tensor(offr[0:1, 0:8], offr[0:1, 0:8], offr[0:1, 8:16], ALU.add).then_inc(se.h, 1)
    se.v += 1
    c.on("vector", fot)
    c.flush()

    def fgate(v, e0=se.v):
        f0 = fl[0:8, 2 * N:3 * N]
        f1 = fl[0:8, 3 * N:4 * N]
        g0 = fl[0:8, 0:N]
        g1 = fl[0:8, N:2 * N]
        cA = grow[0:8, 0:N]
        cB = grow[0:8, N:2 * N]
        ob = c.t["ones_b"][0:8, 0:1].broadcast_to([8, N])
        v.tensor_tensor_scan(cA, ob, f0, 0.0, ALU.mult, ALU.add).then_inc(se.h, 1)
        v.wait_ge(se.h, e0 + 1)
        v.tensor_tensor_scan(cB, ob, f1, 0.0, ALU.mult, ALU.add).then_inc(se.h, 1)
        v.wait_ge(se.h, e0 + 2)
        v.tensor_tensor(cA, cA, cB, ALU.add).then_inc(se.h, 1)
        v.wait_ge(se.h, e0 + 3)
        v.tensor_tensor(cB, cA, f1, ALU.subtract).then_inc(se.h, 1)
        v.wait_ge(se.h, e0 + 4)
        v.engine_nop() if False else None
    se.v += 4
    c.on("vector", fgate)
    gv = se.v

    def ftro(t, gv=gv):
        t.wait_ge(se.h, gv)
        t.transpose(c.t["ps_b1"][0:16, 0:1] if False else c.t["ps_b1"][0:128, 0:1],
                    offr[0:1, 0:16], c.t["idn"][:, :]) if False else None
        t.transpose(c.t["ps_b1"][0:16, 0:128], offr[0:1, 0:128][0:1, 0:16], c.t["idn"][:, :]) if False else None
        t.matmul(c.t["ps_b1"][0:8, 0:1], offr[0:1, 0:8], c.t["ones_f"][0:1, 0:1],
                 start=True, stop=True).then_inc(se.h, 1)
    se.v += 1
    c.on("tensor", ftro)
    trv = se.v

    def fthr(v, trv=trv):
        v.wait_ge(se.h, trv)
        v.tensor_scalar(b_sc[0:8, 0:1], c.t["ps_b1"][0:8, 0:1], -1.0, CAP + 1.0,
                        ALU.mult, ALU.add).then_inc(se.h, 1)
    se.v += 1
    c.on("vector", fthr)
    c.flush()

    def fgate2(v, e0=se.v):
        cA = grow[0:8, 0:N]
        cB = grow[0:8, N:2 * N]
        g0 = fl[0:8, 0:N]
        g1 = fl[0:8, N:2 * N]
        v.tensor_scalar(gtot[0:8, :], cB, b_sc[0:8, 0:1], None, ALU.is_lt).then_inc(se.h, 1)
        v.wait_ge(se.h, e0 + 1)
        v.tensor_tensor(gtot[0:8, :], gtot[0:8, :], g0, ALU.mult).then_inc(se.h, 1)
        v.wait_ge(se.h, e0 + 2)
        v.tensor_scalar(cB, cA, b_sc[0:8, 0:1], None, ALU.is_lt).then_inc(se.h, 1)
        v.wait_ge(se.h, e0 + 3)
        v.tensor_tensor(cB, cB, g1, ALU.mult).then_inc(se.h, 1)
        v.wait_ge(se.h, e0 + 4)
        v.tensor_tensor(gtot[0:8, :], gtot[0:8, :], cB, ALU.add).then_inc(se.h, 1)
    se.v += 5
    c.on("vector", fgate2)
    c.flush()

    sw = c.sem("l1fbsw")
    sv2 = c.sem("l1fbv")
    prev_gbe = [0]
    for tc in range(NCH):
        sldx0, ldv0 = load_x_chunk(c, x_in, tc, tag="l1fblnx%d" % tc)
        emit_ln_chunk(c, "l1fbln", tc, xn, wait=(sldx0, ldv0), xn_ktile_stride=CH)
        c.flush()
        for e in range(8):
            gatev = se.v

            def fwl(g, e=e, tc=tc, gate=se.v):
                g.wait_ge(se.h, gate)
                g.dma_start(wslab[:, :].rearrange("p (k o) -> p k o", o=2048),
                            wd["l1f_W_d"][e].rearrange("(k p) o -> p k o", p=128)).then_inc(sw.h, 16)
                g.dma_start(vslab[:, :].rearrange("p (k o) -> p k o", o=2048),
                            wd["l1f_V_d"][e].rearrange("(k p) o -> p k o", p=128)).then_inc(sw.h, 16)
            sw.v += 32
            c.on("gpsimd", fwl)
            wv = sw.v
            def fgs(t, e=e, tc=tc, gate=se.v):
                if gate:
                    t.wait_ge(se.h, gate)
                t.matmul(ps_o[0:1, :CH], c.t["idnf2"][0:8, e:e + 1],
                         gtot[0:8, tc * CH:(tc + 1) * CH], start=True, stop=True).then_inc(se.h, 1)
            se.v += 1
            c.on("tensor", fgs)
            gsv = se.v

            def fge(s, gsv=gsv):
                s.wait_ge(se.h, gsv)
                s.copy(c.t["strow"][0:1, 0:CH], ps_o[0:1, :CH]).then_inc(se.h, 1)
            se.v += 1
            c.on("scalar", fge)
            gev = se.v

            def fgb(t, gev=gev, pg=prev_gbe[0]):
                t.wait_ge(se.h, gev)
                if pg:
                    t.wait_ge(se.h, pg)
                t.matmul(c.t["ps_b2"][:, :CH], c.t["onesr_f"][0:1, :], c.t["strow"][0:1, 0:CH],
                         start=True, stop=True).then_inc(se.h, 1)
            se.v += 1
            c.on("tensor", fgb)
            gbv = se.v

            def fgbe(v, gbv=gbv):
                v.wait_ge(se.h, gbv)
                v.tensor_copy(g_bc[:, :CH], c.t["ps_b2"][:, :CH]).then_inc(se.h, 1)
            se.v += 1
            c.on("vector", fgbe)
            gbev = se.v
            prev_gbe[0] = gbev
            for of in range(16):
                def fmm(t, e=e, tc=tc, of=of, wv=wv, gbev=gbev, gate=se.v - 1):
                    t.wait_ge(sw.h, wv)
                    t.wait_ge(se.h, gbev)
                    if gate > gbev:
                        t.wait_ge(se.h, gate)
                    last = None
                    for k in range(KT):
                        last = t.matmul(psA[:, :CH], wslab[:, k * 2048 + of * 128:k * 2048 + (of + 1) * 128],
                                        xn[:, k * CH:(k + 1) * CH],
                                        start=(k == 0), stop=(k == KT - 1))
                    for k in range(KT):
                        last = t.matmul(psB[:, :CH], vslab[:, k * 2048 + of * 128:k * 2048 + (of + 1) * 128],
                                        xn[:, k * CH:(k + 1) * CH],
                                        start=(k == 0), stop=(k == KT - 1))
                    last.then_inc(se.h, 1)
                se.v += 1
                c.on("tensor", fmm)
                mmv = se.v

                def fab(s, of=of, mmv=mmv, gate=se.v + 1):
                    s.wait_ge(se.h, mmv)
                    s.copy(a_sc[:, :CH], psA[:, :CH])
                    s.activation(b_sc[:, :CH], psB[:, :CH], AF.Silu).then_inc(se.h, 1)
                se.v += 1
                c.on("scalar", fab)
                abv = se.v

                def fact(v, of=of, abv=abv, e0=se.v + 1):
                    v.wait_ge(se.h, abv)
                    v.tensor_tensor(a_sc[:, :CH], a_sc[:, :CH], b_sc[:, :CH], ALU.mult).then_inc(se.h, 1)
                    v.wait_ge(se.h, e0)
                    v.tensor_tensor(act2[:, of * CH:(of + 1) * CH], a_sc[:, :CH],
                                    g_bc[:, :CH], ALU.mult).then_inc(se.h, 1)
                se.v += 2
                c.on("vector", fact)
            actv = se.v
            for of2 in range(KT):
                def fwo2(g, e=e, of2=of2, gate=se.v):
                    if gate:
                        g.wait_ge(se.h, gate)
                    src3 = wd["l1f_Wout_d"][e].rearrange("(k p) o -> p k o", p=128)
                    g.dma_start(oslab[:, :].rearrange("p (k o) -> p k o", o=128),
                                src3[:, :, of2 * 128:(of2 + 1) * 128]).then_inc(sw.h, 16)
                sw.v += 16
                c.on("gpsimd", fwo2)
                wv2 = sw.v

                def fmm2(t, e=e, of2=of2, actv=actv, gate=se.v - 1, wv2=wv2):
                    t.wait_ge(sw.h, wv2)
                    t.wait_ge(se.h, actv)
                    if gate > actv:
                        t.wait_ge(se.h, gate)
                    ps = (psA, psB)[of2 % 2]
                    last = None
                    for k in range(16):
                        last = t.matmul(ps[:, :CH], oslab[:, k * 128:(k + 1) * 128],
                                        act2[:, k * CH:(k + 1) * CH],
                                        start=(k == 0), stop=(k == 15))
                    last.then_inc(se.h, 1)
                se.v += 1
                c.on("tensor", fmm2)
                m2v = se.v

                def facc(v, e=e, of2=of2, m2v=m2v):
                    v.wait_ge(se.h, m2v)
                    ps = (psA, psB)[of2 % 2]
                    dst = outacc[:, of2 * CH:(of2 + 1) * CH]
                    if e == 0:
                        v.tensor_copy(dst, ps[:, :CH]).then_inc(se.h, 1)
                    else:
                        v.tensor_tensor(dst, dst, ps[:, :CH], ALU.add).then_inc(se.h, 1)
                se.v += 1
                c.on("vector", facc)
            c.flush()
        sfin = c.sem("l1fbfin")
        sldx, ldv = load_x_chunk(c, x_in, tc, tag="l1fblx%d" % tc)

        def fadd(v, tc=tc, ldv=ldv):
            v.wait_ge(c.sem("l1fblx%d" % tc).h, ldv)
            last = None
            for k in range(KT):
                sl = c.t["xc"][:, k * CH:(k + 1) * CH]
                last = v.tensor_tensor(sl, sl, outacc[:, k * CH:(k + 1) * CH], ALU.add)
            last.then_inc(sfin.h, 1)
        sfin.v += 1
        c.on("vector", fadd)
        store_x_chunk(c, x_out, tc, sfin, sfin.v)
        c.flush()


STAGE_DRAM["l1fa"] = [("l1f_rwT_d", [D, 8]), ("l1f_rb_d", [8, 1])]
STAGE_DRAM["l1fb"] = [("l1f_fl_d", [8, 4 * N]), ("l1f_cntp_d", [8, 16]),
                      ("l1f_W_d", [8, D, 2048]), ("l1f_V_d", [8, D, 2048]),
                      ("l1f_Wout_d", [8, 2048, D])]

_BUILD_CACHE_MOE = {}


def build_l1fa():
    if "a" in _BUILD_CACHE_MOE:
        return _BUILD_CACHE_MOE["a"]
    stack = ExitStack()
    nc = bass.Bass("TRN2", target_bir_lowering=False)
    x_in = nc.dram_tensor("x_in", [KT, 128, N], F32, kind="ExternalInput").ap()
    fl_out = nc.dram_tensor("fl_out", [8, 4 * N], F32, kind="ExternalOutput").ap()
    cnt_out = nc.dram_tensor("cnt_out", [16, 1], F32, kind="ExternalOutput").ap()
    wd = {nm: nc.dram_tensor(nm, list(sh), F32, kind="ExternalInput").ap()
          for nm, sh in STAGE_DRAM["l1fa"]}
    c = Ctx(nc, stack)
    emit_setup(c)
    alloc_shared(c)
    emit_l1fa(c, x_in, wd, fl_out, cnt_out)
    sout = c.sem("sout")

    def fwait(sy):
        sy.wait_ge(sout.h, sout.v)
    c.on("sync", fwait)
    c.flush()
    stack.close()
    _BUILD_CACHE_MOE["a"] = nc
    return nc


def build_l1fb():
    if "b" in _BUILD_CACHE_MOE:
        return _BUILD_CACHE_MOE["b"]
    stack = ExitStack()
    nc = bass.Bass("TRN2", target_bir_lowering=False)
    x_in = nc.dram_tensor("x_in", [KT, 128, N], F32, kind="ExternalInput").ap()
    x_out = nc.dram_tensor("x_out", [KT, 128, N], F32, kind="ExternalOutput").ap()
    wd = {nm: nc.dram_tensor(nm, list(sh), F32, kind="ExternalInput").ap()
          for nm, sh in STAGE_DRAM["l1fb"]}
    c = Ctx(nc, stack)
    emit_setup(c)
    alloc_shared(c)
    emit_l1fb(c, x_in, x_out, wd)
    sout = c.sem("sout")

    def fwait(sy):
        sy.wait_ge(sout.h, sout.v)
    c.on("sync", fwait)
    c.flush()
    stack.close()
    _BUILD_CACHE_MOE["b"] = nc
    return nc


def run_l1f(xs, inp):
    ncA = build_l1fa()
    wA = {"l1f_rwT_d": np.ascontiguousarray(inp["L1_ffn_router_w"].T),
          "l1f_rb_d": np.ascontiguousarray(inp["L1_ffn_router_b"].reshape(8, 1))}
    res = run_bass_kernel_spmd(ncA, [dict(x_in=xs[ci], **wA) for ci in range(8)],
                               core_ids=list(range(8)))
    fls = [np.asarray(r["fl_out"]) for r in res.results]
    cnts = [np.asarray(r["cnt_out"]) for r in res.results]
    ncB = build_l1fb()
    maps = []
    for ci in range(8):
        cntp = np.zeros((8, 16), np.float32)
        for r in range(ci):
            cntp[r, :] = cnts[r][:, 0]
        maps.append(dict(x_in=xs[ci], l1f_fl_d=fls[ci], l1f_cntp_d=cntp,
                         l1f_W_d=inp["L1_ffn_W"], l1f_V_d=inp["L1_ffn_V"],
                         l1f_Wout_d=inp["L1_ffn_W_out"]))
    res = run_bass_kernel_spmd(ncB, maps, core_ids=list(range(8)))
    return [np.asarray(r["x_out"]) for r in res.results]


_OLD_RUN = run_stage


def run_stage(stage, xs, inp):
    if stage == "l1f":
        return run_l1f(xs, inp)
    return _OLD_RUN(stage, xs, inp)


def emit_l0a_solution(c, s3, SL, FL, PS, ps_m, ps_m2, psB, pb, o_fm, kt_, s0,
                      Xh, Xbh, XTh, qf, idnp, masks):

    def TQ(f):
        s3.v += 1
        c.on("tensor", f)
        return s3.v

    def VQ(f, n=1):
        s3.v += n
        c.on("vector", f)
        return s3.v

    triuI = masks[pb:pb + 64, 128:192]

    def FLc(i, j):
        return FL(i)[:, j:j + 1]

    def solve(Lsl, Ltsl, Rsl, OUTsl, OUTTsl):
        m0 = TQ(lambda t, g=s3.v: (
            t.wait_ge(s3.h, g),
            t.matmul(PS(ps_m), SL(Ltsl), SL(Rsl), start=True, stop=True,
                     tile_position=(pb, pb)).then_inc(s3.h, 1)))
        VQ(lambda v, m0=m0, e=s3.v + 1: (
            v.wait_ge(s3.h, m0),
            v.tensor_tensor(FL(3), SL(Rsl), PS(ps_m), ALU.subtract).then_inc(s3.h, 1),
            v.wait_ge(s3.h, e),
            v.tensor_copy(SL(18), FL(3)).then_inc(s3.h, 1)), n=2)
        curP, curPt = Lsl, Ltsl
        for lev in range(5):
            m1 = TQ(lambda t, g=s3.v, curP=curP, curPt=curPt: (
                t.wait_ge(s3.h, g),
                t.matmul(PS(ps_m), SL(curPt), SL(curP), start=True, stop=True,
                         tile_position=(pb, pb)).then_inc(s3.h, 1)))
            e1 = VQ(lambda v, m1=m1: (
                v.wait_ge(s3.h, m1),
                v.tensor_copy(SL(16), PS(ps_m)).then_inc(s3.h, 1)))
            m2 = TQ(lambda t, e1=e1: (
                t.wait_ge(s3.h, e1),
                t.transpose(PS(ps_m2), SL(16), idnp,
                            tile_position=(pb, pb)).then_inc(s3.h, 1)))
            e2 = VQ(lambda v, m2=m2: (
                v.wait_ge(s3.h, m2),
                v.tensor_copy(SL(17), PS(ps_m2)).then_inc(s3.h, 1)))
            m3 = TQ(lambda t, e2=e2: (
                t.wait_ge(s3.h, e2),
                t.matmul(PS(ps_m), SL(17), SL(18), start=True, stop=True,
                         tile_position=(pb, pb)).then_inc(s3.h, 1)))
            VQ(lambda v, m3=m3, e=s3.v + 1: (
                v.wait_ge(s3.h, m3),
                v.tensor_tensor(FL(3), FL(3), PS(ps_m), ALU.add).then_inc(s3.h, 1),
                v.wait_ge(s3.h, e),
                v.tensor_copy(SL(18), FL(3)).then_inc(s3.h, 1)), n=2)
            curP, curPt = 16, 17
        fo = VQ(lambda v, g=s3.v: (
            v.wait_ge(s3.h, g),
            v.tensor_copy(SL(OUTsl), FL(3)).then_inc(s3.h, 1)))
        mt = TQ(lambda t, fo=fo: (
            t.wait_ge(s3.h, fo),
            t.transpose(PS(ps_m2), SL(OUTsl), idnp,
                        tile_position=(pb, pb)).then_inc(s3.h, 1)))
        VQ(lambda v, mt=mt: (
            v.wait_ge(s3.h, mt),
            v.tensor_copy(SL(OUTTsl), PS(ps_m2)).then_inc(s3.h, 1)))

    solve(3, 13, 11, 5, 7)
    solve(4, 14, 12, 6, 8)
    mz = TQ(lambda t, g=s3.v: (
        t.wait_ge(s3.h, g),
        t.matmul(PS(ps_m), qf, XTh, start=True, stop=True,
                 tile_position=(pb, pb)).then_inc(s3.h, 1)))
    ez = VQ(lambda v, mz=mz: (
        v.wait_ge(s3.h, mz),
        v.tensor_copy(SL(2), PS(ps_m)).then_inc(s3.h, 1)))
    mzt = TQ(lambda t, ez=ez: (
        t.wait_ge(s3.h, ez),
        t.transpose(PS(ps_m2), SL(2), idnp,
                    tile_position=(pb, pb)).then_inc(s3.h, 1)))
    ezt = VQ(lambda v, mzt=mzt: (
        v.wait_ge(s3.h, mzt),
        v.tensor_copy(FL(4), PS(ps_m2)),
        v.tensor_copy(SL(9), PS(ps_m2)).then_inc(s3.h, 1)))
    m1m = TQ(lambda t, ezt=ezt: (
        t.wait_ge(s3.h, ezt),
        t.matmul(PS(ps_m), SL(7), SL(9), start=True, stop=True,
                 tile_position=(pb, pb)).then_inc(s3.h, 1)))
    e1m = VQ(lambda v, m1m=m1m: (
        v.wait_ge(s3.h, m1m),
        v.tensor_tensor(SL(10), PS(ps_m), triuI, ALU.mult).then_inc(s3.h, 1)))
    m2m = TQ(lambda t, e1m=e1m: (
        t.wait_ge(s3.h, e1m),
        t.matmul(PS(ps_m), SL(8), qf, start=True, stop=True,
                 tile_position=(pb, pb)).then_inc(s3.h, 1)))
    e2m = VQ(lambda v, m2m=m2m, e=s3.v + 1: (
        v.wait_ge(s3.h, m2m),
        v.tensor_tensor(FL(5), PS(ps_m), FL(1), ALU.mult).then_inc(s3.h, 1),
        v.wait_ge(s3.h, e),
        v.tensor_tensor(SL(15), FL(5), triuI, ALU.mult).then_inc(s3.h, 1)), n=2)
    mk = TQ(lambda t, e2m=e2m: (
        t.wait_ge(s3.h, e2m),
        t.matmul(PS(ps_m), SL(0), SL(10), start=True, stop=True,
                 tile_position=(pb, pb)),
        t.matmul(PS(psB), SL(0), SL(15), start=True, stop=True,
                 tile_position=(pb, pb)).then_inc(s3.h, 1)))
    osl = o_fm[pb:pb + 64, kt_ * CH + s0:kt_ * CH + s0 + 64]
    VQ(lambda v, mk=mk, e=s3.v + 1, e2=s3.v + 2, osl=osl: (
        v.wait_ge(s3.h, mk),
        v.tensor_tensor(FL(5), FL(4), PS(ps_m), ALU.subtract).then_inc(s3.h, 1),
        v.wait_ge(s3.h, e),
        v.tensor_tensor(FL(5), FL(5), FL(2), ALU.mult).then_inc(s3.h, 1),
        v.wait_ge(s3.h, e2),
        v.tensor_tensor(osl, FL(5), PS(psB), ALU.add).then_inc(s3.h, 1)), n=3)
    mr = TQ(lambda t, g=s3.v: (
        t.wait_ge(s3.h, g),
        t.matmul(PS(ps_m), SL(7), Xbh, start=True, stop=True,
                 tile_position=(pb, pb)).then_inc(s3.h, 1)))
    er = VQ(lambda v, mr=mr, e=s3.v + 1: (
        v.wait_ge(s3.h, mr),
        v.tensor_copy(SL(20), PS(ps_m)).then_inc(s3.h, 1),
        v.wait_ge(s3.h, e),
        v.tensor_scalar_mul(SL(19), SL(6), FLc(1, 63)).then_inc(s3.h, 1)), n=2)
    mk2 = TQ(lambda t, er=er: (
        t.wait_ge(s3.h, er),
        t.matmul(PS(ps_m), SL(0), SL(20), start=True, stop=True,
                 tile_position=(pb, pb)),
        t.matmul(PS(psB), SL(0), SL(19), start=True, stop=True,
                 tile_position=(pb, pb)).then_inc(s3.h, 1)))
    VQ(lambda v, mk2=mk2, e=s3.v + 1, e2=s3.v + 2, Xh=Xh: (
        v.wait_ge(s3.h, mk2),
        v.tensor_tensor(FL(5), Xh, PS(ps_m), ALU.subtract).then_inc(s3.h, 1),
        v.wait_ge(s3.h, e),
        v.tensor_scalar_mul(FL(5), FL(5), FLc(2, 63)).then_inc(s3.h, 1),
        v.wait_ge(s3.h, e2),
        v.tensor_tensor(Xh, FL(5), PS(psB), ALU.add).then_inc(s3.h, 1)), n=3)


def emit_l0a(c: Ctx, x_in, x_out, wd):
    CC = 64
    xn = c.sb("xn0", [128, KT * N], BF16)
    wbig = c.sb("wbig", [128, KT * D], BF16)
    pre_d = c.nc.dram_tensor("pre_scr", [3, KT, 128, N], BF16).ap()
    qh = c.sb("qh", [128, KT * CH], BF16)
    kh = c.sb("kh", [128, KT * CH], BF16)
    vh = c.sb("vh", [128, KT * CH], BF16)
    prew = c.sb("prew", [128, KT * (CH + 3)], BF16)
    cq32 = c.sb("cq32", [128, KT * CH], F32)
    lam = c.sb("lam", [16, CH], F32)
    bgr = c.sb("bgr", [16, CH], F32)
    wab = c.sb("wab", [128, KT * 32], BF16)
    abc = c.sb("abc", [16, 4], F32)
    convs = c.sb("convs", [128, KT * 4 * 3], F32)
    masks = c.sb("masks", [128, 192], F32)
    idnf16 = c.sb("idnf16", [16, 16], F32)
    lrowb = c.sb("lrowb", [1, 64], BF16)
    X_sb = c.sb("X_sb", [128, 8 * 64], F32)
    Xbf = c.sb("Xbf", [128, 64], BF16)
    XTbf = c.sb("XTbf", [128, 64], BF16)
    o_fm = c.sb("o_fm", [128, KT * CH], BF16)
    woT = c.sb("woT", [128, KT * 128], BF16)
    bo_c = c.sb("bo_c", [128, KT], F32)
    qkvb = c.sb("qkvb", [128, 3 * KT], F32)
    smat = c.sb("smat", [128, 24 * 64], BF16)
    sfm = c.sb("sfm", [128, 8 * 64], F32)
    lrow = c.sb("lrow", [1, 2 * CH], F32)
    lcol = c.sb("lcol", [128, 2], F32)
    ps_m = c.ps("ps_m", [128, 64])
    ps_m2 = c.ps("ps_m2", [128, 64], BF16)
    psA, psB = c.t["mm_psA"], c.t["mm_psB"]
    tmp = c.t["tmp"]
    sqb = c.t["sq"]
    strow = c.t["strow"]
    xc = c.t["xc"]

    sld = c.sem("l0aw")

    def fw(g):
        g.dma_start(wab[:, :].rearrange("p (k o) -> p k o", o=32),
                    wd["l0a_wabT_d"].rearrange("(k p) o -> p k o", p=128)).then_inc(sld.h, 16)
        g.dma_start(abc[0:16, 0:2], wd["l0a_abb_d"]).then_inc(sld.h, 16)
        g.dma_start(convs[:, :], wd["l0a_conv_d"]).then_inc(sld.h, 16)
        g.dma_start(masks[:, :], wd["l0a_tril_d"]).then_inc(sld.h, 16)
        g.dma_start(bo_c[:, :], wd["l0a_bo_d"]).then_inc(sld.h, 16)
        g.dma_start(qkvb[:, :], wd["l0a_qkvb_d"]).then_inc(sld.h, 16)
    sld.v += 96
    c.on("gpsimd", fw)

    def fz(v):
        v.memset(X_sb[:, :], 0.0)
        v.memset(prew[:, :], 0.0)
    c.on("vector", fz)
    emit_ln_all(c, "l0aln", x_in, xn)
    c.flush()
    c.dma_barrier(sld)

    def fneg(v):
        v.tensor_scalar_mul(abc[0:16, 2:3], abc[0:16, 0:1], -1.0)
        v.tensor_tensor(idnf16[:, :], c.t["riot"][0:16, 0:16],
                        c.t["ciot"][0:16, 0:1].broadcast_to([16, 16]), ALU.is_equal)
    c.on("vector", fneg)
    c.flush()

    se = c.sem("l0ase")
    sw = c.sem("l0asw")
    for iw, wname in enumerate(("l0a_wqT_d", "l0a_wkT_d", "l0a_wvT_d")):
        def fwl(g, wname=wname, gate=se.v):
            if gate:
                g.wait_ge(se.h, gate)
            g.dma_start(wbig[:, :].rearrange("p (k o) -> p k o", o=D),
                        wd[wname].rearrange("(k p) o -> p k o", p=128)).then_inc(sw.h, 16)
        sw.v += 16
        c.on("gpsimd", fwl)
        wv_ = sw.v
        for ci in range(NCH):
            for of in range(KT):
                def fmm(t, iw=iw, ci=ci, of=of, wv_=wv_, gate=se.v - 1):
                    t.wait_ge(sw.h, wv_)
                    if gate > 0:
                        t.wait_ge(se.h, gate)
                    ps = (psA, psB)[of % 2]
                    last = None
                    for k in range(KT):
                        last = t.matmul(ps[:, :CH], wbig[:, k * D + of * 128:k * D + (of + 1) * 128],
                                        xn[:, k * N + ci * CH:k * N + ci * CH + CH],
                                        start=(k == 0), stop=(k == KT - 1))
                    last.then_inc(se.h, 1)
                se.v += 1
                c.on("tensor", fmm)
                mmv = se.v

                def fev(s, iw=iw, ci=ci, of=of, mmv=mmv, dmv=sw.v):
                    s.wait_ge(se.h, mmv)
                    if of == 0:
                        s.wait_ge(sw.h, dmv)
                    ps = (psA, psB)[of % 2]
                    s.activation(cq32[:, of * CH:(of + 1) * CH], ps[:, :CH], AF.Silu,
                                 bias=qkvb[:, iw * KT + of:iw * KT + of + 1],
                                 scale=1.0).then_inc(se.h, 1)
                se.v += 1
                c.on("scalar", fev)
            chv = se.v

            def fst(g, iw=iw, ci=ci, chv=chv):
                g.wait_ge(se.h, chv)
                dst = pre_d[iw].rearrange("k p t -> p k t")[:, :, ci * CH:(ci + 1) * CH]
                g.dma_start(dst, cq32[:, :].rearrange("p (k t) -> p k t", t=CH)).then_inc(sw.h, 16)
            sw.v += 16
            c.on("gpsimd", fst)
        c.flush()

    s3 = c.sem("l0as3")
    for ci in range(NCH):
        def fgm(t, ci=ci, gate=s3.v, gate2=se.v):
            if gate:
                t.wait_ge(s3.h, gate)
            if gate2:
                t.wait_ge(se.h, gate2)
            for k in range(KT):
                t.matmul(psA[0:16, :CH], wab[:, k * 32:k * 32 + 16],
                         xn[:, k * N + ci * CH:k * N + ci * CH + CH],
                         start=(k == 0), stop=(k == KT - 1))
            last = None
            for k in range(KT):
                last = t.matmul(psB[0:16, :CH], wab[:, k * 32 + 16:(k + 1) * 32],
                                xn[:, k * N + ci * CH:k * N + ci * CH + CH],
                                start=(k == 0), stop=(k == KT - 1))
            last.then_inc(s3.h, 1)
        s3.v += 1
        c.on("tensor", fgm)
        gmv = s3.v

        def fge(s, ci=ci, gmv=gmv, e1=s3.v + 1, e2=s3.v + 2):
            s.wait_ge(s3.h, gmv)
            s.activation(lam[0:16, :CH], psA[0:16, :CH], AF.Abs, bias=abc[0:16, 0:1], scale=1.0)
            s.activation(bgr[0:16, :CH], psA[0:16, :CH], AF.Relu, bias=abc[0:16, 2:3],
                         scale=-1.0).then_inc(s3.h, 1)
            s.wait_ge(s3.h, e1)
            s.activation(lam[0:16, :CH], lam[0:16, :CH], AF.Exp, bias=0.0, scale=-1.0).then_inc(s3.h, 1)
            s.wait_ge(s3.h, e2)
            s.activation(lam[0:16, :CH], lam[0:16, :CH], AF.Ln, bias=1.0, scale=1.0).then_inc(s3.h, 1)
        s3.v += 3
        c.on("scalar", fge)
        gev = s3.v

        def fgv(v, ci=ci, gev=gev, e1=s3.v + 1, e2=s3.v + 2, e3=s3.v + 3):
            v.wait_ge(s3.h, gev)
            v.tensor_tensor(lam[0:16, :CH], lam[0:16, :CH], bgr[0:16, :CH], ALU.add).then_inc(s3.h, 1)
            v.wait_ge(s3.h, e1)
            v.tensor_scalar_mul(lam[0:16, :CH], lam[0:16, :CH], -1.0).then_inc(s3.h, 1)
            v.wait_ge(s3.h, e2)
            v.tensor_tensor_scan(lam[0:16, :CH], c.t["ones_f"][0:16, 0:1].broadcast_to([16, CH]),
                                 lam[0:16, :CH], 0.0, ALU.mult, ALU.add).then_inc(s3.h, 1)
        s3.v += 3
        c.on("vector", fgv)
        lamv = s3.v

        def fgb(s, ci=ci, gmv=gmv, lamv=lamv):
            s.wait_ge(s3.h, lamv)
            s.activation(bgr[0:16, :CH], psB[0:16, :CH], AF.Sigmoid,
                         bias=abc[0:16, 1:2], scale=1.0).then_inc(s3.h, 1)
        s3.v += 3 if False else 1
        c.on("scalar", fgb)
        c.flush()

        for iw in range(3):
            sldw = c.sem("l0aldw")

            def fld(g, iw=iw, ci=ci, gate=s3.v, gw=sw.v):
                if gate:
                    g.wait_ge(s3.h, gate)
                g.wait_ge(sw.h, gw)
                lo = ci * CH - 2
                w0 = CH + 3
                srcbase = pre_d[iw].rearrange("k p t -> p k t")
                if ci == 0:
                    g.dma_start(prew[:, :].rearrange("p (k t) -> p k t", t=w0)[:, :, 2:w0],
                                srcbase[:, :, 0:CH + 1]).then_inc(sldw.h, 16)
                elif ci == NCH - 1:
                    g.dma_start(prew[:, :].rearrange("p (k t) -> p k t", t=w0)[:, :, 0:CH + 2],
                                srcbase[:, :, lo:lo + CH + 2]).then_inc(sldw.h, 16)
                else:
                    g.dma_start(prew[:, :].rearrange("p (k t) -> p k t", t=w0),
                                srcbase[:, :, lo:lo + w0]).then_inc(sldw.h, 16)
            sldw.v += 16
            c.on("gpsimd", fld)
            ldv = sldw.v

            def fzz(v, iw=iw, ci=ci, ldv=ldv):
                v.wait_ge(sldw.h, ldv)
                if ci == 0:
                    for k in range(KT):
                        v.memset(prew[:, k * (CH + 3):k * (CH + 3) + 2], 0.0)
                if ci == NCH - 1:
                    for k in range(KT):
                        v.memset(prew[:, k * (CH + 3) + CH + 2:(k + 1) * (CH + 3)], 0.0)
                v.tensor_copy(tmp[:, 0:1], prew[:, 0:1]).then_inc(s3.h, 1)
            s3.v += 1
            c.on("vector", fzz)
            zzv = s3.v

            for k in range(KT):
                base = k * (CH + 3)

                def fcv(v, iw=iw, k=k, base=base, zzv=zzv, e0=s3.v):
                    v.wait_ge(s3.h, zzv)
                    if e0:
                        v.wait_ge(s3.h, e0)
                    v.tensor_scalar_mul(sqb[:, :CH], prew[:, base:base + CH],
                                        convs[:, iw * 32 + k * 4:iw * 32 + k * 4 + 1]).then_inc(s3.h, 1)
                    for j in range(1, 4):
                        v.wait_ge(s3.h, e0 + j)
                        v.scalar_tensor_tensor(sqb[:, :CH], prew[:, base + j:base + j + CH],
                                               convs[:, iw * 32 + k * 4 + j:iw * 32 + k * 4 + j + 1],
                                               sqb[:, :CH], ALU.mult, ALU.add).then_inc(s3.h, 1)
                s3.v += 4
                c.on("vector", fcv)
                cvv = s3.v

                def fsl(s, iw=iw, k=k, cvv=cvv):
                    s.wait_ge(s3.h, cvv)
                    s.activation(cq32[:, k * CH:(k + 1) * CH], sqb[:, :CH],
                                 AF.Silu).then_inc(s3.h, 1)
                s3.v += 1
                c.on("scalar", fsl)
            convv = s3.v
            if iw == 2:
                def fvc(v, convv=convv):
                    v.wait_ge(s3.h, convv)
                    v.tensor_copy(vh[:, :], cq32[:, :]).then_inc(s3.h, 1)
                s3.v += 1
                c.on("vector", fvc)
            else:
                dst = (qh, kh)[iw]
                def fsqA(s, convv=convv):
                    s.wait_ge(s3.h, convv)
                    last = None
                    for k in range(4):
                        last = s.activation(sqb[:, k * CH:(k + 1) * CH],
                                            cq32[:, k * CH:(k + 1) * CH], AF.Square)
                    last.then_inc(s3.h, 1)
                s3.v += 1
                c.on("scalar", fsqA)
                sqav = s3.v

                def fmmA(t, sqav=sqav):
                    t.wait_ge(s3.h, sqav)
                    last = None
                    for k in range(4):
                        last = t.matmul(c.t["ps_s"][0:1, :CH], c.t["ones_f"][:, :],
                                        sqb[:, k * CH:(k + 1) * CH],
                                        start=(k == 0), stop=False)
                    last.then_inc(s3.h, 1)
                s3.v += 1
                c.on("tensor", fmmA)
                mav = s3.v

                def fsqB(s, mav=mav):
                    s.wait_ge(s3.h, mav)
                    last = None
                    for k in range(4, KT):
                        last = s.activation(sqb[:, (k - 4) * CH:(k - 3) * CH],
                                            cq32[:, k * CH:(k + 1) * CH], AF.Square)
                    last.then_inc(s3.h, 1)
                s3.v += 1
                c.on("scalar", fsqB)
                sqbv = s3.v

                def fmmB(t, sqbv=sqbv):
                    t.wait_ge(s3.h, sqbv)
                    last = None
                    for k in range(4, KT):
                        last = t.matmul(c.t["ps_s"][0:1, :CH], c.t["ones_f"][:, :],
                                        sqb[:, (k - 4) * CH:(k - 3) * CH],
                                        start=False, stop=(k == KT - 1))
                    last.then_inc(s3.h, 1)
                s3.v += 1
                c.on("tensor", fmmB)
                mbv = s3.v

                def fns(s, mbv=mbv):
                    s.wait_ge(s3.h, mbv)
                    s.activation(strow[0:1, 0:CH], c.t["ps_s"][0:1, :CH], AF.Sqrt).then_inc(s3.h, 1)
                s3.v += 1
                c.on("scalar", fns)
                nsv = s3.v

                def fnr(v, nsv=nsv, e1=s3.v + 1, e2=s3.v + 2):
                    v.wait_ge(s3.h, nsv)
                    v.tensor_scalar_add(strow[0:1, 0:CH], strow[0:1, 0:CH], 1e-6).then_inc(s3.h, 1)
                    v.wait_ge(s3.h, e1)
                    v.reciprocal(strow[0:1, CH:2 * CH], strow[0:1, 0:CH]).then_inc(s3.h, 1)
                s3.v += 2
                c.on("vector", fnr)
                nrv = s3.v

                def fnb(t, nrv=nrv):
                    t.wait_ge(s3.h, nrv)
                    t.matmul(c.t["ps_b1"][:, :CH], c.t["onesr_f"][0:1, :],
                             strow[0:1, CH:2 * CH], start=True, stop=True).then_inc(s3.h, 1)
                s3.v += 1
                c.on("tensor", fnb)
                nbv = s3.v

                def fnn(v, dst=dst, nbv=nbv, e1=s3.v + 1):
                    v.wait_ge(s3.h, nbv)
                    v.tensor_copy(c.t["rstd_b"][:, :CH], c.t["ps_b1"][:, :CH]).then_inc(s3.h, 1)
                    v.wait_ge(s3.h, e1)
                    for k in range(KT):
                        v.tensor_tensor(dst[:, k * CH:(k + 1) * CH], cq32[:, k * CH:(k + 1) * CH],
                                        c.t["rstd_b"][:, :CH], ALU.mult)
                    v.tensor_copy(tmp[:, 0:1], dst[:, 0:1]).then_inc(s3.h, 1)
                s3.v += 2
                c.on("vector", fnn)
            c.flush()

        for cc in range(8):
            s0 = cc * 64
            for h in range(16):
                pj, hh = h // 2, h % 2
                pb = hh * 64
                kt_ = h // 2

                def PS(p_, lo=0, hi=64, pb=pb):
                    return p_[pb + lo:pb + hi, 0:64]

                def SL(i, pb=pb):
                    return smat[pb:pb + 64, i * 64:(i + 1) * 64]

                def FL(i, pb=pb):
                    return sfm[pb:pb + 64, i * 64:(i + 1) * 64]
                lcolb = lcol[pb:pb + 64, :]
                qf = qh[pb:pb + 64, kt_ * CH + s0:kt_ * CH + s0 + 64]
                kf = kh[pb:pb + 64, kt_ * CH + s0:kt_ * CH + s0 + 64]
                vf = vh[pb:pb + 64, kt_ * CH + s0:kt_ * CH + s0 + 64]
                idnp = c.t["idn"][pb:pb + 64, pb:pb + 64]
                Xh = X_sb[pb:pb + 64, pj * 64:(pj + 1) * 64]
                Xbh = Xbf[pb:pb + 64, :]
                XTh = XTbf[pb:pb + 64, :]

                def TQ(f):
                    s3.v += 1
                    c.on("tensor", f)
                    return s3.v

                def VQ(f, n=1):
                    s3.v += n
                    c.on("vector", f)
                    return s3.v

                def SQ(f, n=1):
                    s3.v += n
                    c.on("scalar", f)
                    return s3.v

                v0 = TQ(lambda t, h=h, s0=s0, gate=s3.v: (
                    t.wait_ge(s3.h, gate) if gate else None,
                    t.matmul(ps_m[0:1, 0:64], idnf16[0:16, h:h + 1], lam[0:16, s0:s0 + 64],
                             start=True, stop=True),
                    t.matmul(psB[0:1, 0:64], idnf16[0:16, h:h + 1], bgr[0:16, s0:s0 + 64],
                             start=True, stop=True).then_inc(s3.h, 1)))
                v1 = SQ(lambda s, v0=v0: (
                    s.wait_ge(s3.h, v0),
                    s.copy(lrow[0:1, 0:64], ps_m[0:1, 0:64]),
                    s.copy(lrow[0:1, CH:CH + 64], psB[0:1, 0:64]),
                    s.copy(lrowb[0:1, 0:64], psB[0:1, 0:64]).then_inc(s3.h, 1)))
                if cc > 0:
                    vb = TQ(lambda t, h=h, s0=s0, v1=v1: (
                        t.wait_ge(s3.h, v1),
                        t.matmul(ps_m[0:1, 0:1], idnf16[0:16, h:h + 1],
                                 lam[0:16, s0 - 1:s0], start=True, stop=True).then_inc(s3.h, 1)))
                    v1 = VQ(lambda v, vb=vb, e0=s3.v + 1: (
                        v.wait_ge(s3.h, vb),
                        v.tensor_copy(lrow[0:1, 64:65], ps_m[0:1, 0:1]).then_inc(s3.h, 1),
                        v.wait_ge(s3.h, e0),
                        v.tensor_scalar(lrow[0:1, 0:64], lrow[0:1, 0:64], lrow[0:1, 64:65],
                                        None, ALU.subtract).then_inc(s3.h, 1)), n=2)
                v2 = TQ(lambda t, v1=v1, pb=pb: (
                    t.wait_ge(s3.h, v1),
                    t.matmul(ps_m[pb:pb + 64, 0:1], lrow[0:1, 0:64], c.t["ones_f"][0:1, 0:1],
                             start=True, stop=True, tile_position=(0, pb)),
                    t.matmul(psB[pb:pb + 64, 0:64], c.t["onesr_f"][0:1, 0:64], lrow[0:1, 0:64],
                             start=True, stop=True, tile_position=(0, pb)).then_inc(s3.h, 1)))
                v3 = VQ(lambda v, v2=v2, e0=s3.v + 1, e1=s3.v + 2, e2=s3.v + 3,
                        lcolb=lcolb, PS=PS, FL=FL: (
                    v.wait_ge(s3.h, v2),
                    v.tensor_copy(lcolb[:, 0:1], PS(ps_m, 0, 64)[:, 0:1]),
                    v.tensor_copy(FL(4), PS(psB)).then_inc(s3.h, 1),
                    v.wait_ge(s3.h, e0),
                    v.tensor_scalar(FL(3), FL(4), lcolb[:, 0:1], None, ALU.subtract).then_inc(s3.h, 1),
                    v.wait_ge(s3.h, e1),
                    v.tensor_scalar_mul(FL(3), FL(3), -1.0).then_inc(s3.h, 1),
                    v.wait_ge(s3.h, e2),
                    v.tensor_scalar(FL(7), FL(3), 0.0, None, ALU.max),
                    v.tensor_scalar(FL(3), FL(3), 0.0, None, ALU.min).then_inc(s3.h, 1)), n=4)
                v4 = SQ(lambda s, v3=v3, e0=s3.v + 1, FL=FL: (
                    s.wait_ge(s3.h, v3),
                    s.activation(FL(0), FL(3), AF.Exp).then_inc(s3.h, 1),
                    s.wait_ge(s3.h, e0),
                    s.activation(FL(1), FL(7), AF.Exp, bias=0.0, scale=-1.0),
                    s.activation(FL(2), FL(4), AF.Exp).then_inc(s3.h, 1)), n=2)
                v5 = TQ(lambda t, v4=v4, pb=pb: (
                    t.wait_ge(s3.h, v4),
                    t.matmul(ps_m[pb:pb + 64, 0:1], lrow[0:1, CH:CH + 64], c.t["ones_f"][0:1, 0:1],
                             start=True, stop=True, tile_position=(0, pb)).then_inc(s3.h, 1)))
                v6 = VQ(lambda v, v5=v5, lcolb=lcolb, PS=PS: (
                    v.wait_ge(s3.h, v5),
                    v.tensor_copy(lcolb[:, 1:2], PS(ps_m)[:, 0:1]).then_inc(s3.h, 1)))
                v7 = TQ(lambda t, kf=kf, v6=v6, pb=pb, idnp=idnp: (
                    t.wait_ge(s3.h, v6),
                    t.matmul(ps_m[pb:pb + 64, 0:64], kf, kf, start=True, stop=True,
                             tile_position=(pb, pb)),
                    t.transpose(ps_m2[pb:pb + 64, 0:64], kf, idnp,
                                tile_position=(pb, pb)).then_inc(s3.h, 1)))
                v8 = VQ(lambda v, v7=v7, PS=PS, FL=FL, SL=SL: (
                    v.wait_ge(s3.h, v7),
                    v.tensor_copy(FL(5), PS(ps_m)),
                    v.tensor_copy(SL(0), PS(ps_m2)).then_inc(s3.h, 1)))
                v9 = TQ(lambda t, vf=vf, v8=v8, pb=pb, idnp=idnp: (
                    t.wait_ge(s3.h, v8),
                    t.transpose(ps_m2[pb:pb + 64, 0:64], vf, idnp,
                                tile_position=(pb, pb)).then_inc(s3.h, 1)))
                v9 = VQ(lambda v, v9=v9, PS=PS, SL=SL: (
                    v.wait_ge(s3.h, v9),
                    v.tensor_copy(SL(1), PS(ps_m2)).then_inc(s3.h, 1)))
                vbb = TQ(lambda t, v9=v9, pb=pb: (
                    t.wait_ge(s3.h, v9),
                    t.matmul(ps_m[pb:pb + 64, 0:64], c.t["onesr_b"][0:1, 0:64],
                             lrowb[0:1, 0:64], start=True, stop=True,
                             tile_position=(0, pb)).then_inc(s3.h, 1)))
                v10 = VQ(lambda v, vbb=vbb, e0=s3.v + 1, e1=s3.v + 2, e2=s3.v + 3,
                         e3=s3.v + 4, Xh=Xh, Xbh=Xbh, lcolb=lcolb, pb=pb, PS=PS, FL=FL, SL=SL: (
                    v.wait_ge(s3.h, vbb),
                    v.tensor_copy(FL(6), PS(ps_m)).then_inc(s3.h, 1),
                    v.wait_ge(s3.h, e0),
                    v.tensor_scalar_mul(FL(7), FL(5), lcolb[:, 1:2]).then_inc(s3.h, 1),
                    v.wait_ge(s3.h, e1),
                    v.tensor_tensor(SL(3), FL(7), masks[pb:pb + 64, 0:64], ALU.mult),
                    v.tensor_tensor(FL(7), FL(5), FL(6), ALU.mult).then_inc(s3.h, 1),
                    v.wait_ge(s3.h, e2),
                    v.tensor_tensor(SL(13), FL(7), masks[pb:pb + 64, 64:128], ALU.mult).then_inc(s3.h, 1),
                    v.wait_ge(s3.h, e3),
                    v.tensor_tensor(SL(4), SL(3), FL(0), ALU.mult),
                    v.tensor_tensor(SL(14), SL(13), FL(1), ALU.mult),
                    v.tensor_scalar_mul(SL(11), SL(0), lcolb[:, 1:2]),
                    v.tensor_scalar_mul(SL(12), SL(1), lcolb[:, 1:2]),
                    v.tensor_copy(Xbh, Xh).then_inc(s3.h, 1)), n=5)
                v11 = TQ(lambda t, v10=v10, pb=pb, idnp=idnp, Xbh=Xbh: (
                    t.wait_ge(s3.h, v10),
                    t.transpose(ps_m2[pb:pb + 64, 0:64], Xbh, idnp,
                                tile_position=(pb, pb)).then_inc(s3.h, 1)))
                v12 = VQ(lambda v, v11=v11, XTh=XTh, PS=PS: (
                    v.wait_ge(s3.h, v11),
                    v.tensor_copy(XTh, PS(ps_m2)).then_inc(s3.h, 1)))
                emit_l0a_solution(c, s3, SL, FL, PS, ps_m, ps_m2, psB, pb, o_fm,
                                  kt_, s0, Xh, Xbh, XTh, qf, idnp, masks)
            c.flush()
        sfin = c.sem("l0afin")
        sldx, ldv = load_x_chunk(c, x_in, ci, tag="l0alx%d" % ci)
        for of in range(KT):
            def fwo(g, of=of, gate=se.v):
                if gate:
                    g.wait_ge(se.h, gate)
                src3 = wd["l0a_woT_d"].rearrange("(k p) o -> p k o", p=128)
                g.dma_start(woT[:, :].rearrange("p (k o) -> p k o", o=128),
                            src3[:, :, of * 128:(of + 1) * 128]).then_inc(sw.h, 16)
            sw.v += 16
            c.on("gpsimd", fwo)
            wv2 = sw.v

            def fmm(t, of=of, wv2=wv2, gate=se.v - 1, dv=s3.v):
                t.wait_ge(sw.h, wv2)
                t.wait_ge(s3.h, dv)
                if gate > 0:
                    t.wait_ge(se.h, gate)
                ps = (psA, psB)[of % 2]
                last = None
                for k in range(KT):
                    last = t.matmul(ps[:, :CH], woT[:, k * 128:(k + 1) * 128],
                                    o_fm[:, k * CH:(k + 1) * CH],
                                    start=(k == 0), stop=(k == KT - 1))
                last.then_inc(se.h, 1)
            se.v += 1
            c.on("tensor", fmm)
            mmv = se.v
            slot = (tmp[:, :CH], sqb[:, :CH])[of % 2]

            def fse2(s, of=of, mmv=mmv, slot=slot, vprev=sfin.v):
                s.wait_ge(se.h, mmv)
                if vprev > 1:
                    s.wait_ge(sfin.h, vprev - 1)
                ps = (psA, psB)[of % 2]
                s.activation(slot, ps[:, :CH], AF.Identity, bias=bo_c[:, of:of + 1],
                             scale=1.0).then_inc(se.h, 1)
            se.v += 1
            c.on("scalar", fse2)
            sev = se.v

            def fvv(v, of=of, ci=ci, sev=sev, slot=slot, ldv=ldv):
                v.wait_ge(se.h, sev)
                v.wait_ge(c.sem("l0alx%d" % ci).h, ldv)
                sl = xc[:, of * CH:(of + 1) * CH]
                v.tensor_tensor(sl, sl, slot, ALU.add).then_inc(sfin.h, 1)
            sfin.v += 1
            c.on("vector", fvv)
        store_x_chunk(c, x_out, ci, sfin, sfin.v)
        c.flush()


STAGE_DRAM["l0a"] = [("l0a_wqT_d", [D, D]), ("l0a_wkT_d", [D, D]), ("l0a_wvT_d", [D, D]),
                     ("l0a_woT_d", [D, D]), ("l0a_wabT_d", [D, 32]),
                     ("l0a_abb_d", [16, 2]), ("l0a_conv_d", [128, 96]),
                     ("l0a_tril_d", [128, 192]), ("l0a_bo_d", [128, 8]),
                     ("l0a_qkvb_d", [128, 24])]


def _stage_l0a(inp):
    def colk(v, kt):
        return np.ascontiguousarray(v.reshape(kt, 128).T)
    cw = np.zeros((128, 96), np.float32)
    for iw, nm in enumerate(("conv_q", "conv_k", "conv_v")):
        w4 = inp["L0_attn_" + nm][:, 0, :]
        for k in range(8):
            cw[:, iw * 32 + k * 4:iw * 32 + (k + 1) * 4] = w4[k * 128:(k + 1) * 128, :]
    m = np.zeros((128, 192), np.float32)
    t1 = np.tril(np.ones((64, 64), np.float32), -1)
    t2 = np.triu(np.ones((64, 64), np.float32), 1)
    t3 = np.triu(np.ones((64, 64), np.float32), 0)
    for half in range(2):
        m[half * 64:(half + 1) * 64, 0:64] = t1
        m[half * 64:(half + 1) * 64, 64:128] = t2
        m[half * 64:(half + 1) * 64, 128:192] = t3
    qkvb = np.zeros((128, 24), np.float32)
    for iw, nm in enumerate(("bq", "bk", "bv")):
        qkvb[:, iw * 8:(iw + 1) * 8] = colk(inp["L0_attn_" + nm], 8)
    wab = np.concatenate([inp["L0_attn_Wa"].T, inp["L0_attn_Wb"].T], axis=1)
    abb = np.stack([inp["L0_attn_ba"], inp["L0_attn_bb"]], axis=1)
    return {"l0a_wqT_d": np.ascontiguousarray(inp["L0_attn_Wq"].T),
            "l0a_wkT_d": np.ascontiguousarray(inp["L0_attn_Wk"].T),
            "l0a_wvT_d": np.ascontiguousarray(inp["L0_attn_Wv"].T),
            "l0a_woT_d": np.ascontiguousarray(inp["L0_attn_Wo"].T),
            "l0a_wabT_d": np.ascontiguousarray(wab),
            "l0a_abb_d": np.ascontiguousarray(abb.astype(np.float32)),
            "l0a_conv_d": cw, "l0a_tril_d": m,
            "l0a_bo_d": colk(inp["L0_attn_bo"], 8),
            "l0a_qkvb_d": qkvb}


_OLD_STAGE4 = stage_weights


def stage_weights(stage, inp):
    if stage == "l0a":
        return _stage_l0a(inp)
    return _OLD_STAGE4(stage, inp)


_OLD_BUILD4 = build_stage


def build_stage(stage):
    EMITTERS.setdefault("l0a", emit_l0a)
    return _OLD_BUILD4(stage)


GROUPS = [("l0a", "l0f"), ("l1a", "l1fa"), ("l1fb", "l2a"), ("l2f",), ("l3a", "l3f")]

_GROUP_CACHE = {}


def _ensure_emitters():
    if "l0f" not in EMITTERS:
        _reg()
    EMITTERS.setdefault("l1a", emit_l1a)
    EMITTERS.setdefault("l3a", emit_l3a)
    EMITTERS.setdefault("l2f", emit_l2f)
    EMITTERS.setdefault("l2a", emit_l2a)
    EMITTERS.setdefault("l0a", emit_l0a)
    EMITTERS.setdefault("l1fb", emit_l1fb)


def build_group(stages):
    if stages in _GROUP_CACHE:
        return _GROUP_CACHE[stages]
    _ensure_emitters()
    stack = ExitStack()
    nc = bass.Bass("TRN2", target_bir_lowering=False)
    x_in = nc.dram_tensor("x_in", [KT, 128, N], F32, kind="ExternalInput").ap()
    x_out = nc.dram_tensor("x_out", [KT, 128, N], F32, kind="ExternalOutput").ap()
    real = [st for st in stages if st != "l1fa"]
    bufs = [x_in]
    for i in range(len(real) - 1):
        bufs.append(nc.dram_tensor("xh%d" % i, [KT, 128, N], F32).ap())
    bufs.append(x_out)
    fl_out = cnt_out = None
    if "l1fa" in stages:
        fl_out = nc.dram_tensor("fl_out", [8, 4 * N], F32, kind="ExternalOutput").ap()
        cnt_out = nc.dram_tensor("cnt_out", [16, 1], F32, kind="ExternalOutput").ap()
    wd = {}
    for st in stages:
        for nm, sh in STAGE_DRAM[st]:
            wd[nm] = nc.dram_tensor(nm, list(sh), F32, kind="ExternalInput").ap()
    c = Ctx(nc, stack)
    c.semstack = stack
    emit_setup(c)
    alloc_shared(c, sq_cols=(4 * CH if "l2f" in stages else KT * CH))
    bi = 0
    for st in stages:
        sstack = ExitStack()
        c.stack = sstack
        if st == "l1fa":
            emit_l1fa(c, bufs[bi], wd, fl_out, cnt_out)
        else:
            EMITTERS[st](c, bufs[bi], bufs[bi + 1], wd)
            bi += 1
        c.flush()
        sstack.close()
    c.stack = stack
    sout = c.sem("sout")

    def fw(sy):
        sy.wait_ge(sout.h, sout.v)
    c.on("sync", fw)
    c.flush()
    stack.close()
    _GROUP_CACHE[stages] = nc
    return nc


def kernel(**inputs):
    import os as _os9
    if _os9.environ.get("UNFUSED"):
        xs = [x_to_dev(inputs["L0_attn_x"][b]) for b in range(8)]
        for stage in STAGE_ORDER:
            xs = run_stage(stage, xs, inputs)
        return np.stack([x_from_dev(x) for x in xs])
    xs = [x_to_dev(inputs["L0_attn_x"][b]) for b in range(8)]
    cores = list(range(8))
    fls = cnts = None
    for grp in GROUPS:
        w = {}
        for st in grp:
            if st == "l1fa":
                w["l1f_rwT_d"] = np.ascontiguousarray(inputs["L1_ffn_router_w"].T)
                w["l1f_rb_d"] = np.ascontiguousarray(inputs["L1_ffn_router_b"].reshape(8, 1))
            elif st == "l1fb":
                w["l1f_W_d"] = inputs["L1_ffn_W"]
                w["l1f_V_d"] = inputs["L1_ffn_V"]
                w["l1f_Wout_d"] = inputs["L1_ffn_W_out"]
            else:
                w.update(stage_weights(st, inputs))
        maps = []
        for ci in cores:
            mp = dict(x_in=xs[ci], **w)
            if "l1fb" in grp:
                cntp = np.zeros((8, 16), np.float32)
                for r in range(ci):
                    cntp[r, :] = cnts[r][:, 0]
                mp["l1f_fl_d"] = fls[ci]
                mp["l1f_cntp_d"] = cntp
            maps.append(mp)
        res = run_bass_kernel_spmd(build_group(grp), maps, core_ids=cores)
        xs = [np.asarray(r["x_out"]) for r in res.results]
        if "l1fa" in grp:
            fls = [np.asarray(r["fl_out"]) for r in res.results]
            cnts = [np.asarray(r["cnt_out"]) for r in res.results]
    return np.stack([x_from_dev(x) for x in xs])
